# version 16
# baseline (speedup 1.0000x reference)
import math
import numpy as np
import ml_dtypes
import concourse.bass as bass
import concourse.mybir as mybir
from concourse.bass_utils import run_bass_kernel_spmd

F32 = mybir.dt.float32
BF16 = mybir.dt.bfloat16
AF = mybir.ActivationFunctionType
ALU = mybir.AluOpType
AX = mybir.AxisListType

T = 2048
D = 1024
NT = T // 128
ND = D // 128
NEG = -3.4028235e38

import os as _os
_PARTIAL = int(_os.environ.get("KPARTIAL", "4"))

ENGINES = ("tensor", "vector", "scalar", "gpsimd", "sync")


def _to_T(tab):
    Tn, C = tab.shape
    return np.ascontiguousarray(tab.reshape(NT, 128, C).transpose(1, 0, 2))


def make_tables():
    tabs = {}
    t = np.arange(T, dtype=np.float64)

    invf = 10000.0 ** (-np.arange(32, dtype=np.float64) / 32.0)
    ang = t[:, None] * invf[None, :]
    c, s = np.cos(ang), np.sin(ang)
    cosA = np.concatenate([c, c], axis=1)
    sinA = np.concatenate([-s, s], axis=1)
    tabs["ropeA_cos"] = _to_T(cosA).astype(ml_dtypes.bfloat16)
    tabs["ropeA_sin"] = _to_T(sinA).astype(ml_dtypes.bfloat16)

    half_idx = np.arange(0, 64, 2, dtype=np.float64)
    invf2 = 1.0 / 10000.0 ** (half_idx / 64.0)
    fr = t[:, None] * invf2[None, :]
    base_v = (half_idx + 0.4 * 64) / (1.4 * 64)
    power = (t - T // 2) / 256.0
    scl = base_v[None, :] ** power[:, None]
    cosf, sinf = np.cos(fr), np.sin(fr)
    qc = np.concatenate([cosf * scl, cosf * scl], 1)
    qs = np.concatenate([-sinf * scl, sinf * scl], 1)
    kc = np.concatenate([cosf / scl, cosf / scl], 1)
    ks = np.concatenate([-sinf / scl, sinf / scl], 1)
    tabs["rotF_qcos"] = _to_T(qc).astype(np.float32)
    tabs["rotF_qsin"] = _to_T(qs).astype(np.float32)
    tabs["rotF_kcos"] = _to_T(kc).astype(np.float32)
    tabs["rotF_ksin"] = _to_T(ks).astype(np.float32)

    base_r = (np.arange(0, 128, 2, dtype=np.float64) + 0.4 * 128) / (1.4 * 128)
    sclr = base_r[None, :] ** (t[:, None] / 512.0)
    invf3 = 1.0 / 10000.0 ** (np.arange(64, dtype=np.float64) / 64.0)
    sin3 = np.sin(t[:, None] * invf3[None, :])
    cos3 = np.cos(t[:, None] * invf3[None, :])

    def inter2(a, b):
        out = np.empty((T, 128), np.float64)
        out[:, 0::2] = a
        out[:, 1::2] = b
        return out
    c2u = cos3 * sclr
    s2u = sin3 * sclr
    c2d = cos3 / sclr
    s2d = sin3 / sclr
    tabs["xpos_qcos"] = _to_T(inter2(c2u, c2u)).astype(ml_dtypes.bfloat16)
    tabs["xpos_qsin"] = _to_T(inter2(-s2u, s2u)).astype(ml_dtypes.bfloat16)
    tabs["xpos_kcos"] = _to_T(inter2(c2d, c2d)).astype(ml_dtypes.bfloat16)
    tabs["xpos_ksin"] = _to_T(inter2(-s2d, s2d)).astype(ml_dtypes.bfloat16)

    gam = 1.0 - 2.0 ** (-5.0 - np.arange(8, dtype=np.float64))
    ij = np.arange(128)
    diff = ij[None, :] - ij[:, None]
    dt_tiles = []
    cross_tiles = []
    kscale_cols = []
    for h in range(8):
        dmat = np.where(diff >= 0, gam[h] ** np.maximum(diff, 0), 0.0)
        dt_tiles.append(dmat)
        cross_tiles.append(np.broadcast_to(gam[h] ** (ij + 1.0), (128, 128)).copy())
        kscale_cols.append(gam[h] ** (127.0 - ij))
    tabs["decayT"] = np.stack(dt_tiles).astype(ml_dtypes.bfloat16)
    tabs["decay_cross"] = np.stack(cross_tiles).astype(ml_dtypes.bfloat16)
    tabs["decay_ks"] = np.stack(kscale_cols, 1).astype(np.float32)
    tabs["gammaC"] = (gam ** 128.0).astype(np.float64)

    m01 = (ij[None, :] >= ij[:, None]).astype(np.float32)
    tabs["mask01T"] = m01.astype(ml_dtypes.bfloat16)

    triU = np.where(ij[None, :] > ij[:, None], NEG, 0.0).astype(np.float32)
    triL = np.where(ij[None, :] < ij[:, None], NEG, 0.0).astype(np.float32)
    tabs["maskU"] = triU
    tabs["maskL"] = triL

    tabs["identity"] = np.eye(128, dtype=np.float32)
    tabs["consts"] = np.full((128, 2), -0.5, np.float32)
    return tabs


class Sem:
    __slots__ = ("h", "n")

    def __init__(self, h):
        self.h = h
        self.n = 0


class Emitter:
    def __init__(self, nc, stack):
        self.nc = nc
        self.stack = stack
        self.q = {e: [] for e in ENGINES}
        self.ms = {e: Sem(stack.enter_context(nc.semaphore(f"ms_{e}"))) for e in ENGINES}
        self.ms["tensorB"] = Sem(stack.enter_context(nc.semaphore("ms_tensorB")))
        self._dsems = {}

    def dsem(self, name):
        if name not in self._dsems:
            self._dsems[name] = Sem(self.stack.enter_context(self.nc.semaphore(f"d_{name}")))
        return self._dsems[name]

    def group(self, name, n=4):
        key = ("grp", name)
        if key not in self._dsems:
            self._dsems[key] = [Sem(self.stack.enter_context(
                self.nc.semaphore(f"d_{name}{i}"))) for i in range(n)]
            self._grp_idx = getattr(self, "_grp_idx", {})
            self._grp_idx[name] = 0
        return self._dsems[key]

    SELF_CHAIN = ("vector", "scalar", "gpsimd")

    @staticmethod
    def _flat(waits):
        out = []
        for w in waits:
            if isinstance(w, list):
                out.extend(w)
            elif w is not None:
                out.append(w)
        return out

    def snap_all(self, engs=("tensor", "tensorB", "vector", "scalar", "gpsimd")):
        return [(self.ms[e], self.ms[e].n) for e in engs if self.ms[e].n > 0]

    def snap_w(self):
        return self.snap_all(("tensor", "tensorB"))

    def grp_waits(self, name):
        key = ("grp", name)
        if key not in self._dsems:
            return []
        return [(s, s.n) for s in self._dsems[key] if s.n > 0]

    def op(self, eng, fn, waits=(), tick=False, tick_sem=None):
        ms = self.ms[tick_sem] if tick_sem else self.ms[eng]
        chain = eng in self.SELF_CHAIN
        wl = [(s.h, v) for (s, v) in self._flat(waits)]
        if chain:
            if ms.n > 0:
                wl.append((ms.h, ms.n))
            tick = True
        elif tick and eng == "tensor" and ms.n > 0:
            wl.append((ms.h, ms.n))
        val = None
        if tick:
            ms.n += 1
            val = ms.n

        def closure(e, fn=fn, wl=wl, tick=tick, msh=ms.h):
            for h, v in wl:
                e.wait_ge(h, v)
            ins = fn(e)
            if tick:
                if ins is None:
                    ins = e.nop()
            if tick and ins is not None:
                ins.then_inc(msh, 1)
        self.q[eng].append(closure)
        return val

    def _dma_common(self, eng, fn, grp, waits, ngrp):
        sems = self.group(grp, ngrp)
        i = self._grp_idx[grp]
        self._grp_idx[grp] = (i + 1) % len(sems)
        sem = sems[i]
        prev = sem.n
        sem.n += 16
        tgt = sem.n
        wl = [(s.h, v) for (s, v) in self._flat(waits)]

        def closure(e, fn=fn, wl=wl, semh=sem.h, prev=prev):
            for h, v in wl:
                e.wait_ge(h, v)
            if prev:
                e.wait_ge(semh, prev)
            fn(e).then_inc(semh, 16)
        self.q[eng].append(closure)
        return (sem, tgt)

    def dma(self, eng, out, in_, grp="io", waits=(), ngrp=4, **kw):
        return self._dma_common(
            eng, (lambda e, out=out, in_=in_, kw=kw:
                  e.dma_start(out=out, in_=in_, **kw)), grp, waits, ngrp)

    def dma_tr(self, eng, out, in_, grp="tr", waits=(), ngrp=4):
        return self._dma_common(
            eng, (lambda e, out=out, in_=in_: e.dma_start_transpose(out, in_)),
            grp, waits, ngrp)

    def wait(self, eng, waits):
        wl = [(s.h, v) for (s, v) in waits]

        def closure(e, wl=wl):
            for h, v in wl:
                e.wait_ge(h, v)
        self.q[eng].append(closure)

    def barrier(self, engines=ENGINES):
        vals = []
        for e in engines:
            vals.append((e, self.op(e, lambda x: x.nop(), tick=True)))
        for e in engines:
            self.wait(e, [(self.ms[o], v) for (o, v) in vals if o != e])

    def emit(self):
        nc = self.nc
        with nc.Block() as block:
            for name in ENGINES:
                deco = getattr(block, name)
                qlist = self.q[name]

                def body(eng, qlist=qlist):
                    for f in qlist:
                        f(eng)
                deco(body)


def build_program(dbg=()):
    from contextlib import ExitStack
    nc = bass.Bass("TRN2", target_bir_lowering=False)
    tabs = make_tables()
    st = ExitStack()

    dram = {}

    def din(name, shape, dtype=F32):
        dram[name] = nc.dram_tensor(name, list(shape), dtype, kind="ExternalInput").ap()
        return dram[name]

    x_in = din("x", (T, D))
    for w in ("aWq", "aWk", "aWv"):
        din(w, (8, 128, 8, 128))
    din("aWo", (4, 128, 8, 256))
    din("fwqkv", (2, 6, 128, 8, 256))
    din("fwo", (2, 4, 128, 4, 256))
    din("fw1", (2, 22, 128, 8, 256))
    din("fw2", (2, 22, 128, D))
    
    din("flng", (2, 128, ND))
    din("flnb", (2, 128, ND))
    din("fb1", (2, 128, 44))
    din("fb2", (2, 128, ND))
    din("fqs", (2, 64))
    din("fks", (2, 64))
    din("bWq", (2, 8, 128, 8, 128))
    din("bWk", (2, 8, 128, 8, 128))
    din("bWv", (2, 8, 128, 8, 128))
    din("bWg", (2, 4, 128, 8, 256))
    din("bWo", (2, 4, 128, 8, 256))
    din("bw1", (2, 16, 128, 8, 256))
    din("bw2", (2, 4096, D))
    for nm in ("bgns", "bgnb", "bln1s", "bln1b", "bln2s", "bln2b"):
        din(nm, (2, 128, ND))
    din("ropeA_cos", (128, NT, 64), BF16)
    din("ropeA_sin", (128, NT, 64), BF16)
    for nm in ("rotF_qcos", "rotF_qsin", "rotF_kcos", "rotF_ksin"):
        din(nm, (128, NT, 64))
    for nm in ("xpos_qcos", "xpos_qsin", "xpos_kcos", "xpos_ksin"):
        din(nm, (128, NT, 128), BF16)
    din("decayT", (8, 128, 128), BF16)
    din("decay_cross", (8, 128, 128), BF16)
    din("decay_ks", (128, 8))
    din("mask01T", (128, 128), BF16)
    din("maskU", (128, 128))
    din("maskL", (128, 128))
    din("identity", (128, 128))
    din("consts", (128, 2))

    y_out = nc.dram_tensor("y", [T, D], F32, kind="ExternalOutput").ap()
    dbg_out = {}
    for name in dbg:
        pass

    xo_spill = nc.dram_tensor("xo_spill", [128, ND, T], F32, kind="Internal").ap()
    wcache = {
        "fw1": nc.dram_tensor("fw1c", [2, 22, 128, 8, 256], BF16,
                              kind="Internal").ap(),
        "fw2": nc.dram_tensor("fw2c", [2, 22, 128, 1024], BF16,
                              kind="Internal").ap(),
        "bw1": nc.dram_tensor("bw1c", [2, 16, 128, 8, 256], BF16,
                              kind="Internal").ap(),
        "bw2": nc.dram_tensor("bw2c", [2, 32, 128, 1024], BF16,
                              kind="Internal").ap(),
    }

    sb = lambda name, shape, dt: st.enter_context(nc.sbuf_tensor(name, list(shape), dt))
    ps = lambda name, shape: st.enter_context(nc.psum_tensor(name, list(shape), F32))

    XI = sb("XI", [128, ND, T], F32)
    BIG = sb("BIG", [128, 24576], BF16)
    NFR = sb("NFR", [128, ND, T + 16], BF16)
    AD = sb("AD", [128, 16384], BF16)
    WBUF = sb("WBUF", [128, 2, 8, 256], BF16)
    TAB = sb("TAB", [128, NT, 512], BF16)
    SC32 = sb("SC32", [128, 832], F32)
    STAT = sb("STAT", [128, 64], F32)
    SMALL = sb("SMALL", [128, 256], F32)
    MISC = sb("MISC", [128, 384], F32)
    MISCB = sb("MISCB", [128, 768], BF16)
    ONESB = sb("ONESB", [128, 8], BF16)

    PSALL = ps("PSALL", [128, 4096])
    PS = [PSALL[:, i * 512:(i + 1) * 512] for i in range(8)]
    NF = NFR[:, :, 16:T + 16]

    E = Emitter(nc, st)

    nc.const_aps.register(F32, -0.5, SMALL[:, 192:193])

    meta = dict(nc=nc, tabs=tabs, E=E, dram=dram, y=y_out, xo_spill=xo_spill,
                wcache=wcache, wcache_ev={},
                XI=XI, BIG=BIG, NF=NF, NFR=NFR, AD=AD, SC32=SC32, STAT=STAT,
                SMALL=SMALL, WBUF=WBUF, TAB=TAB, MISC=MISC,
                MISCB=MISCB, ONESB=ONESB, PS=PS, PSALL=PSALL, ps_guard={},
                dbg=dbg, dbg_out=dbg_out, stack=st)

    emit_forward(meta)
    E.emit()
    st.close()
    return nc, meta


class WStream:

    def __init__(self, meta, key="w", buf=None, nslot=2, grp="w",
                 init_extra=()):
        self.E = meta["E"]
        self.buf = meta["WBUF"] if buf is None else buf
        self.nslot = nslot
        self.grp = grp
        g = meta.setdefault("_wsg_" + key, [None] * nslot)
        self.guard = g
        self.idx = meta.setdefault("_wsi_" + key, [0])
        snap = self.E.snap_w()
        for i in range(nslot):
            if g[i] is None:
                g[i] = [snap]
        self.first_extra = list(self.E._flat(init_extra))
        self.nfirst = 0
        self.slot_wb = {}

    def load(self, src, extra=()):
        i = self.idx[0] % self.nslot
        self.idx[0] += 1
        w = list(self.E._flat(self.guard[i] or [])) + list(extra)
        if self.nfirst < self.nslot:
            w += self.first_extra
            self.nfirst += 1
        n = 1
        for s in src.shape[1:]:
            n *= s
        sl = self.buf[:, i]
        if len(sl.shape) > 2:
            sl = sl.rearrange("p a b -> p (a b)")
        dst = sl[:, 0:n]
        ev = self.E.dma("gpsimd", dst, src, grp=self.grp, waits=w, ngrp=4)
        self.guard[i] = None
        if src.ndim == 3:
            view = dst.rearrange("p (a b) -> p a b", b=src.shape[2])
        else:
            view = dst
        return i, view, ev

    def load_cached(self, src, cache_ap, ckey, cev_map):
        ev_wb = cev_map.get(ckey)
        if ev_wb is not None:
            i, view, ev = self.load(cache_ap, extra=[ev_wb])
        else:
            i, view, ev = self.load(src)
            wb = self.E.dma("sync", cache_ap, view, grp="wb", waits=[ev],
                            ngrp=4)
            cev_map[ckey] = wb
            self.slot_wb[i] = wb
        return i, view, ev

    def done(self, i, ev):
        g = [ev]
        wb = self.slot_wb.pop(i, None)
        if wb is not None:
            g.append(wb)
        self.guard[i] = g

def emit_dbg(meta, name, ap_fn, shape, dtype, waits, queue="sync"):
    if name not in meta["dbg"]:
        return
    nc, E = meta["nc"], meta["E"]
    t = nc.dram_tensor("dbg_" + name, list(shape), dtype, kind="ExternalOutput").ap()
    meta["dbg_out"][name] = t
    ev = E.dma(queue, t, ap_fn(), waits=waits)
    for eng in ENGINES:
        E.wait(eng, [ev])


def emit_forward(meta):
    E = meta["E"]
    emit_load_initial(meta)
    emit_input_and_norm1(meta)
    if _PARTIAL >= 1:
        emit_attn_a(meta)
    if _PARTIAL >= 2:
        emit_ffn_block(meta, app=0)
    if _PARTIAL >= 3:
        emit_attn_b(meta)
    if _PARTIAL >= 4:
        emit_ffn_block(meta, app=1)
    emit_store_output(meta)


def emit_load_initial(meta):
    nc, E = meta["nc"], meta["E"]
    dram = meta["dram"]
    meta["ev_ident"] = E.dma("sync", meta["MISC"][:, 0:128], dram["identity"])
    meta["ev_maskU"] = E.dma("sync", meta["MISC"][:, 128:256], dram["maskU"])
    meta["ev_maskL"] = E.dma("sync", meta["MISC"][:, 256:384], dram["maskL"])
    meta["ev_dks"] = E.dma("sync", meta["SMALL"][:, 184:192], dram["decay_ks"])
    ev_c = E.dma("sync", meta["SMALL"][:, 192:194], dram["consts"])
    meta["ev_consts"] = ev_c
    E.op("vector", lambda e: e.memset(meta["ONESB"][:], 1.0),
         waits=[ev_c], tick=True)
    E.op("vector", lambda e: e.memset(meta["SMALL"][:, 208:209], 0.0))
    E.op("vector", lambda e: e.memset(meta["NFR"][:, :, 0:16], 0.0))
    E.op("vector", lambda e: e.memset(meta["SMALL"][0:1, 0:128], 1.0))
    v_z = E.op("vector", lambda e: e.memset(meta["SMALL"][:, 209:210], 1.0), tick=True)
    meta["v_zo"] = (E.ms["vector"], v_z)


def _tile_stats(meta, xtile, col, eps, dump, waits=(), dim=float(D)):
    E = meta["E"]
    STAT, SMALL = meta["STAT"], meta["SMALL"]
    prev_vv = meta.get("_ts_vv")
    v_r = E.op("vector", (lambda e: e.reduce_sum(SMALL[:, 200:201], xtile, axis=AX.X)),
               waits=list(waits), tick=True)
    v_s = E.op("scalar", (lambda e: e.activation(dump, xtile, AF.Square,
                                                 accum_out=SMALL[:, 201:202])),
               waits=list(waits) + ([(E.ms["vector"], prev_vv)] if prev_vv else []),
               tick=True)
    E.op("vector", (lambda e: e.tensor_scalar(STAT[:, col:col + 1], SMALL[:, 200:201],
                                              -1.0 / dim, None, ALU.mult)),
         waits=[(E.ms["scalar"], v_s), (E.ms["vector"], v_r)])
    E.op("vector", (lambda e: e.tensor_mul(SMALL[:, 202:203], STAT[:, col:col + 1],
                                           STAT[:, col:col + 1])))
    E.op("vector", (lambda e: e.scalar_tensor_tensor(SMALL[:, 203:204], SMALL[:, 201:202],
                                                     1.0 / dim, SMALL[:, 202:203],
                                                     ALU.mult, ALU.subtract)))
    v_v = E.op("vector", (lambda e: e.tensor_scalar(SMALL[:, 203:204], SMALL[:, 203:204],
                                                    eps, None, ALU.add)), tick=True)
    meta["_ts_vv"] = v_v
    E.op("scalar", (lambda e: e.activation(SMALL[:, 204:205], SMALL[:, 203:204], AF.Ln)),
         waits=[(E.ms["vector"], v_v)])
    v_rs = E.op("scalar", (lambda e: e.activation(STAT[:, col + 16:col + 17],
                                                  SMALL[:, 204:205], AF.Exp, scale=-0.5)),
                tick=True)
    return v_rs


def emit_input_and_norm1(meta):
    nc, E = meta["nc"], meta["E"]
    dram = meta["dram"]
    BIG, NF, XI, PS = meta["BIG"], meta["NF"], meta["XI"], meta["PS"]
    ident = meta["MISC"][:, 0:128]
    xf32 = BIG[:].bitcast(F32)
    dump = BIG[:, 8192:9216]
    xs = dram["x"]
    slot_free = [None] * 4
    v_tr_last = None
    guard = {}
    for tt in range(NT):
        sl = tt % 4
        xslot = xf32[:, sl * 1024:(sl + 1) * 1024]
        w = [slot_free[sl]] if slot_free[sl] else []
        if meta.get("_in_slot_scalar", {}).get(sl) is not None:
            w.append((E.ms["scalar"], meta["_in_slot_scalar"][sl]))
        ev_ld = E.dma("sync", xslot, xs[tt * 128:(tt + 1) * 128, :], waits=w)
        sc = tt % 8
        v_rs = _tile_stats(meta, xslot, sc, 1e-5, dump, waits=[ev_ld])
        asl = BIG[:, 9216 + (tt % 2) * 1024: 10240 + (tt % 2) * 1024]
        aw = [(E.ms["scalar"], v_rs)]
        if guard.get("a" + str(tt % 2)):
            aw.append(guard["a" + str(tt % 2)])
        v_ap = E.op("vector",
                    (lambda e, xslot=xslot, asl=asl, sc=sc: e.tensor_scalar(
                        asl, xslot, meta["STAT"][:, sc:sc + 1],
                        meta["STAT"][:, 16 + sc:17 + sc], ALU.add, ALU.mult)),
                    waits=aw, tick=True)
        ev_tr = E.dma_tr("sync", NF[:, :, tt * 128:(tt + 1) * 128], asl,
                         waits=[(E.ms["vector"], v_ap)])
        guard["a" + str(tt % 2)] = ev_tr
        v_tr_last = ev_tr
        last_ev = None
        for dt in range(ND):
            bank = (tt * ND + dt) % 2
            pst = PS[6 + bank]
            w2 = [guard[6 + bank]] if (6 + bank) in guard else []
            vtw = E.op("tensor",
                       (lambda e, s=xslot.rearrange("p (d q) -> p d q", q=128)[:, dt, :],
                        p=pst: e.transpose(p[:, 0:128], s, ident)),
                       waits=w2 + ([ev_ld, meta["ev_ident"]] if dt == 0 else []),
                       tick=True)
            last_ev = E.op("vector",
                           (lambda e, p=pst, d=XI[:, dt, tt * 128:(tt + 1) * 128]:
                            e.tensor_copy(d, p[:, 0:128])),
                           waits=[(E.ms["tensor"], vtw)], tick=True)
            guard[6 + bank] = (E.ms["vector"], last_ev)
            meta["ps_guard"][6 + bank] = guard[6 + bank]
        slot_free[sl] = (E.ms["vector"], last_ev)
        meta.setdefault("_in_slot_scalar", {})[sl] = v_rs
    meta["v_nf_ready"] = E.grp_waits("tr")
    meta["v_xi_ready"] = (E.ms["vector"], E.ms["vector"].n)


def emit_norm_F(meta, srcF, outF, eps, gamma=None, beta=None, shift=False,
                waits=(), out_dt_f32=None):
    nc, E = meta["nc"], meta["E"]
    BIG = meta["BIG"]
    STAT = meta["STAT"]
    SMALL = meta["SMALL"]
    stageT = BIG[:, 8192:24576].rearrange("p (tt d) -> p tt d", d=D)
    dump = meta["AD"][:, 15360:16384]
    cast_guard = {}
    v_tr = None
    stage_evs = []
    for dt in range(ND):
        csl = BIG[:, (dt % 2) * 2048:(dt % 2) * 2048 + 2048]
        w = list(waits) if dt == 0 else []
        if cast_guard.get(dt % 2):
            w.append(cast_guard[dt % 2])
        if shift and dt >= 4:
            E.op("scalar", (lambda e, d=csl[:, 0:2]: e.memzero(d)), waits=w)
            v_c = E.op("scalar", (lambda e, s=srcF[:, dt, 0:T - 1], d=csl[:, 1:T]:
                                  e.copy(d, s)), tick=True)
        else:
            v_c = E.op("scalar", (lambda e, s=srcF[:, dt, :], d=csl: e.copy(d, s)),
                       waits=w, tick=True)
        tw = [(E.ms["scalar"], v_c)]
        if dt == 0 and meta.get("_stageT_guard"):
            tw.append(meta["_stageT_guard"])
        v_tr = E.dma_tr("sync", stageT[:, :, dt * 128:(dt + 1) * 128], csl,
                        waits=tw)
        cast_guard[dt % 2] = v_tr
        stage_evs.append(v_tr)
    meta["_norm_cast_evs"] = list(stage_evs)
    all_cast = E.grp_waits("tr")
    rdy_tb = []
    for tb in range(4):
        base = 212 + (tb % 2) * 16
        sums = SMALL[:, base:base + 4]
        sqs = SMALL[:, base + 4:base + 8]
        m2 = SMALL[:, base + 8:base + 12]
        sc0 = (tb * 4) % 8
        nm = STAT[:, sc0:sc0 + 4]
        rs = STAT[:, 16 + sc0:16 + sc0 + 4]
        E.op("vector", (lambda e, s3=stageT[:, tb * 4:(tb + 1) * 4, :], o=sums:
                        e.reduce_sum(o, s3, axis=AX.X)),
             waits=all_cast if tb == 0 else [])
        v_s = None
        for j in range(4):
            v_s = E.op("scalar",
                       (lambda e, t=stageT[:, tb * 4 + j, :], o=sqs[:, j:j + 1]:
                        e.activation(dump, t, AF.Square, accum_out=o)),
                       waits=all_cast if (tb == 0 and j == 0) else [], tick=True)
        E.op("vector", (lambda e, nm=nm, s=sums: e.tensor_scalar(
            nm, s, -1.0 / float(D), None, ALU.mult)))
        E.op("vector", (lambda e, m2=m2, nm=nm: e.tensor_mul(m2, nm, nm)))
        E.op("vector", (lambda e, m2=m2, eps=eps: e.tensor_scalar(
            m2, m2, -eps, None, ALU.add)))
        v_var = E.op("vector", (lambda e, m2=m2, q=sqs: e.scalar_tensor_tensor(
            m2, q, 1.0 / float(D), m2, ALU.mult, ALU.subtract)),
            waits=[(E.ms["scalar"], v_s)], tick=True)
        E.op("scalar", (lambda e, m2=m2: e.activation(m2, m2, AF.Ln)),
             waits=[(E.ms["vector"], v_var)])
        v_rs = E.op("scalar", (lambda e, rs=rs, m2=m2: e.activation(
            rs, m2, AF.Exp, scale=-0.5)), tick=True)
        bk_evs = []
        for j in range(4):
            tt = tb * 4 + j
            sc = tt % 8
            v_ap = E.op("vector",
                        (lambda e, t=stageT[:, tt, :], sc=sc: e.tensor_scalar(
                            t, t, STAT[:, sc:sc + 1], STAT[:, 16 + sc:17 + sc],
                            ALU.add, ALU.mult)),
                        waits=[(E.ms["scalar"], v_rs)] if j == 0 else [],
                        tick=True)
            bk_evs.append(E.dma_tr("sync", outF[:, :, tt * 128:(tt + 1) * 128],
                                   stageT[:, tt, :],
                                   waits=[(E.ms["vector"], v_ap)]))
        if gamma is not None:
            v_g = None
            for dt in range(ND):
                v_g = E.op("vector",
                           (lambda e, dt=dt, t0=tb * 512: e.tensor_scalar(
                               outF[:, dt, t0:t0 + 512], outF[:, dt, t0:t0 + 512],
                               gamma[:, dt:dt + 1], beta[:, dt:dt + 1],
                               ALU.mult, ALU.add)),
                           waits=bk_evs if dt == 0 else [], tick=True)
            rdy_tb.append([(E.ms["vector"], v_g)])
        else:
            rdy_tb.append(list(bk_evs))
    meta["_stageT_guard"] = E.grp_waits("tr")
    meta["_norm_rdy_tb"] = rdy_tb
    rdy = rdy_tb[-1] if gamma is not None else E.grp_waits("tr")
    return rdy


def emit_store_output(meta):
    nc, E = meta["nc"], meta["E"]
    XI, BIG, PS = meta["XI"], meta["BIG"], meta["PS"]
    ident = meta["MISC"][:, 0:128]
    xf32 = BIG[:].bitcast(F32)
    E.barrier()
    guard = {k: v for k, v in meta["ps_guard"].items() if k in (6, 7)}
    v_out = None
    for tt in range(NT):
        sl = tt % 4
        xslot = xf32[:, sl * 1024:(sl + 1) * 1024]
        for dt in range(ND):
            bank = (tt * ND + dt) % 2
            pst = PS[6 + bank]
            w = [guard[6 + bank]] if (6 + bank) in guard else []
            vtw = E.op("tensor",
                       (lambda e, s=XI[:, dt, tt * 128:(tt + 1) * 128], p=pst:
                        e.transpose(p[:, 0:128], s, ident)),
                       waits=w + ([guard["s" + str(sl)]] if ("s" + str(sl)) in guard and dt == 0 else []),
                       tick=True)
            ve = E.op("vector",
                      (lambda e, p=pst, d=xslot.rearrange("p (d q) -> p d q", q=128)[:, dt, :]:
                       e.tensor_copy(d, p[:, 0:128])),
                      waits=[(E.ms["tensor"], vtw)], tick=True)
            guard[6 + bank] = (E.ms["vector"], ve)
            last_ev = ve
        v_out = E.dma("sync", meta["y"][tt * 128:(tt + 1) * 128, :], xslot,
                      grp="y", waits=[(E.ms["vector"], last_ev)])
        guard["s" + str(sl)] = v_out
    E.wait("sync", [v_out])


def emit_attn_a(meta):
    nc, E = meta["nc"], meta["E"]
    dram = meta["dram"]
    NF, AD, XI, BIG = meta["NF"], meta["AD"], meta["XI"], meta["BIG"]
    PS = meta["PS"]
    TAB, SC32, MISCB, ONESB = meta["TAB"], meta["SC32"], meta["MISCB"], meta["ONESB"]
    nf_rdy = meta["v_nf_ready"]

    ev_t1 = E.dma("sync", TAB[:, :, 0:64], dram["ropeA_cos"])
    ev_t2 = E.dma("sync", TAB[:, :, 64:128], dram["ropeA_sin"])
    cosT = TAB[:, :, 0:64]
    sinT = TAB[:, :, 64:128]

    ev_mask = E.dma("sync", MISCB[:, 512:640], dram["mask01T"])

    AD_ = AD[:]
    qF = AD_[:, 0:2048]
    kF = AD_[:, 2048:4096]
    vFb = AD_[:, 4096:6144]
    qT = AD_[:, 6144:8192]
    kT = AD_[:, 8192:10240]
    vT = AD_[:, 10240:12288]
    qrT = AD_[:, 12288:14336]
    krT = AD_[:, 14336:16384]
    oT_all = BIG[:, 8192:24576].rearrange(
        "p (tt c) -> p tt c", c=1024)
    ndT = BIG[:, 4096:6176].rearrange("p (c a) -> p c a", a=130)
    bigf = BIG[:].bitcast(F32)
    Rf32 = SC32[:, 0:128]
    Rb16 = MISCB[:, 0:128]
    stT = MISCB[:, 256:512].rearrange("p (a c) -> p a c", a=2)
    mask01 = MISCB[:, 512:640]

    W = {"q": dram["aWq"], "k": dram["aWk"], "v": dram["aWv"]}

    def wslice(w, hp):
        return W[w][hp]

    order = [(hp, w) for hp in range(8) for w in ("q", "k", "v")]
    wslots = {}
    ws = WStream(meta)

    def load_w(i):
        hp, w = order[i]
        sl, view, ev = ws.load(wslice(w, hp))
        wslots[(hp, w)] = (view, ev, sl)

    load_w(0)
    load_w(1)
    nload = [2]

    ps_guard = meta["ps_guard"]

    def mm(bank_i, out_sl, lhsT, rhs, first, last, waits=(), tp=None, po=(0, 128),
           notick=False, msem=None):
        w = list(waits)
        if first and bank_i in ps_guard:
            w.append(ps_guard[bank_i])
        return E.op("tensor",
                    (lambda e, o=PS[bank_i][po[0]:po[1], out_sl[0]:out_sl[1]],
                     l=lhsT, r=rhs, f=first, s=last, tp=tp:
                     e.matmul(o, l, r, start=f, stop=s, tile_position=tp,
                              skip_group_check=True)),
                    waits=w, tick=(last and not notick), tick_sem=msem)

    scan_done = None
    for hp in range(8):
        proj_wait = [nf_rdy, ev_t1, ev_t2, ev_mask] if hp == 0 else []
        if scan_done is not None:
            proj_wait = [(E.ms["tensor"], scan_done)]
        evacs = []
        for wi, w in enumerate(("q", "k", "v")):
            buf, ev_w, sl = wslots[(hp, w)]
            dstF = (qF, kF, vFb)[wi]
            lastmm = None
            for tb in range(4):
                bank_i = tb % 2
                for kk in range(ND):
                    lastmm = mm(bank_i, (0, 512), buf[:, kk, 0:128],
                                NF[:, kk, tb * 512:(tb + 1) * 512],
                                first=(kk == 0), last=(kk == ND - 1),
                                waits=([ev_w] + proj_wait)
                                if (tb == 0 and kk == 0) else ())
                ve = E.op("scalar",
                          (lambda e, b=PS[bank_i], d=dstF[:, tb * 512:(tb + 1) * 512]:
                           e.copy(d, b[:, 0:512])),
                          waits=[(E.ms["tensor"], lastmm)], tick=True)
                ps_guard[bank_i] = (E.ms["scalar"], ve)
                evacs.append(ve)
            ws.done(sl, (E.ms["tensor"], lastmm))
            if nload[0] < len(order):
                load_w(nload[0])
                nload[0] += 1
        v_evac = evacs[-1]
        ev_tq = E.dma_tr("sync", qT.rearrange("p (tt c) -> p tt c", c=128), qF,
                         waits=[(E.ms["scalar"], v_evac)])
        ev_tk = E.dma_tr("sync", kT.rearrange("p (tt c) -> p tt c", c=128), kF)
        ev_tv = E.dma_tr("sync", vT.rearrange("p (tt c) -> p tt c", c=128), vFb)
        def rope_phi(eng, src, dst, scr0, scr1, wait_tr, sguard):
            v_d = None
            for h in range(2):
                s4 = src[:, h * 1024:(h + 1) * 1024].rearrange(
                    "p (tt a c) -> p tt a c", a=2, c=64)
                d4 = dst[:, h * 1024:(h + 1) * 1024].rearrange(
                    "p (tt a c) -> p tt a c", a=2, c=64)
                r0 = scr0.rearrange("p (tt a c) -> p tt a c", a=2, c=64)
                r1 = scr1.rearrange("p (tt a c) -> p tt a c", a=2, c=64)
                tsl = slice(h * 8, (h + 1) * 8)
                cb = cosT[:, tsl, :].unsqueeze(2).broadcast_to([128, 8, 2, 64])
                sb_lo = sinT[:, tsl, 0:32].unsqueeze(2).broadcast_to([128, 8, 2, 32])
                sb_hi = sinT[:, tsl, 32:64].unsqueeze(2).broadcast_to([128, 8, 2, 32])
                w0 = [wait_tr] if h == 0 else []
                if h == 0 and sguard:
                    w0.append(sguard)
                if h == 1 and eng != "vector" and v_d is not None:
                    w0.append((E.ms["vector"], v_d))
                E.op(eng, (lambda e, r0=r0, s4=s4, cb=cb: e.tensor_mul(r0, s4, cb)),
                     waits=w0)
                E.op(eng, (lambda e, r1=r1, s4=s4, sb_lo=sb_lo:
                           e.tensor_mul(r1[:, :, :, 0:32], s4[:, :, :, 32:64], sb_lo)))
                E.op(eng, (lambda e, r1=r1, s4=s4, sb_hi=sb_hi:
                           e.tensor_mul(r1[:, :, :, 32:64], s4[:, :, :, 0:32], sb_hi)))
                v_a = E.op(eng, (lambda e, r0=r0, r1=r1: e.tensor_add(r0, r0, r1)),
                           tick=True)
                r0f = scr0
                r1f = scr1
                mw = [] if eng == "vector" else [(E.ms[eng], v_a)]
                v_m = E.op("vector", (lambda e, r0f=r0f, r1f=r1f:
                                      e.tensor_scalar(r1f, r0f, 0.0, None, ALU.min)),
                           waits=mw, tick=True)
                v_e = E.op("scalar", (lambda e, r1f=r1f: e.activation(r1f, r1f, AF.Exp)),
                           waits=[(E.ms["vector"], v_m)], tick=True)
                v_d = E.op("vector", (lambda e, d4=d4, r0=r0, r1=r1:
                                      e.scalar_tensor_tensor(d4, r0, 1.0, r1,
                                                             ALU.add, ALU.max)),
                           waits=[(E.ms["scalar"], v_e)], tick=True)
            return v_d
        v_q = rope_phi("vector", qT, qrT, bigf[:, 0:1024], bigf[:, 1024:2048],
                       ev_tq, meta.get("_aa_scrq"))
        v_k = rope_phi("gpsimd", kT, krT, bigf[:, 2048:3072], bigf[:, 3072:4096],
                       ev_tk, meta.get("_aa_scrk"))
        meta["_aa_scrq"] = (E.ms["vector"], v_q)
        meta["_aa_scrk"] = (E.ms["vector"], v_k)
        ev_bq = E.dma_tr("sync", qF.rearrange("p (tt c) -> p tt c", c=128), qrT,
                         waits=[(E.ms["vector"], v_q)])
        ev_bk = E.dma_tr("sync", kF.rearrange("p (tt c) -> p tt c", c=128), krT,
                         waits=[(E.ms["vector"], v_k)])
        qFr = qF.rearrange("p (cc c) -> p cc c", c=128)
        kFr = kF.rearrange("p (cc c) -> p cc c", c=128)
        vTr = vT.rearrange("p (tt c) -> p tt c", c=128)
        krTr = krT.rearrange("p (tt c) -> p tt c", c=128)

        v_state = None
        for c in range(NT):
            cw = [ev_bq, ev_bk, ev_tv] if c == 0 else []
            mSa = mm(2, (0, 128), kFr[0:64, c, :], qFr[0:64, c, :], True, True,
                     waits=cw)
            mSb = mm(3, (0, 128), kFr[64:128, c, :], qFr[64:128, c, :], True, True,
                     tp=(64, 0), msem="tensorB")
            vsA = E.op("vector", lambda e: e.tensor_mul(stT[:, 0, :], PS[2][:, 0:128], mask01),
                       waits=[(E.ms["tensor"], mSa)], tick=True)
            vsB = E.op("vector", lambda e: e.tensor_mul(stT[:, 1, :], PS[3][:, 0:128], mask01),
                       waits=[(E.ms["tensorB"], mSb)], tick=True)
            ps_guard[2] = (E.ms["vector"], vsA)
            ps_guard[3] = (E.ms["vector"], vsB)
            nm = {}
            for hh, bank_i in enumerate((4, 5)):
                e0, e1 = (0, 64) if hh == 0 else (64, 128)
                vs = vsA if hh == 0 else vsB
                msem = None if hh == 0 else "tensorB"
                w0 = [(E.ms["vector"], vs)]
                mm(bank_i, (0, 64), stT[:, hh, :], vTr[:, c, e0:e1], True, False,
                   waits=w0)
                last = (c == 0)
                nm[hh] = mm(bank_i, (64, 65), stT[:, hh, :], ONESB[:, 0:1], False,
                            last, msem=msem)
                if c > 0:
                    nm[hh] = mm(bank_i, (0, 65), qFr[e0:e1, c, :], Rb16[e0:e1, 0:65],
                                False, True, waits=[(E.ms["vector"], v_state)],
                                tp=None if hh == 0 else (64, 0), msem=msem)
            dw = [] if v_state is None else [(E.ms["vector"], v_state)]
            mm(6, (0, 64), krTr[:, c, 0:64], vTr[:, c, 0:64], True, False, waits=dw,
               po=(0, 64))
            mda = mm(6, (64, 65), krTr[:, c, 0:64], ONESB[:, 0:1], False, True,
                     po=(0, 64))
            mm(7, (0, 64), krTr[:, c, 64:128], vTr[:, c, 64:128], True, False,
               waits=dw, tp=(0, 64), po=(64, 128), msem="tensorB")
            mdb = mm(7, (64, 65), krTr[:, c, 64:128], ONESB[:, 0:1], False, True,
                     tp=(0, 64), po=(64, 128), msem="tensorB")
            dwait = [(E.ms["tensor"], mda), (E.ms["tensorB"], mdb)]
            if c == 0:
                E.op("vector", lambda e: e.tensor_copy(Rf32[0:64, 0:65], PS[6][0:64, 0:65]),
                     waits=dwait)
                vsa = E.op("vector", lambda e: e.tensor_copy(Rf32[64:128, 0:65], PS[7][64:128, 0:65]),
                           tick=True)
            else:
                E.op("vector", lambda e: e.tensor_add(Rf32[0:64, 0:65], Rf32[0:64, 0:65],
                                                      PS[6][0:64, 0:65]),
                     waits=dwait)
                vsa = E.op("vector", lambda e: e.tensor_add(Rf32[64:128, 0:65], Rf32[64:128, 0:65],
                                                            PS[7][64:128, 0:65]),
                           tick=True)
            v_state = E.op("vector", lambda e: e.tensor_copy(Rb16[:, 0:65], Rf32[:, 0:65]),
                           tick=True)
            ps_guard[6] = (E.ms["vector"], v_state)
            ps_guard[7] = (E.ms["vector"], v_state)
            for hh in range(2):
                bank_i = 4 + hh
                mw = ([(E.ms["tensor"], nm[0])] if hh == 0
                      else [(E.ms["tensorB"], nm[1])])
                vsc = E.op("scalar",
                           (lambda e, b=PS[bank_i], d=ndT[:, c, hh * 65:(hh + 1) * 65]:
                            e.copy(d, b[:, 0:65])),
                           waits=mw, tick=True)
                ps_guard[bank_i] = (E.ms["scalar"], vsc)
        SM = meta["SMALL"]
        v_nd = E.op("vector", lambda e: e.tensor_scalar(
            SM[:, 128:160], ndT[:, :, :].rearrange("p c (a b) -> p c a b", b=65)[:, :, :, 64],
            1e-6, None, ALU.max), waits=[(E.ms["scalar"], E.ms["scalar"].n)])
        v_nd = E.op("vector", lambda e: e.reciprocal(SM[:, 160:192], SM[:, 128:160]),
                    tick=True)
        nd4 = ndT[:, :, :].rearrange("p c (a b) -> p c a b", b=65)[:, :, :, 0:64]
        rd4 = SM[:, 160:192].rearrange("p (c a) -> p c a", a=2).unsqueeze(3).broadcast_to(
            [128, 16, 2, 64])
        o4 = oT_all[:, :, hp * 128:(hp + 1) * 128].rearrange(
            "p c (a b) -> p c a b", b=64)
        v_div = E.op("vector", (lambda e, nd4=nd4, rd4=rd4, o4=o4:
                                e.tensor_mul(o4, nd4, rd4)), tick=True)
        scan_done = E.op("tensor", lambda e: e.nop(),
                         waits=[(E.ms["tensorB"], E.ms["tensorB"].n),
                                (E.ms["vector"], v_div)], tick=True)

    v_oT = E.op("scalar", lambda e: e.nop(), tick=True)
    for tt in range(NT):
        E.dma_tr("sync", NF[:, :, tt * 128:(tt + 1) * 128],
                 oT_all[:, tt, :],
                 waits=[(E.ms["scalar"], v_oT),
                        (E.ms["vector"], E.ms["vector"].n)] if tt == 0 else [])
    ev_of = E.grp_waits("tr")
    emit_proj_accum(meta, dram["aWo"], NF, XI, ev_of, ps_guard)
    emit_dbg(meta, "x1F", lambda: XI[:], (128, ND, T), F32,
             [(E.ms["vector"], E.ms["vector"].n)])


def emit_proj_accum(meta, wdram, srcF, XI, src_rdy, ps_guard, bias_col=None,
                    n_in=ND, accumulate=True, outF=None, act=None):
    nc, E = meta["nc"], meta["E"]
    PS = meta["PS"]
    ws = WStream(meta)
    loads = {}

    def load_m(j):
        sl, view, ev = ws.load(wdram[j])
        loads[j] = (view[:, 0:n_in, :], ev, sl)

    load_m(0)
    load_m(1)
    for m in range(ND):
        buf, ev_w, sl = loads[m // 2]
        half = m % 2
        for tb in range(4):
            bank_i = tb % 2
            lastmm = None
            for kk in range(n_in):
                w = []
                if tb == 0 and kk == 0:
                    w = [ev_w, src_rdy]
                if kk == 0 and bank_i in ps_guard:
                    w.append(ps_guard[bank_i])
                lastmm = E.op("tensor",
                              (lambda e, b=buf, kk=kk, tb=tb, bi=bank_i, f=(kk == 0),
                               s=(kk == n_in - 1), h0=half * 128:
                               e.matmul(PS[bi][:, 0:512], b[:, kk, h0:h0 + 128],
                                        srcF[:, kk, tb * 512:(tb + 1) * 512],
                                        start=f, stop=s, skip_group_check=True)),
                              waits=w, tick=(kk == n_in - 1))
            if accumulate:
                ve = E.op("vector",
                          (lambda e, bi=bank_i, m=m, tb=tb:
                           e.tensor_add(XI[:, m, tb * 512:(tb + 1) * 512],
                                        XI[:, m, tb * 512:(tb + 1) * 512],
                                        PS[bi][:, 0:512])),
                          waits=[(E.ms["tensor"], lastmm)], tick=True)
                ps_guard[bank_i] = (E.ms["vector"], ve)
            else:
                dst = (outF if outF is not None else XI)
                if act is not None:
                    ve = E.op("scalar",
                              (lambda e, bi=bank_i, d=dst[:, m, tb * 512:(tb + 1) * 512]:
                               e.activation(d, PS[bi][:, 0:512], act)),
                              waits=[(E.ms["tensor"], lastmm)], tick=True)
                else:
                    ve = E.op("scalar",
                              (lambda e, bi=bank_i, d=dst[:, m, tb * 512:(tb + 1) * 512]:
                               e.copy(d, PS[bi][:, 0:512])),
                              waits=[(E.ms["tensor"], lastmm)], tick=True)
                ps_guard[bank_i] = (E.ms["scalar"], ve)
        if m % 2 == 1:
            ws.done(sl, (E.ms["tensor"], E.ms["tensor"].n))
            if m // 2 + 2 < ND // 2:
                load_m(m // 2 + 2)


def emit_ffn_block(meta, app):
    meta["_ffn_app"] = app
    nc, E = meta["nc"], meta["E"]
    dram = meta["dram"]
    XI, BIG, NF, AD = meta["XI"], meta["BIG"], meta["NF"], meta["AD"]
    PS, PSALL = meta["PS"], meta["PSALL"]
    ps_guard = meta["ps_guard"]
    SC32, STAT, SMALL, TAB, MISC = (meta["SC32"], meta["STAT"], meta["SMALL"],
                                    meta["TAB"], meta["MISC"])
    xo = meta["xo_spill"]

    pw = [(E.ms["vector"], E.ms["vector"].n), (E.ms["scalar"], E.ms["scalar"].n),
          (E.ms["tensor"], E.ms["tensor"].n)]
    ev_sp = E.dma("sync", xo[:], XI[:], grp="spill", waits=pw, ngrp=2)
    rdy_n = emit_norm_F(meta, XI, NF, eps=1e-5, waits=pw)
    meta["_blk_rdy_tb"] = meta["_norm_rdy_tb"]
    v_up = None
    for dt in range(ND):
        v_up = E.op("scalar",
                    (lambda e, dt=dt: e.copy(XI[:, dt, :], NF[:, dt, :])),
                    waits=([ev_sp] + rdy_n) if dt == 0 else [], tick=True)
    meta["_g_ready"] = (E.ms["scalar"], v_up)
    emit_dbg(meta, f"a{app}_n2F", lambda: meta["NF"][:], (128, ND, T), BF16,
             rdy_n)

    for l in range(2):
        emit_ffn_layer(meta, app, l)

    rb_guard = [(E.ms["vector"], E.ms["vector"].n)]
    bigf = BIG[:].bitcast(F32)
    for dt in range(ND):
        sl = dt % 2
        stg = bigf[:, sl * 2048:(sl + 1) * 2048]
        w = list(rb_guard) if dt < 2 else [meta[f"_rb{sl}"]]
        ev = E.dma("sync", stg, xo[:, dt, :], grp="spill", waits=w, ngrp=2)
        v_a = E.op("vector",
                   (lambda e, dt=dt, stg=stg: e.tensor_add(XI[:, dt, :],
                                                           XI[:, dt, :], stg)),
                   waits=[ev], tick=True)
        meta[f"_rb{sl}"] = (E.ms["vector"], v_a)
    emit_dbg(meta, f"x{2 if app == 0 else 4}F", lambda: XI[:], (128, ND, T), F32,
             [(E.ms["vector"], E.ms["vector"].n)])


def emit_ffn_layer(meta, app, l):
    nc, E = meta["nc"], meta["E"]
    dram = meta["dram"]
    XI, BIG, NF, AD = meta["XI"], meta["BIG"], meta["NF"], meta["AD"]
    PS, PSALL = meta["PS"], meta["PSALL"]
    ps_guard = meta["ps_guard"]
    SC32, STAT, SMALL, TAB = meta["SC32"], meta["STAT"], meta["SMALL"], meta["TAB"]
    NFR = meta["NFR"]

    app = meta.get("_ffn_app", 0)
    par = (app * 2 + l) % 2
    rb = 256 + par * 144
    qs_row = SC32[0:1, rb:rb + 64]
    ks_row = SC32[0:1, rb + 64:rb + 128]
    lng = SC32[:, rb + 128:rb + 136]
    lnb = SC32[:, rb + 136:rb + 144]
    ev_qs = E.dma("sync", qs_row, dram["fqs"][l:l + 1, :], grp="io")
    ev_ks = E.dma("sync", ks_row, dram["fks"][l:l + 1, :], grp="io")
    ev_ln = E.dma("sync", lng, dram["flng"][l], grp="io")
    ev_lnb = E.dma("sync", lnb, dram["flnb"][l], grp="io")
    snapT = E.snap_all()
    ev_tb = []
    for i, nm in enumerate(("rotF_qcos", "rotF_qsin", "rotF_kcos", "rotF_ksin")):
        ev_tb.append(E.dma("gpsimd", TAB[:, :, i * 64:(i + 1) * 64], dram[nm],
                           grp="w", ngrp=2, waits=snapT if i == 0 else []))
    qs_bc = SC32[:, 544:608]
    ks_bc = SC32[:, 608:672]
    mm0 = E.op("tensor",
               (lambda e: e.matmul(PS[6][:, 0:64], SMALL[0:1, 0:128],
                                   qs_row, start=True, stop=True,
                                   skip_group_check=True)),
               waits=[ev_qs, meta["v_zo"]] + ([ps_guard[6]] if 6 in ps_guard else []),
               tick=True)
    mm1 = E.op("tensor",
               (lambda e: e.matmul(PS[7][:, 0:64], SMALL[0:1, 0:128],
                                   ks_row, start=True, stop=True,
                                   skip_group_check=True)),
               waits=[ev_ks] + ([ps_guard[7]] if 7 in ps_guard else []), tick=True)
    v_qsb = E.op("vector", lambda e: e.tensor_scalar(qs_bc, PS[6][:, 0:64], 8.0,
                                                     None, ALU.mult),
                 waits=[(E.ms["tensor"], mm0)], tick=True)
    v_ksb = E.op("vector", lambda e: e.tensor_copy(ks_bc, PS[7][:, 0:64]),
                 waits=[(E.ms["tensor"], mm1)], tick=True)
    ps_guard[6] = (E.ms["vector"], v_qsb)
    ps_guard[7] = (E.ms["vector"], v_ksb)
    tQC, tQS = TAB[:, :, 0:64], TAB[:, :, 64:128]
    tKC, tKS = TAB[:, :, 128:192], TAB[:, :, 192:256]
    mQC, mKC = TAB[:, :, 256:320], TAB[:, :, 320:384]
    mQS, mKS = TAB[:, :, 384:448], TAB[:, :, 448:512]

    def bcn(col_ap, n=64):
        return col_ap.unsqueeze(1).broadcast_to([128, NT, n])

    E.op("vector", lambda e: e.tensor_mul(mQC, tQC, bcn(qs_bc)), waits=ev_tb)
    E.op("vector", lambda e: e.tensor_mul(mKC, tKC, bcn(ks_bc)))
    E.op("vector", lambda e: e.tensor_mul(mQS[:, :, 0:32], tQS[:, :, 0:32],
                                          bcn(qs_bc[:, 32:64], 32)))
    E.op("vector", lambda e: e.tensor_mul(mQS[:, :, 32:64], tQS[:, :, 32:64],
                                          bcn(qs_bc[:, 0:32], 32)))
    E.op("vector", lambda e: e.tensor_mul(mKS[:, :, 0:32], tKS[:, :, 0:32],
                                          bcn(ks_bc[:, 32:64], 32)))
    v_mg = E.op("vector", lambda e: e.tensor_mul(mKS[:, :, 32:64], tKS[:, :, 32:64],
                                                 bcn(ks_bc[:, 0:32], 32)), tick=True)

    def h_rhs(kk, t0, t1):
        if kk < 4:
            return NF[:, kk, t0:t1]
        return NFR[:, kk, 15 + t0:15 + t1]
    qkT = BIG[:, 8192:24576].rearrange("p (tt c) -> p tt c", c=1024)
    vTb = BIG[:, 0:8192].rearrange("p (tt c) -> p tt c", c=512)
    evsl = [AD[:, 0:2048], AD[:, 2048:4096]]
    esl_guard = {}
    vt_ev = {}
    wq = dram["fwqkv"][l]
    wslots = {}
    ws = WStream(meta)

    def load_qkv(j):
        sl, view, ev = ws.load(wq[j])
        wslots[j] = (view, ev, sl)

    load_qkv(0)
    load_qkv(1)
    if l > 0:
        v_c = None
        for dt in range(ND):
            v_c = E.op("scalar",
                       (lambda e, dt=dt: e.copy(NF[:, dt, :], XI[:, dt, :])),
                       waits=[(E.ms["vector"], E.ms["vector"].n)] if dt == 0 else [],
                       tick=True)
        nf_tb = [[(E.ms["scalar"], v_c)]] * 4
    else:
        nf_tb = meta["_blk_rdy_tb"]
    for mi in range(12):
        buf, ev_w, sl = wslots[mi // 2]
        halfq = (mi % 2) * 128
        lastmm = None
        for tb in range(4):
            bank_i = tb % 2
            for kk in range(ND):
                w = []
                if kk == 0:
                    if tb == 0:
                        w = [ev_w]
                    if mi == 0:
                        w += list(E._flat(nf_tb[tb]))
                        if tb > 0:
                            w += list(E._flat(nf_tb[tb - 1]))
                    if bank_i in ps_guard:
                        w.append(ps_guard[bank_i])
                lastmm = E.op("tensor",
                              (lambda e, b=buf, kk=kk,
                               rhs=h_rhs(kk, tb * 512, (tb + 1) * 512),
                               bi=bank_i, f=(kk == 0), s=(kk == ND - 1), h0=halfq:
                               e.matmul(PS[bi][:, 0:512], b[:, kk, h0:h0 + 128],
                                        rhs, start=f, stop=s,
                                        skip_group_check=True)),
                              waits=w, tick=(kk == ND - 1))
            ew = [(E.ms["tensor"], lastmm)]
            if (mi % 2) in esl_guard and tb == 0:
                ew.append(esl_guard[mi % 2])
            ve = E.op("scalar",
                      (lambda e, bi=bank_i,
                       d=evsl[mi % 2][:, tb * 512:(tb + 1) * 512]:
                       e.copy(d, PS[bi][:, 0:512])),
                      waits=ew, tick=True)
            ps_guard[bank_i] = (E.ms["scalar"], ve)
        if mi % 2 == 1:
            ws.done(sl, (E.ms["tensor"], lastmm))
            if mi // 2 + 2 < 6:
                load_qkv(mi // 2 + 2)
        if mi < 4:
            dst = qkT[:, :, mi * 128:(mi + 1) * 128]
        elif mi < 8:
            dst = qkT[:, :, 512 + (mi - 4) * 128:512 + (mi - 3) * 128]
        else:
            dst = vTb[:, :, (mi - 8) * 128:(mi - 7) * 128]
        ev_t = E.dma_tr("sync", dst, evsl[mi % 2], waits=[(E.ms["scalar"], ve)])
        esl_guard[mi % 2] = ev_t
        if mi >= 8:
            vt_ev[mi - 8] = ev_t
    qkT_rdy = E.grp_waits("tr")

    MB = NF[:, 0, 0:1280]
    mw0 = [(E.ms["tensor"], E.ms["tensor"].n), meta["ev_maskU"], meta["ev_maskL"]]
    E.op("vector", lambda e: e.memset(MB, 0.0), waits=mw0)
    E.op("vector", lambda e: e.tensor_copy(MB[:, 0:128], meta["MISC"][:, 256:384]))
    E.op("vector", lambda e: e.tensor_copy(MB[:, 512:640], meta["MISC"][:, 128:256]))
    E.op("vector", lambda e: e.tensor_copy(MB[:, 1152:1280], meta["MISC"][:, 128:256]))

    nfrf = NFR[:].rearrange("p a b -> p (a b)").bitcast(F32)
    sq = nfrf[:, 2064:3088]
    u_s = nfrf[:, 3088:3600]
    w_s = nfrf[:, 3600:4112]
    u_g = nfrf[:, 4112:4624]
    w_g = nfrf[:, 4624:5136]
    qF = AD[:, 0:8192].rearrange("p (j t) -> p j t", t=2048)
    kF = AD[:, 8192:16384].rearrange("p (j t) -> p j t", t=2048)
    v_rot_last = None
    g_rn_prev = None
    for tt in range(NT):
        w0 = list(qkT_rdy) + [(E.ms["vector"], v_mg)] if tt == 0 else []
        v_sq = E.op("scalar", (lambda e, tt=tt: e.activation(sq, qkT[:, tt, :],
                                                             AF.Square)),
                    waits=w0, tick=True)
        E.op("vector", (lambda e: e.reduce_sum(
            STAT[:, 32:48], sq.rearrange("p (s c) -> p s c", c=64), axis=AX.X)),
            waits=[(E.ms["scalar"], v_sq)] +
                  ([(E.ms["gpsimd"], g_rn_prev)] if g_rn_prev else []))
        v_ns = E.op("vector", lambda e: e.tensor_scalar(STAT[:, 32:48], STAT[:, 32:48],
                                                        1e-24, None, ALU.max),
                    tick=True)
        E.op("scalar", lambda e: e.activation(STAT[:, 48:64], STAT[:, 32:48], AF.Ln),
             waits=[(E.ms["vector"], v_ns)])
        v_rn = E.op("scalar", lambda e: e.activation(STAT[:, 32:48], STAT[:, 48:64],
                                                     AF.Exp, scale=-0.5), tick=True)
        for half in range(2):
            eng = "vector" if half == 0 else "gpsimd"
            seg = qkT[:, tt, half * 512:(half + 1) * 512].rearrange(
                "p (s c) -> p s c", c=64)
            rn_bc = STAT[:, 32 + half * 8:40 + half * 8].unsqueeze(2).broadcast_to(
                [128, 8, 64])
            cosb = (mQC if half == 0 else mKC)[:, tt, :]
            sinb = (mQS if half == 0 else mKS)[:, tt, :]
            cos_bc = cosb.unsqueeze(1).broadcast_to([128, 8, 64])
            sin_lo = sinb[:, 0:32].unsqueeze(1).broadcast_to([128, 8, 32])
            sin_hi = sinb[:, 32:64].unsqueeze(1).broadcast_to([128, 8, 32])
            uv = (u_s if half == 0 else u_g).rearrange("p (s c) -> p s c", c=64)
            wv = (w_s if half == 0 else w_g).rearrange("p (s c) -> p s c", c=64)
            hw = [(E.ms["scalar"], v_rn)]
            if half == 1 and tt == 0:
                hw.append((E.ms["vector"], v_mg))
            g_rn = E.op(eng, (lambda e, seg=seg, rn_bc=rn_bc, uv=uv:
                              e.tensor_mul(uv, seg, rn_bc)),
                        waits=hw, tick=True)
            if half == 1:
                g_rn_prev = g_rn
            E.op(eng, (lambda e, uv=uv, wv=wv, sin_lo=sin_lo:
                       e.tensor_mul(wv[:, :, 0:32], uv[:, :, 32:64], sin_lo)))
            E.op(eng, (lambda e, uv=uv, wv=wv, sin_hi=sin_hi:
                       e.tensor_mul(wv[:, :, 32:64], uv[:, :, 0:32], sin_hi)))
            E.op(eng, (lambda e, uv=uv, cos_bc=cos_bc:
                       e.tensor_mul(uv, uv, cos_bc)))
            v_rot_last = E.op(eng, (lambda e, seg=seg, uv=uv, wv=wv:
                                    e.tensor_add(seg, uv, wv)), tick=True)
            if half == 0:
                v_rot_q = v_rot_last
            else:
                g_rot_k = v_rot_last
        E.dma_tr("sync", qF[:, :, tt * 128:(tt + 1) * 128], qkT[:, tt, 0:512],
                 waits=[(E.ms["vector"], v_rot_q)])
        E.dma_tr("sync", kF[:, :, tt * 128:(tt + 1) * 128], qkT[:, tt, 512:1024],
                 waits=[(E.ms["gpsimd"], g_rot_k)])
    qkF_rdy = E.grp_waits("tr")

    srow = [BIG[:, 8192:9216], BIG[:, 9216:10240],
            BIG[:, 14336:15360], BIG[:, 15360:16384]]
    stT = BIG[:, 10240:14336].rearrange("p (j i) -> p j i", i=512)
    stTB = TAB[:].rearrange("p a b -> p (a b)")[:, 0:4096].rearrange(
        "p (j i) -> p j i", i=512)
    oTF = BIG[:, 16384:24576].rearrange("p (j t) -> p j t", t=2048)
    SREG = (0, 1024, 3072)
    sg = {}
    st_ev = {}
    strip = 0
    reg_first = [True, True, True]
    for pj in range(4):
        for w in range(4):
            for it in range(4):
                for hh in range(2):
                    e0, e1 = (0, 64) if hh == 0 else (64, 128)
                    r = strip % 3
                    strip += 1
                    base = SREG[r]
                    msem = None if hh == 0 else "tensorB"
                    mssem = E.ms["tensor" if hh == 0 else "tensorB"]
                    span = (it + 1) * 128 if w == 0 else 640
                    jcol0 = (w - 1) * 512 + it * 128 if w > 0 else 0
                    icol = w * 512 + it * 128
                    Sreg = PSALL[:, base: base + span]
                    wk = list(qkF_rdy) if (pj == 0 and w == 0 and hh == 0
                                           and it == 0) else []
                    if reg_first[r]:
                        reg_first[r] = False
                        for b in ((0, 1), (2, 3), (6, 7))[r]:
                            if b in ps_guard:
                                wk.append(ps_guard[b])
                    if ("s", r) in sg:
                        wk.append(sg[("s", r)])
                    pieces = [512, span - 512] if span > 512 else [span]
                    off = 0
                    mS = None
                    for pi, nw in enumerate(pieces):
                        if nw <= 0:
                            continue
                        lastp = (pi == len(pieces) - 1) or pieces[-1] <= 0
                        mS = E.op("tensor",
                                  (lambda e,
                                   o=PSALL[:, base + off: base + off + nw],
                                   lh=qF[e0:e1, pj, icol:icol + 128],
                                   rh=kF[e0:e1, pj, jcol0 + off:jcol0 + off + nw],
                                   tp=(64, 0) if hh else None:
                                   e.matmul(o, lh, rh, start=True, stop=True,
                                            tile_position=tp,
                                            skip_group_check=True)),
                                  waits=wk if off == 0 else [],
                                  tick=lastp, tick_sem=msem)
                        off += nw
                    msel = MB[:, 0:640] if w > 0 else MB[:, 1280 - span:1280]
                    macol = SMALL[:, 140 + r:141 + r]
                    sucol = SMALL[:, 144 + r:145 + r]
                    recol = SMALL[:, 148 + r:149 + r]
                    E.op("vector",
                         (lambda e, S=Sreg, M=msel:
                          e.tensor_add(S, S, M)), waits=[(mssem, mS)])
                    v_ttr = E.op("vector",
                                 (lambda e, S=Sreg, macol=macol:
                                  e.reduce_max(macol, S, axis=AX.X,
                                               negate=True)), tick=True)
                    ssl = srow[(2 * it + hh) % 4]
                    ew = [(E.ms["vector"], v_ttr)]
                    if ("sr", (2 * it + hh) % 4) in sg:
                        ew.append(sg[("sr", (2 * it + hh) % 4)])
                    v_exp = E.op("scalar",
                                 (lambda e, S=Sreg, d=ssl, span=span,
                                  macol=macol, sucol=sucol:
                                  e.activation(d[:, 0:span], S, AF.Exp,
                                               bias=macol,
                                               accum_out=sucol)),
                                 waits=ew, tick=True)
                    sg[("s", r)] = (E.ms["scalar"], v_exp)
                    rcw = [(E.ms["scalar"], v_exp)]
                    if ("rc", r) in sg:
                        rcw.append(sg[("rc", r)])
                    v_rc = E.op("vector",
                                (lambda e, recol=recol, sucol=sucol:
                                 e.reciprocal(recol, sucol)),
                                waits=rcw, tick=True)
                    v_sc = E.op("scalar",
                                (lambda e, d=ssl, span=span, recol=recol:
                                 e.activation(d[:, 0:span], d[:, 0:span],
                                              AF.Copy, scale=recol)),
                                waits=[(E.ms["vector"], v_rc)], tick=True)
                    sg[("rc", r)] = (E.ms["scalar"], v_sc)
                    nblk = span // 128
                    stT_h = stT if hh == 0 else stTB
                    j0 = 4 if w == 0 else it
                    ev_st = E.dma_tr("sync",
                                     stT_h[:, j0:j0 + nblk,
                                           it * 128:(it + 1) * 128],
                                     ssl[:, 0:span],
                                     waits=[(E.ms["scalar"], v_sc)])
                    sg[("sr", (2 * it + hh) % 4)] = ev_st
                    st_ev[(it, hh)] = ev_st
            for hh in range(2):
                e0, e1 = (0, 64) if hh == 0 else (64, 128)
                msem = None if hh == 0 else "tensorB"
                mssem = E.ms["tensor" if hh == 0 else "tensorB"]
                ob = 4 if hh == 0 else 5
                stT_h = stT if hh == 0 else stTB
                for it in range(4):
                    jr = list(range(it, it + 5)) if w > 0 else list(range(4, 5 + it))
                    ow = [st_ev[(it, hh)]]
                    if (ob, "o") in sg:
                        ow.append(sg.pop((ob, "o")))
                    elif ob in ps_guard:
                        ow.append(ps_guard.pop(ob))
                    if w == 0 and it == 0:
                        ow.append(vt_ev[pj])
                    lastmm = None
                    for ji, jrel in enumerate(jr):
                        jt_abs = (w - 1) * 4 + jrel
                        lastmm = E.op("tensor",
                                      (lambda e,
                                       o=PSALL[e0:e1, ob * 512:ob * 512 + 128],
                                       lh=vTb[:, jt_abs, pj * 128 + e0:pj * 128 + e1],
                                       rh=stT_h[:, jrel, it * 128:(it + 1) * 128],
                                       f=(ji == 0), s=(ji == len(jr) - 1),
                                       tp=None if hh == 0 else (0, 64):
                                       e.matmul(o, lh, rh, start=f, stop=s,
                                                tile_position=tp,
                                                skip_group_check=True)),
                                      waits=ow if ji == 0 else [],
                                      tick=(ji == len(jr) - 1), tick_sem=msem)
                    v_oe = E.op("scalar",
                                (lambda e, it=it,
                                 d=oTF[e0:e1, pj, w * 512 + it * 128:w * 512 + (it + 1) * 128],
                                 s=PSALL[e0:e1, ob * 512:ob * 512 + 128]:
                                 e.copy(d, s)),
                                waits=[(mssem, lastmm)], tick=True)
                    sg[(ob, "o")] = (E.ms["scalar"], v_oe)
    v_attn = E.op("scalar", lambda e: e.nop(), tick=True)
    emit_dbg(meta, f"fl{l}_oT", lambda: BIG[:, 16384:24576], (128, 8192), BF16,
             [(E.ms["scalar"], v_attn)])
    emit_dbg(meta, f"fl{l}_qkT", lambda: BIG[:, 8192:24576], (128, 16384), BF16,
             [(E.ms["vector"], v_rot_last)])
    ps_guard[0] = ps_guard[1] = sg[("s", 0)]
    ps_guard[2] = ps_guard[3] = sg[("s", 1)]
    ps_guard[6] = ps_guard[7] = sg[("s", 2)]
    ps_guard[4] = sg[(4, "o")]
    ps_guard[5] = sg[(5, "o")]

    emit_proj_accum(meta, dram["fwo"][l], oTF, XI,
                    [(E.ms["scalar"], v_attn)], ps_guard, n_in=4)
    v_res1 = (E.ms["vector"], E.ms["vector"].n)
    emit_dbg(meta, f"fl{l}_res1", lambda: XI[:], (128, ND, T), F32, [v_res1])

    rdy2 = emit_norm_F(meta, XI, NF, eps=1e-5, gamma=lng,
                       beta=lnb, shift=True,
                       waits=[v_res1, ev_ln, ev_lnb])

    emit_ffn_glu(meta, l, rdy2)
    emit_dbg(meta, f"fl{l}_res2", lambda: XI[:],
             (128, ND, T), F32, [(E.ms["vector"], E.ms["vector"].n)])


def emit_ffn_glu(meta, l, rdy):
    nc, E = meta["nc"], meta["E"]
    dram = meta["dram"]
    XI, NF, AD, PS, PSALL = (meta["XI"], meta["NF"], meta["AD"], meta["PS"],
                             meta["PSALL"])
    ps_guard = meta["ps_guard"]
    SC32 = meta["SC32"]
    w1r = dram["fw1"][l]
    w2r = dram["fw2"][l]
    NH = 2730
    nch = 22
    glu = AD[:, 0:11264].rearrange("p (i t) -> p i t", t=512)
    gel = AD[:, 11264:11776]
    ev_b1 = E.dma("sync", SC32[:, 672:716], dram["fb1"][l], grp="io",
                  waits=[meta["_norm_cast_evs"][0]])
    ev_b2 = E.dma("sync", SC32[:, 716:724], dram["fb2"][l], grp="io")
    rdy_tb = meta["_norm_rdy_tb"]
    wload = {}
    BIGW = meta["BIG"][:, 0:8192].rearrange("p (s b) -> p s b", s=4)
    ws = WStream(meta, key="bigw", buf=BIGW, nslot=4, grp="bigw",
                 init_extra=list(E._flat(meta["_norm_cast_evs"])) +
                 list(E._flat(E.snap_w())))

    wc, wce = meta["wcache"], meta["wcache_ev"]

    def load_w1(i):
        cw = min(128, NH - i * 128)
        sl, view, ev = ws.load_cached(w1r[i], wc["fw1"][l][i],
                                      ("fw1", l, i), wce)
        wload[i] = (view, ev, sl, cw)

    def load_w2(i):
        rw = min(128, NH - i * 128)
        sl, view, ev = ws.load_cached(w2r[i], wc["fw2"][l][i],
                                      ("fw2", l, i), wce)
        wload[("w2", i)] = (view, ev, sl, rw)

    for tb in range(4):
        t0 = tb * 512
        for _j in range(4):
            load_w1(_j)
        for i in range(nch):
            bufP, evP, slP, cwP = wload[i]
            ba = i % 2
            bg = 2 + i % 2
            lastA = lastG = None
            for ag, bi in ((0, ba), (1, bg)):
                cw = cwP
                last = None
                for kk in range(ND):
                    w = []
                    if kk == 0:
                        w = [evP]
                        if i == 0 and ag == 0:
                            w += list(E._flat(rdy_tb[tb]))
                        if bi in ps_guard:
                            w.append(ps_guard[bi])
                    last = E.op("tensor",
                                (lambda e, b=bufP, kk=kk, cw=cw, bi=bi,
                                 f=(kk == 0), s=(kk == ND - 1), t0=t0,
                                 h0=ag * 128:
                                 e.matmul(PSALL[0:cw, bi * 512:bi * 512 + 512],
                                          b[:, kk, h0:h0 + cw],
                                          NF[:, kk, t0:t0 + 512],
                                          start=f, stop=s, skip_group_check=True)),
                                waits=w, tick=(kk == ND - 1))
                if ag == 0:
                    lastA = last
                else:
                    lastG = last
            cwA = cwG = cwP
            ws.done(slP, (E.ms["tensor"], lastG))
            if i + 4 < nch:
                load_w1(i + 4)
            gw = [(E.ms["tensor"], lastG), ev_b1]
            if "pglu" in wload:
                gw.append(wload["pglu"])
            v_g = E.op("scalar",
                       (lambda e, cw=cwG, bi=bg, i=i:
                        e.activation(gel[0:cw, 0:512],
                                     PSALL[0:cw, bi * 512:bi * 512 + 512], AF.Gelu,
                                     bias=SC32[0:cw, 694 + i:695 + i])),
                       waits=gw, tick=True)
            ps_guard[bg] = (E.ms["scalar"], v_g)
            v_glu = E.op("vector",
                         (lambda e, cw=cwA, bi=ba, i=i:
                          e.scalar_tensor_tensor(glu[0:cw, i, :],
                                                 PSALL[0:cw, bi * 512:bi * 512 + 512],
                                                 SC32[0:cw, 672 + i:673 + i],
                                                 gel[0:cw, 0:512],
                                                 ALU.add, ALU.mult)),
                         waits=[(E.ms["tensor"], lastA), (E.ms["scalar"], v_g),
                                ev_b1], tick=True)
            ps_guard[ba] = (E.ms["vector"], v_glu)
            wload["pglu"] = (E.ms["vector"], v_glu)
        v_glu_all = (E.ms["vector"], E.ms["vector"].n)
        for _j in range(4):
            load_w2(_j)
        vfin = None
        for i in range(nch):
            bview, ev, sl, rw = wload[("w2", i)]
            for m in range(ND):
                w = []
                if m == 0:
                    w = [ev]
                    if i == 0:
                        w.append(v_glu_all)
                if i == 0 and m in ps_guard:
                    w.append(ps_guard[m])
                vfin = E.op("tensor",
                            (lambda e, bv=bview, m=m, rw=rw, f=(i == 0),
                             s=(i == nch - 1), i=i:
                             e.matmul(PSALL[:, m * 512:m * 512 + 512],
                                      bv[0:rw, m * 128:(m + 1) * 128],
                                      glu[0:rw, i, :], start=f, stop=s,
                                      skip_group_check=True)),
                            waits=w, tick=(m == ND - 1))
            ws.done(sl, (E.ms["tensor"], vfin))
            if i + 4 < nch:
                load_w2(i + 4)
        for m in range(ND):
            v_e = E.op("vector",
                       (lambda e, m=m, t0=t0:
                        e.scalar_tensor_tensor(XI[:, m, t0:t0 + 512],
                                               PSALL[:, m * 512:m * 512 + 512],
                                               SC32[:, 716 + m:717 + m],
                                               XI[:, m, t0:t0 + 512],
                                               ALU.add, ALU.add)),
                       waits=[(E.ms["tensor"], vfin), ev_b2], tick=True)
            ps_guard[m] = (E.ms["vector"], v_e)


def emit_attn_b(meta):
    nc, E = meta["nc"], meta["E"]
    dram = meta["dram"]
    XI, BIG, NF, AD = meta["XI"], meta["BIG"], meta["NF"], meta["AD"]
    xo = meta["xo_spill"]
    SC32 = meta["SC32"]

    pw = [(E.ms["vector"], E.ms["vector"].n), (E.ms["scalar"], E.ms["scalar"].n),
          (E.ms["tensor"], E.ms["tensor"].n)]
    ev_sp = E.dma("sync", xo[:], XI[:], grp="spill", waits=pw, ngrp=2)
    rdy_n = emit_norm_F(meta, XI, NF, eps=1e-5, waits=pw)
    v_up = None
    for dt in range(ND):
        v_up = E.op("scalar",
                    (lambda e, dt=dt: e.copy(XI[:, dt, :], NF[:, dt, :])),
                    waits=([ev_sp] + rdy_n) if dt == 0 else [], tick=True)
    meta["_g_ready"] = (E.ms["scalar"], v_up)
    snapT = E.snap_all()
    TAB = meta["TAB"]
    for i, nm in enumerate(("xpos_qcos", "xpos_qsin", "xpos_kcos", "xpos_ksin")):
        E.dma("sync", TAB[:, :, i * 128:(i + 1) * 128], dram[nm], grp="io",
              waits=snapT if i == 0 else [])
    meta["_xpos_rdy"] = E.grp_waits("io")

    for l in range(2):
        emit_retention_layer(meta, l)

    rb_guard = [(E.ms["vector"], E.ms["vector"].n),
                (E.ms["scalar"], E.ms["scalar"].n),
                (E.ms["tensor"], E.ms["tensor"].n)]
    bigf = BIG[:].bitcast(F32)
    for dt in range(ND):
        sl = dt % 2
        stg = bigf[:, sl * 2048:(sl + 1) * 2048]
        w = list(rb_guard) if dt < 2 else [meta[f"_rb{sl}"]]
        ev = E.dma("sync", stg, xo[:, dt, :], grp="spill", waits=w, ngrp=2)
        v_a = E.op("vector",
                   (lambda e, dt=dt, stg=stg: e.tensor_add(XI[:, dt, :],
                                                           XI[:, dt, :], stg)),
                   waits=[ev], tick=True)
        meta[f"_rb{sl}"] = (E.ms["vector"], v_a)
    emit_dbg(meta, "x3F", lambda: XI[:], (128, ND, T), F32,
             [(E.ms["vector"], E.ms["vector"].n)])


def emit_retention_layer(meta, l):
    nc, E = meta["nc"], meta["E"]
    dram = meta["dram"]
    XI, BIG, NF, AD = meta["XI"], meta["BIG"], meta["NF"], meta["AD"]
    PS, PSALL = meta["PS"], meta["PSALL"]
    ps_guard = meta["ps_guard"]
    SC32, STAT, SMALL, TAB, MISCB = (meta["SC32"], meta["STAT"], meta["SMALL"],
                                     meta["TAB"], meta["MISCB"])
    gam = [1.0 - 2.0 ** (-5.0 - hh) for hh in range(8)]
    gamC = [float(g ** 128) for g in gam]

    rb = 724 + (l % 2) * 48
    E.dma("sync", SC32[:, rb:rb + 8], dram["bln1s"][l], grp="io")
    E.dma("sync", SC32[:, rb + 8:rb + 16], dram["bln1b"][l], grp="io")
    E.dma("sync", SC32[:, rb + 16:rb + 24], dram["bgns"][l], grp="io")
    E.dma("sync", SC32[:, rb + 24:rb + 32], dram["bgnb"][l], grp="io")
    E.dma("sync", SC32[:, rb + 32:rb + 40], dram["bln2s"][l], grp="io")
    E.dma("sync", SC32[:, rb + 40:rb + 48], dram["bln2b"][l], grp="io")
    ev_small = E.grp_waits("io")

    gw = [meta["_g_ready"]] if l == 0 else [(E.ms["vector"], E.ms["vector"].n)]
    rdy1 = emit_norm_F(meta, XI, NF, eps=1e-6, gamma=SC32[:, rb:rb + 8],
                       beta=SC32[:, rb + 8:rb + 16],
                       waits=gw + ev_small + meta["_xpos_rdy"])
    rdy1_tb = meta["_norm_rdy_tb"]

    qT = AD[:, 0:2048]
    kT = AD[:, 2048:4096]
    vT = AD[:, 4096:6144]
    stgbuf = [AD[:, 6144:8192], AD[:, 14336:16384]]
    kFh = AD[:, 8192:10240]
    scr_u = AD[:, 10240:12288]
    scr_w = AD[:, 12288:14336]
    qTs = AD[:, 14336:14464]
    ksc = AD[:, 14464:14592]
    stT = MISCB[:, 256:384]
    decT = MISCB[:, 384:512]
    decX = MISCB[:, 512:640]
    Rf32 = SC32[:, 128:256]
    Rb16 = MISCB[:, 0:128]
    OTF = BIG[:, 8192:24576].rearrange("p (h t) -> p h t", t=2048)

    wslots = {}
    order = [(h, w) for h in range(8) for w in ("q", "k", "v")]
    Wd = {"q": dram["bWq"], "k": dram["bWk"], "v": dram["bWv"]}
    ws = WStream(meta)

    def load_w(i):
        h, w = order[i]
        sl, view, ev = ws.load(Wd[w][l, h])
        wslots[(h, w)] = (view, ev, sl)

    load_w(0)
    load_w(1)
    nload = [2]
    tQC, tQS = TAB[:, :, 0:128], TAB[:, :, 128:256]
    tKC, tKS = TAB[:, :, 256:384], TAB[:, :, 384:512]
    dec_guard = (E.ms["vector"], E.ms["vector"].n) if E.ms["vector"].n else None
    scan_done = None
    prev_scan = {0: None, 1: None}
    prev_bq = None
    prev_bk = None
    prev_hoist = None
    for h in range(8):
        par = h % 2
        qFh = stgbuf[par]
        dw = [dec_guard] if dec_guard else []
        ev_dec1 = E.dma("scalar", decT, dram["decayT"][h], grp="io", waits=dw)
        ev_dec = [ev_dec1, E.dma("scalar", decX, dram["decay_cross"][h],
                                 grp="io")]
        stg_guard = [prev_scan[par]] if prev_scan[par] else []
        ev_t = {}
        for wi, w in enumerate(("q", "k", "v")):
            buf, ev_w, sl = wslots[(h, w)]
            lastmm = None
            for tb in range(4):
                bank_i = tb % 2
                for kk in range(ND):
                    wt = []
                    if kk == 0:
                        if tb == 0:
                            wt = [ev_w]
                        if h == 0 and wi == 0:
                            wt += list(E._flat(rdy1_tb[tb]))
                        if bank_i in ps_guard:
                            wt.append(ps_guard[bank_i])
                    lastmm = E.op("tensor",
                                  (lambda e, b=buf, kk=kk, bi=bank_i, tb=tb,
                                   f=(kk == 0), s=(kk == ND - 1):
                                   e.matmul(PS[bi][:, 0:512], b[:, kk, 0:128],
                                            NF[:, kk, tb * 512:(tb + 1) * 512],
                                            start=f, stop=s, skip_group_check=True)),
                                  waits=wt, tick=(kk == ND - 1))
                ew = [(E.ms["tensor"], lastmm)]
                if tb == 0:
                    ew += list(stg_guard)
                ve = E.op("scalar",
                          (lambda e, bi=bank_i, d=qFh[:, tb * 512:(tb + 1) * 512]:
                           e.copy(d, PS[bi][:, 0:512])),
                          waits=ew, tick=True)
                ps_guard[bank_i] = (E.ms["scalar"], ve)
            ws.done(sl, (E.ms["tensor"], lastmm))
            if nload[0] < len(order):
                load_w(nload[0])
                nload[0] += 1
            if wi == 0:
                dstT, tw = qT, ([prev_bq] if prev_bq else [])
            elif wi == 1:
                dstT, tw = kT, (([prev_bk] if prev_bk else []) +
                                ([(E.ms["vector"], prev_hoist)]
                                 if prev_hoist else []))
            else:
                dstT, tw = vT, ([(E.ms["tensor"], scan_done)]
                                if scan_done is not None else [])
            ev_t[w] = E.dma_tr("sync", dstT.rearrange("p (tt c) -> p tt c", c=128),
                               qFh, waits=[(E.ms["scalar"], ve)] + tw)
            stg_guard = [ev_t[w]]

        def xpos2(src, cosT_, sinT_, wait0, eng):
            s3 = src.rearrange("p (tt c) -> p tt c", c=128)
            w3 = scr_w.rearrange("p (tt c) -> p tt c", c=128)
            s4 = src.rearrange("p (tt a b) -> p tt a b", a=64, b=2)
            u4 = scr_u.rearrange("p (tt a b) -> p tt a b", a=64, b=2)
            sf4 = sinT_.rearrange("p tt (a b) -> p tt a b", b=2)
            E.op(eng, (lambda e: e.tensor_tensor(w3, s3, cosT_, ALU.mult)),
                 waits=wait0)
            E.op(eng, (lambda e: e.tensor_tensor(u4[:, :, :, 0:1], s4[:, :, :, 1:2],
                                                 sf4[:, :, :, 0:1], ALU.mult)))
            E.op(eng, (lambda e: e.tensor_tensor(u4[:, :, :, 1:2], s4[:, :, :, 0:1],
                                                 sf4[:, :, :, 1:2], ALU.mult)))
            return E.op(eng, (lambda e: e.tensor_tensor(src, scr_w, scr_u,
                                                        ALU.add)), tick=True)
        xw = [ev_t["q"], ev_t["k"], ev_t["v"]]
        if scan_done is not None:
            xw.append((E.ms["tensor"], scan_done))
        v_xq = xpos2(qT, tQC, tQS, xw, "vector")
        v_xk = xpos2(kT, tKC, tKS, [(E.ms["vector"], v_xq)], "vector")
        ev_bq = E.dma_tr("sync", qFh.rearrange("p (tt c) -> p tt c", c=128), qT,
                         waits=[(E.ms["vector"], v_xk)])
        ev_bk = E.dma_tr("sync", kFh.rearrange("p (tt c) -> p tt c", c=128), kT)
        qFr = qFh.rearrange("p (cc c) -> p cc c", c=128)
        kFr = kFh.rearrange("p (cc c) -> p cc c", c=128)
        vTr = vT.rearrange("p (cc c) -> p cc c", c=128)
        qTs_all = scr_u.rearrange("p (cc c) -> p cc c", c=128)
        ksc_all = scr_w.rearrange("p (cc c) -> p cc c", c=128)
        E.op("vector", (lambda e: e.tensor_tensor(
            qTs_all, qFr, decX.unsqueeze(1).broadcast_to([128, NT, 128]),
            ALU.mult)), waits=[ev_bq, ev_dec[1]])
        v_hoist = E.op("vector", (lambda e: e.tensor_scalar(
            scr_w, kT, SMALL[:, 184 + h:185 + h], None, ALU.mult)), tick=True)
        v_state = None
        g2 = {}
        for c in range(NT):
            wS = [ev_bq, ev_bk] + ev_dec if c == 0 else []
            if "S" in g2:
                wS.append(g2["S"])
            mS = E.op("tensor",
                      (lambda e, c=c: e.matmul(PS[2][:, 0:128], kFr[:, c, :],
                                               qFr[:, c, :], start=True, stop=True,
                                               skip_group_check=True)),
                      waits=wS, tick=True)
            v_st = E.op("vector", (lambda e: e.tensor_mul(stT, PS[2][:, 0:128],
                                                          decT)),
                        waits=[(E.ms["tensor"], mS)], tick=True)
            g2["S"] = (E.ms["vector"], v_st)
            wn = [(E.ms["vector"], v_st)]
            if c == 0:
                wn.append((E.ms["vector"], v_hoist))
            if "O" in g2:
                wn.append(g2["O"])
            mnum = E.op("tensor",
                        (lambda e, c=c, s=(c == 0):
                         e.matmul(PS[4][:, 0:128], vTr[:, c, :], stT,
                                  start=True, stop=s, skip_group_check=True)),
                        waits=wn, tick=(c == 0))
            if c > 0:
                mnum = E.op("tensor",
                            (lambda e, c=c: e.matmul(PS[4][:, 0:128], Rb16,
                                                     qTs_all[:, c, :],
                                                     start=False, stop=True,
                                                     skip_group_check=True)),
                            waits=[(E.ms["vector"], v_state)], tick=True)
            wd = []
            if "R" in g2:
                wd.append(g2["R"])
            if v_state is not None:
                wd.append((E.ms["vector"], v_state))
            mdr = E.op("tensor",
                       (lambda e, c=c: e.matmul(PS[6][:, 0:128], ksc_all[:, c, :],
                                                vTr[:, c, :],
                                                start=True, stop=True,
                                                skip_group_check=True)),
                       waits=wd, tick=True)
            if c == 0:
                vsa = E.op("vector", lambda e: e.tensor_copy(Rf32, PS[6][:, 0:128]),
                           waits=[(E.ms["tensor"], mdr)], tick=True)
            else:
                vsa = E.op("vector",
                           (lambda e, g=gamC[h]: e.scalar_tensor_tensor(
                               Rf32, Rf32, g, PS[6][:, 0:128], ALU.mult, ALU.add)),
                           waits=[(E.ms["tensor"], mdr)], tick=True)
            v_state = E.op("vector", lambda e: e.tensor_copy(Rb16, Rf32),
                           tick=True)
            g2["R"] = (E.ms["vector"], v_state)
            v_oe = E.op("scalar",
                        (lambda e, c=c, h=h: e.copy(OTF[:, h, c * 128:(c + 1) * 128],
                                                    PS[4][:, 0:128])),
                        waits=[(E.ms["tensor"], mnum)], tick=True)
            g2["O"] = (E.ms["scalar"], v_oe)
        ps_guard[2] = g2["S"]
        ps_guard[4] = g2["O"]
        ps_guard[6] = g2["R"]
        scan_done = E.op("tensor", lambda e: e.nop(),
                         waits=[(E.ms["scalar"], E.ms["scalar"].n)], tick=True)
        prev_scan[par] = (E.ms["tensor"], scan_done)
        prev_bq, prev_bk, prev_hoist = ev_bq, ev_bk, v_hoist
        dec_guard = (E.ms["vector"], E.ms["vector"].n)

    adT = AD[:].rearrange("p (tt d) -> p tt d", d=1024)
    for h in range(8):
        E.dma_tr("sync", adT[:, :, h * 128:(h + 1) * 128], OTF[:, h, :],
                 waits=[(E.ms["tensor"], scan_done),
                        (E.ms["scalar"], E.ms["scalar"].n),
                        (E.ms["vector"], E.ms["vector"].n)] if h == 0 else [])
    ot_moved = E.grp_waits("tr")

    emit_proj_accum(meta, dram["bWg"][l], NF, None, ot_moved, ps_guard,
                    n_in=ND, accumulate=False, outF=OTF, act=AF.Silu)
    v_gate = (E.ms["scalar"], E.ms["scalar"].n)

    dump = BIG[:, 0:1024]
    v_ap = None
    for tt in range(NT):
        w0 = list(ot_moved) + [v_gate] if tt == 0 else []
        a3 = adT[:, tt, :].rearrange("p (s c) -> p s c", c=128)
        v_sq = E.op("scalar", (lambda e, tt=tt: e.activation(
            dump, adT[:, tt, :], AF.Square)), waits=w0, tick=True)
        E.op("vector", (lambda e, a3=a3: e.reduce_sum(STAT[:, 32:40], a3,
                                                      axis=AX.X)),
             waits=w0 if tt == 0 else [])
        E.op("vector", (lambda e: e.reduce_sum(
            SMALL[:, 144:152], dump.rearrange("p (s c) -> p s c", c=128),
            axis=AX.X)), waits=[(E.ms["scalar"], v_sq)])
        E.op("vector", lambda e: e.tensor_scalar(STAT[:, 32:40], STAT[:, 32:40],
                                                 -1.0 / 128.0, None, ALU.mult))
        E.op("vector", lambda e: e.tensor_mul(SMALL[:, 152:160], STAT[:, 32:40],
                                              STAT[:, 32:40]))
        E.op("vector", lambda e: e.scalar_tensor_tensor(
            SMALL[:, 144:152], SMALL[:, 144:152], 1.0 / 128.0, SMALL[:, 152:160],
            ALU.mult, ALU.subtract))
        v_v = E.op("vector", lambda e: e.tensor_scalar(
            SMALL[:, 144:152], SMALL[:, 144:152], 1e-6, None, ALU.add), tick=True)
        E.op("scalar", lambda e: e.activation(SMALL[:, 152:160], SMALL[:, 144:152],
                                              AF.Ln), waits=[(E.ms["vector"], v_v)])
        v_rs = E.op("scalar", lambda e: e.activation(STAT[:, 40:48],
                                                     SMALL[:, 152:160], AF.Exp,
                                                     scale=-0.5), tick=True)
        nm_bc = STAT[:, 32:40].unsqueeze(2).broadcast_to([128, 8, 128])
        rs_bc = STAT[:, 40:48].unsqueeze(2).broadcast_to([128, 8, 128])
        E.op("vector", (lambda e, a3=a3, nm_bc=nm_bc: e.tensor_tensor(
            a3, a3, nm_bc, ALU.add)), waits=[(E.ms["scalar"], v_rs)])
        v_ap = E.op("vector", (lambda e, a3=a3, rs_bc=rs_bc: e.tensor_tensor(
            a3, a3, rs_bc, ALU.mult)), tick=True)
    for tt in range(NT):
        E.dma_tr("sync", NF[:, :, tt * 128:(tt + 1) * 128], adT[:, tt, :],
                 waits=[(E.ms["vector"], v_ap), v_gate] if tt == 0 else [])
    y_rdy = E.grp_waits("tr")
    v_gg = None
    for dt in range(ND):
        E.op("vector",
             (lambda e, dt=dt: e.tensor_scalar(NF[:, dt, :], NF[:, dt, :],
                                               SC32[:, 208 + dt:209 + dt],
                                               SC32[:, 216 + dt:217 + dt],
                                               ALU.mult, ALU.add)),
             waits=list(y_rdy) if dt == 0 else [])
        v_gg = E.op("vector",
                    (lambda e, dt=dt: e.tensor_mul(OTF[:, dt, :], NF[:, dt, :],
                                                   OTF[:, dt, :])), tick=True)
    emit_proj_accum(meta, dram["bWo"][l], OTF, XI, [(E.ms["vector"], v_gg)],
                    ps_guard, n_in=ND)
    v_res1 = (E.ms["vector"], E.ms["vector"].n)
    v_wo_mm = (E.ms["tensor"], E.ms["tensor"].n)
    emit_dbg(meta, f"bl{l}_res1", lambda: XI[:], (128, ND, T), F32, [v_res1])

    rdy2 = emit_norm_F(meta, XI, NF, eps=1e-6, gamma=SC32[:, rb + 32:rb + 40],
                       beta=SC32[:, rb + 40:rb + 48], waits=[v_res1, v_wo_mm])

    emit_dbg(meta, f"bl{l}_xn2", lambda: NF[:], (128, ND, T), BF16, rdy2)
    emit_mlp_plain(meta, dram["bw1"][l], dram["bw2"][l], 4096, AF.Gelu_apprx_tanh,
                   rdy2, l)
    emit_dbg(meta, f"bl{l}_hst", lambda: meta["AD"][:], (128, 16384), BF16,
             [(E.ms["scalar"], E.ms["scalar"].n)])
    emit_dbg(meta, f"bl{l}_res2", lambda: XI[:], (128, ND, T), F32,
             [(E.ms["vector"], E.ms["vector"].n)])


def emit_mlp_plain(meta, w1d, w2d, NH, act, rdy, l):
    nc, E = meta["nc"], meta["E"]
    XI, NF, AD, PS, PSALL = (meta["XI"], meta["NF"], meta["AD"], meta["PS"],
                             meta["PSALL"])
    ps_guard = meta["ps_guard"]
    nch = NH // 128
    hst = AD[:, 0:nch * 512].rearrange("p (i t) -> p i t", t=512)
    rdy_tb = meta["_norm_rdy_tb"]
    wload = {}
    BIGW = meta["BIG"][:, 0:8192].rearrange("p (s b) -> p s b", s=4)
    ws = WStream(meta, key="bigw", buf=BIGW, nslot=4, grp="bigw",
                 init_extra=list(E._flat(meta["_norm_cast_evs"])))
    wc, wce = meta["wcache"], meta["wcache_ev"]

    def load_w1(i):
        sl, view, ev = ws.load_cached(w1d[i], wc["bw1"][l][i],
                                      ("bw1", l, i), wce)
        wload[i] = (view, ev, sl)

    def load_w2(i):
        sl, view, ev = ws.load_cached(w2d[i * 128:(i + 1) * 128, :],
                                      wc["bw2"][l][i], ("bw2", l, i), wce)
        wload[("w2", i)] = (view, ev, sl)

    for tb in range(4):
        t0 = tb * 512
        for _j in range(4):
            load_w1(_j)
        for i in range(nch):
            buf, ev, sl = wload[i // 2]
            halfm = (i % 2) * 128
            bi = i % 2
            last = None
            for kk in range(ND):
                w = []
                if kk == 0:
                    w = [ev]
                    if i == 0:
                        w += list(E._flat(rdy_tb[tb]))
                    if bi in ps_guard:
                        w.append(ps_guard[bi])
                last = E.op("tensor",
                            (lambda e, b=buf, kk=kk, bi=bi, f=(kk == 0),
                             s=(kk == ND - 1), t0=t0, h0=halfm:
                             e.matmul(PSALL[:, bi * 512:bi * 512 + 512],
                                      b[:, kk, h0:h0 + 128],
                                      NF[:, kk, t0:t0 + 512],
                                      start=f, stop=s, skip_group_check=True)),
                            waits=w, tick=(kk == ND - 1))
            if i % 2 == 1:
                ws.done(sl, (E.ms["tensor"], last))
                if i // 2 + 4 < nch // 2:
                    load_w1(i // 2 + 4)
            v_a = E.op("scalar",
                       (lambda e, bi=bi, i=i: e.activation(
                           hst[:, i, :], PSALL[:, bi * 512:bi * 512 + 512], act)),
                       waits=[(E.ms["tensor"], last)], tick=True)
            ps_guard[bi] = (E.ms["scalar"], v_a)
        v_all = (E.ms["scalar"], E.ms["scalar"].n)
        for _j in range(4):
            load_w2(_j)
        vfin = None
        for i in range(nch):
            bview, ev, sl = wload[("w2", i)]
            for m in range(ND):
                w = []
                if m == 0:
                    w = [ev]
                    if i == 0:
                        w.append(v_all)
                if i == 0 and (4 + m % 4) in ps_guard and m < 4:
                    pass
                if i == 0 and m in ps_guard:
                    w.append(ps_guard[m])
                vfin = E.op("tensor",
                            (lambda e, bv=bview, m=m, f=(i == 0),
                             s=(i == nch - 1), hv=hst[:, i, :]:
                             e.matmul(PSALL[:, m * 512:m * 512 + 512],
                                      bv[:, m * 128:(m + 1) * 128],
                                      hv, start=f, stop=s,
                                      skip_group_check=True)),
                            waits=w, tick=(m == ND - 1))
            ws.done(sl, (E.ms["tensor"], vfin))
            if i + 4 < nch:
                load_w2(i + 4)
        for m in range(ND):
            v_e = E.op("vector",
                       (lambda e, m=m, t0=t0:
                        e.tensor_add(XI[:, m, t0:t0 + 512],
                                     XI[:, m, t0:t0 + 512],
                                     PSALL[:, m * 512:m * 512 + 512])),
                       waits=[(E.ms["tensor"], vfin)], tick=True)
            ps_guard[m] = (E.ms["vector"], v_e)


_BUILD_CACHE = {}
_DBG = ()


def _chunkmaj(W):
    K, M = W.shape
    return np.ascontiguousarray(
        W.reshape(K // 128, 128, M // 128, 128).transpose(2, 1, 0, 3))


def _pairmaj(W):
    K, M = W.shape
    a = W.reshape(K // 128, 128, M // 256, 256).transpose(2, 1, 0, 3)
    return np.ascontiguousarray(a)


def _layout_inputs(inputs):
    f = np.float32
    o = {}
    o["aWq"] = _chunkmaj(inputs["L0_attn_a_Wq"].T.astype(f))
    o["aWk"] = _chunkmaj(inputs["L0_attn_a_Wk"].T.astype(f))
    o["aWv"] = _chunkmaj(inputs["L0_attn_a_Wv"].T.astype(f))
    o["aWo"] = _pairmaj(inputs["L0_attn_a_Wo"].T.astype(f))
    o["fwqkv"] = np.stack([_pairmaj(inputs["L0_ffn_wqkv"][l]) for l in range(2)])
    o["fwo"] = np.stack([_pairmaj(inputs["L0_ffn_wo"][l]) for l in range(2)])
    w1 = inputs["L0_ffn_w1"]
    w1p = np.zeros((2, 22, 128, 8, 256), f)
    for l in range(2):
        for i in range(22):
            for ag in range(2):
                c0 = i * 128 + ag * 2730
                cw = min(128, 2730 - i * 128)
                blk = w1[l][:, c0:c0 + cw]
                w1p[l, i, :, :, ag * 128:ag * 128 + cw] = \
                    blk.reshape(8, 128, cw).transpose(1, 0, 2)
    o["fw1"] = w1p
    w2 = inputs["L0_ffn_w2"]
    w2p = np.zeros((2, 22, 128, 1024), f)
    for l in range(2):
        for i in range(22):
            rw = min(128, 2730 - i * 128)
            w2p[l, i, 0:rw] = w2[l][i * 128:i * 128 + rw]
    o["fw2"] = w2p
    o["fqs"] = np.ascontiguousarray(inputs["L0_ffn_q_scale"])
    o["fks"] = np.ascontiguousarray(inputs["L0_ffn_k_scale"])

    def vec_F(v):
        sh = v.shape[:-1]
        return np.ascontiguousarray(v.reshape(sh + (ND, 128)).swapaxes(-1, -2))
    o["flng"] = vec_F(inputs["L0_ffn_ln_g"])
    o["flnb"] = vec_F(inputs["L0_ffn_ln_b"])
    b1 = inputs["L0_ffn_b1"]
    b1p = np.zeros((2, 128, 44), f)
    for i in range(22):
        aw = min(128, 2730 - i * 128)
        b1p[:, 0:aw, i] = b1[:, i * 128:i * 128 + aw]
        b1p[:, 0:aw, 22 + i] = b1[:, 2730 + i * 128:2730 + i * 128 + aw]
    o["fb1"] = np.ascontiguousarray(b1p)
    o["fb2"] = vec_F(inputs["L0_ffn_b2"])
    def headmaj(Wh):
        return np.ascontiguousarray(
            Wh.reshape(2, 8, 8, 128, 128).transpose(0, 1, 3, 2, 4))
    o["bWq"] = headmaj(inputs["L0_attn_b_Wq"])
    o["bWk"] = headmaj(inputs["L0_attn_b_Wk"])
    o["bWv"] = headmaj(inputs["L0_attn_b_Wv"])
    o["bWg"] = np.stack([_pairmaj(inputs["L0_attn_b_Wg"][l]) for l in range(2)])
    o["bWo"] = np.stack([_pairmaj(inputs["L0_attn_b_Wo"][l]) for l in range(2)])
    o["bw1"] = np.stack([_pairmaj(inputs["L0_attn_b_ffn_w1"][l]) for l in range(2)])
    o["bw2"] = np.ascontiguousarray(inputs["L0_attn_b_ffn_w2"])
    o["bgns"] = vec_F(inputs["L0_attn_b_gn_scale"])
    o["bgnb"] = vec_F(inputs["L0_attn_b_gn_bias"])
    o["bln1s"] = vec_F(inputs["L0_attn_b_ln1_scale"])
    o["bln1b"] = vec_F(inputs["L0_attn_b_ln1_bias"])
    o["bln2s"] = vec_F(inputs["L0_attn_b_ln2_scale"])
    o["bln2b"] = vec_F(inputs["L0_attn_b_ln2_bias"])
    return o


def kernel(**inputs):
    key = (_PARTIAL, tuple(_DBG))
    if key not in _BUILD_CACHE:
        _BUILD_CACHE[key] = build_program(dbg=tuple(_DBG))
    nc, meta = _BUILD_CACHE[key]
    tabs = meta["tabs"]
    shared = _layout_inputs(inputs)
    for k, v in tabs.items():
        if k == "gammaC":
            continue
        shared[k] = np.asarray(v)
    x_full = np.asarray(inputs["L0_attn_a_x"])
    in_maps = []
    for c in range(8):
        m = dict(shared)
        m["x"] = np.ascontiguousarray(x_full[c])
        in_maps.append(m)
    res = run_bass_kernel_spmd(nc, in_maps, core_ids=list(range(8)))
    out = np.stack([np.asarray(r["y"]) for r in res.results])
    if _DBG:
        kernel._dbg_results = [{k: np.asarray(r["dbg_" + k]) for k in _DBG}
                               for r in res.results]
    return out


# version 17
# speedup vs baseline: 1.1691x; 1.1691x over previous
import math
import numpy as np
import ml_dtypes
import concourse.bass as bass
import concourse.mybir as mybir
from concourse.bass_utils import run_bass_kernel_spmd

F32 = mybir.dt.float32
BF16 = mybir.dt.bfloat16
AF = mybir.ActivationFunctionType
ALU = mybir.AluOpType
AX = mybir.AxisListType

T = 2048
D = 1024
NT = T // 128
ND = D // 128
NEG = -3.4028235e38

import os as _os
_PARTIAL = int(_os.environ.get("KPARTIAL", "4"))

ENGINES = ("tensor", "vector", "scalar", "gpsimd", "sync")


def _to_T(tab):
    Tn, C = tab.shape
    return np.ascontiguousarray(tab.reshape(NT, 128, C).transpose(1, 0, 2))


def make_tables():
    tabs = {}
    t = np.arange(T, dtype=np.float64)

    invf = 10000.0 ** (-np.arange(32, dtype=np.float64) / 32.0)
    ang = t[:, None] * invf[None, :]
    c, s = np.cos(ang), np.sin(ang)
    cosA = np.concatenate([c, c], axis=1)
    sinA = np.concatenate([-s, s], axis=1)
    tabs["ropeA_cos"] = _to_T(cosA).astype(ml_dtypes.bfloat16)
    tabs["ropeA_sin"] = _to_T(sinA).astype(ml_dtypes.bfloat16)

    half_idx = np.arange(0, 64, 2, dtype=np.float64)
    invf2 = 1.0 / 10000.0 ** (half_idx / 64.0)
    fr = t[:, None] * invf2[None, :]
    base_v = (half_idx + 0.4 * 64) / (1.4 * 64)
    power = (t - T // 2) / 256.0
    scl = base_v[None, :] ** power[:, None]
    cosf, sinf = np.cos(fr), np.sin(fr)
    qc = np.concatenate([cosf * scl, cosf * scl], 1)
    qs = np.concatenate([-sinf * scl, sinf * scl], 1)
    kc = np.concatenate([cosf / scl, cosf / scl], 1)
    ks = np.concatenate([-sinf / scl, sinf / scl], 1)
    tabs["rotF_qcos"] = _to_T(qc).astype(np.float32)
    tabs["rotF_qsin"] = _to_T(qs).astype(np.float32)
    tabs["rotF_kcos"] = _to_T(kc).astype(np.float32)
    tabs["rotF_ksin"] = _to_T(ks).astype(np.float32)

    base_r = (np.arange(0, 128, 2, dtype=np.float64) + 0.4 * 128) / (1.4 * 128)
    sclr = base_r[None, :] ** (t[:, None] / 512.0)
    invf3 = 1.0 / 10000.0 ** (np.arange(64, dtype=np.float64) / 64.0)
    sin3 = np.sin(t[:, None] * invf3[None, :])
    cos3 = np.cos(t[:, None] * invf3[None, :])

    def inter2(a, b):
        out = np.empty((T, 128), np.float64)
        out[:, 0::2] = a
        out[:, 1::2] = b
        return out
    c2u = cos3 * sclr
    s2u = sin3 * sclr
    c2d = cos3 / sclr
    s2d = sin3 / sclr
    tabs["xpos_qcos"] = _to_T(inter2(c2u, c2u)).astype(ml_dtypes.bfloat16)
    tabs["xpos_qsin"] = _to_T(inter2(-s2u, s2u)).astype(ml_dtypes.bfloat16)
    tabs["xpos_kcos"] = _to_T(inter2(c2d, c2d)).astype(ml_dtypes.bfloat16)
    tabs["xpos_ksin"] = _to_T(inter2(-s2d, s2d)).astype(ml_dtypes.bfloat16)

    gam = 1.0 - 2.0 ** (-5.0 - np.arange(8, dtype=np.float64))
    ij = np.arange(128)
    diff = ij[None, :] - ij[:, None]
    dt_tiles = []
    cross_tiles = []
    kscale_cols = []
    for h in range(8):
        dmat = np.where(diff >= 0, gam[h] ** np.maximum(diff, 0), 0.0)
        dt_tiles.append(dmat)
        cross_tiles.append(np.broadcast_to(gam[h] ** (ij + 1.0), (128, 128)).copy())
        kscale_cols.append(gam[h] ** (127.0 - ij))
    tabs["decayT"] = np.stack(dt_tiles).astype(ml_dtypes.bfloat16)
    tabs["decay_cross"] = np.stack(cross_tiles).astype(ml_dtypes.bfloat16)
    tabs["decay_ks"] = np.stack(kscale_cols, 1).astype(np.float32)
    tabs["gammaC"] = (gam ** 128.0).astype(np.float64)

    m01 = (ij[None, :] >= ij[:, None]).astype(np.float32)
    tabs["mask01T"] = m01.astype(ml_dtypes.bfloat16)

    triU = np.where(ij[None, :] > ij[:, None], NEG, 0.0).astype(np.float32)
    triL = np.where(ij[None, :] < ij[:, None], NEG, 0.0).astype(np.float32)
    tabs["maskU"] = triU
    tabs["maskL"] = triL

    tabs["identity"] = np.eye(128, dtype=np.float32)
    tabs["consts"] = np.full((128, 2), -0.5, np.float32)
    return tabs


class Sem:
    __slots__ = ("h", "n")

    def __init__(self, h):
        self.h = h
        self.n = 0


class Emitter:
    def __init__(self, nc, stack):
        self.nc = nc
        self.stack = stack
        self.q = {e: [] for e in ENGINES}
        self.ms = {e: Sem(stack.enter_context(nc.semaphore(f"ms_{e}"))) for e in ENGINES}
        self.ms["tensorB"] = Sem(stack.enter_context(nc.semaphore("ms_tensorB")))
        self._dsems = {}

    def dsem(self, name):
        if name not in self._dsems:
            self._dsems[name] = Sem(self.stack.enter_context(self.nc.semaphore(f"d_{name}")))
        return self._dsems[name]

    def group(self, name, n=4):
        key = ("grp", name)
        if key not in self._dsems:
            self._dsems[key] = [Sem(self.stack.enter_context(
                self.nc.semaphore(f"d_{name}{i}"))) for i in range(n)]
            self._grp_idx = getattr(self, "_grp_idx", {})
            self._grp_idx[name] = 0
        return self._dsems[key]

    SELF_CHAIN = ("vector", "scalar", "gpsimd")

    @staticmethod
    def _flat(waits):
        out = []
        for w in waits:
            if isinstance(w, list):
                out.extend(w)
            elif w is not None:
                out.append(w)
        return out

    def snap_all(self, engs=("tensor", "tensorB", "vector", "scalar", "gpsimd")):
        return [(self.ms[e], self.ms[e].n) for e in engs if self.ms[e].n > 0]

    def snap_w(self):
        return self.snap_all(("tensor", "tensorB"))

    def grp_waits(self, name):
        key = ("grp", name)
        if key not in self._dsems:
            return []
        return [(s, s.n) for s in self._dsems[key] if s.n > 0]

    def op(self, eng, fn, waits=(), tick=False, tick_sem=None):
        ms = self.ms[tick_sem] if tick_sem else self.ms[eng]
        chain = eng in self.SELF_CHAIN
        wl = [(s.h, v) for (s, v) in self._flat(waits)]
        if chain:
            if ms.n > 0:
                wl.append((ms.h, ms.n))
            tick = True
        elif tick and eng == "tensor" and ms.n > 0:
            wl.append((ms.h, ms.n))
        val = None
        if tick:
            ms.n += 1
            val = ms.n

        def closure(e, fn=fn, wl=wl, tick=tick, msh=ms.h):
            for h, v in wl:
                e.wait_ge(h, v)
            ins = fn(e)
            if tick:
                if ins is None:
                    ins = e.nop()
            if tick and ins is not None:
                ins.then_inc(msh, 1)
        self.q[eng].append(closure)
        return val

    def _dma_common(self, eng, fn, grp, waits, ngrp):
        sems = self.group(grp, ngrp)
        i = self._grp_idx[grp]
        self._grp_idx[grp] = (i + 1) % len(sems)
        sem = sems[i]
        prev = sem.n
        sem.n += 16
        tgt = sem.n
        wl = [(s.h, v) for (s, v) in self._flat(waits)]

        def closure(e, fn=fn, wl=wl, semh=sem.h, prev=prev):
            for h, v in wl:
                e.wait_ge(h, v)
            if prev:
                e.wait_ge(semh, prev)
            fn(e).then_inc(semh, 16)
        self.q[eng].append(closure)
        return (sem, tgt)

    def dma(self, eng, out, in_, grp="io", waits=(), ngrp=4, **kw):
        return self._dma_common(
            eng, (lambda e, out=out, in_=in_, kw=kw:
                  e.dma_start(out=out, in_=in_, **kw)), grp, waits, ngrp)

    def dma_tr(self, eng, out, in_, grp="tr", waits=(), ngrp=4):
        return self._dma_common(
            eng, (lambda e, out=out, in_=in_: e.dma_start_transpose(out, in_)),
            grp, waits, ngrp)

    def wait(self, eng, waits):
        wl = [(s.h, v) for (s, v) in waits]

        def closure(e, wl=wl):
            for h, v in wl:
                e.wait_ge(h, v)
        self.q[eng].append(closure)

    def barrier(self, engines=ENGINES):
        vals = []
        for e in engines:
            vals.append((e, self.op(e, lambda x: x.nop(), tick=True)))
        for e in engines:
            self.wait(e, [(self.ms[o], v) for (o, v) in vals if o != e])

    def emit(self):
        nc = self.nc
        with nc.Block() as block:
            for name in ENGINES:
                deco = getattr(block, name)
                qlist = self.q[name]

                def body(eng, qlist=qlist):
                    for f in qlist:
                        f(eng)
                deco(body)


def build_program(dbg=()):
    from contextlib import ExitStack
    nc = bass.Bass("TRN2", target_bir_lowering=False)
    tabs = make_tables()
    st = ExitStack()

    dram = {}

    def din(name, shape, dtype=F32):
        dram[name] = nc.dram_tensor(name, list(shape), dtype, kind="ExternalInput").ap()
        return dram[name]

    x_in = din("x", (T, D))
    for w in ("aWq", "aWk", "aWv"):
        din(w, (8, 128, 8, 128))
    din("aWo", (4, 128, 8, 256))
    din("fwqkv", (2, 6, 128, 8, 256))
    din("fwo", (2, 4, 128, 4, 256))
    din("fw1", (2, 22, 128, 8, 256))
    din("fw2", (2, 22, 128, D))
    
    din("flng", (2, 128, ND))
    din("flnb", (2, 128, ND))
    din("fb1", (2, 128, 44))
    din("fb2", (2, 128, ND))
    din("fqs", (2, 64))
    din("fks", (2, 64))
    din("bWq", (2, 8, 128, 8, 128))
    din("bWk", (2, 8, 128, 8, 128))
    din("bWv", (2, 8, 128, 8, 128))
    din("bWg", (2, 4, 128, 8, 256))
    din("bWo", (2, 4, 128, 8, 256))
    din("bw1", (2, 16, 128, 8, 256))
    din("bw2", (2, 4096, D))
    for nm in ("bgns", "bgnb", "bln1s", "bln1b", "bln2s", "bln2b"):
        din(nm, (2, 128, ND))
    din("ropeA_cos", (128, NT, 64), BF16)
    din("ropeA_sin", (128, NT, 64), BF16)
    for nm in ("rotF_qcos", "rotF_qsin", "rotF_kcos", "rotF_ksin"):
        din(nm, (128, NT, 64))
    for nm in ("xpos_qcos", "xpos_qsin", "xpos_kcos", "xpos_ksin"):
        din(nm, (128, NT, 128), BF16)
    din("decayT", (8, 128, 128), BF16)
    din("decay_cross", (8, 128, 128), BF16)
    din("decay_ks", (128, 8))
    din("mask01T", (128, 128), BF16)
    din("maskU", (128, 128))
    din("maskL", (128, 128))
    din("identity", (128, 128))
    din("consts", (128, 2))

    y_out = nc.dram_tensor("y", [T, D], F32, kind="ExternalOutput").ap()
    dbg_out = {}
    for name in dbg:
        pass

    xo_spill = nc.dram_tensor("xo_spill", [128, ND, T], F32, kind="Internal").ap()
    wcache = {
        "fw1": nc.dram_tensor("fw1c", [2, 22, 128, 8, 256], BF16,
                              kind="Internal").ap(),
        "fw2": nc.dram_tensor("fw2c", [2, 22, 128, 1024], BF16,
                              kind="Internal").ap(),
        "bw1": nc.dram_tensor("bw1c", [2, 16, 128, 8, 256], BF16,
                              kind="Internal").ap(),
        "bw2": nc.dram_tensor("bw2c", [2, 32, 128, 1024], BF16,
                              kind="Internal").ap(),
    }

    sb = lambda name, shape, dt: st.enter_context(nc.sbuf_tensor(name, list(shape), dt))
    ps = lambda name, shape: st.enter_context(nc.psum_tensor(name, list(shape), F32))

    XI = sb("XI", [128, ND, T], F32)
    BIG = sb("BIG", [128, 24576], BF16)
    NFR = sb("NFR", [128, ND, T + 16], BF16)
    AD = sb("AD", [128, 16384], BF16)
    WBUF = sb("WBUF", [128, 2, 8, 256], BF16)
    TAB = sb("TAB", [128, NT, 512], BF16)
    SC32 = sb("SC32", [128, 832], F32)
    STAT = sb("STAT", [128, 64], F32)
    SMALL = sb("SMALL", [128, 256], F32)
    MISC = sb("MISC", [128, 384], F32)
    MISCB = sb("MISCB", [128, 768], BF16)
    ONESB = sb("ONESB", [128, 8], BF16)

    PSALL = ps("PSALL", [128, 4096])
    PS = [PSALL[:, i * 512:(i + 1) * 512] for i in range(8)]
    NF = NFR[:, :, 16:T + 16]

    E = Emitter(nc, st)

    nc.const_aps.register(F32, -0.5, SMALL[:, 192:193])

    meta = dict(nc=nc, tabs=tabs, E=E, dram=dram, y=y_out, xo_spill=xo_spill,
                wcache=wcache, wcache_ev={},
                XI=XI, BIG=BIG, NF=NF, NFR=NFR, AD=AD, SC32=SC32, STAT=STAT,
                SMALL=SMALL, WBUF=WBUF, TAB=TAB, MISC=MISC,
                MISCB=MISCB, ONESB=ONESB, PS=PS, PSALL=PSALL, ps_guard={},
                dbg=dbg, dbg_out=dbg_out, stack=st)

    emit_forward(meta)
    E.emit()
    st.close()
    return nc, meta


class WStream:

    def __init__(self, meta, key="w", buf=None, nslot=2, grp="w",
                 init_extra=()):
        self.E = meta["E"]
        self.buf = meta["WBUF"] if buf is None else buf
        self.nslot = nslot
        self.grp = grp
        g = meta.setdefault("_wsg_" + key, [None] * nslot)
        self.guard = g
        self.idx = meta.setdefault("_wsi_" + key, [0])
        snap = self.E.snap_w()
        for i in range(nslot):
            if g[i] is None:
                g[i] = [snap]
        self.first_extra = list(self.E._flat(init_extra))
        self.nfirst = 0
        self.slot_wb = {}

    def load(self, src, extra=()):
        i = self.idx[0] % self.nslot
        self.idx[0] += 1
        w = list(self.E._flat(self.guard[i] or [])) + list(extra)
        if self.nfirst < self.nslot:
            w += self.first_extra
            self.nfirst += 1
        n = 1
        for s in src.shape[1:]:
            n *= s
        sl = self.buf[:, i]
        if len(sl.shape) > 2:
            sl = sl.rearrange("p a b -> p (a b)")
        dst = sl[:, 0:n]
        ev = self.E.dma("gpsimd", dst, src, grp=self.grp, waits=w, ngrp=4)
        self.guard[i] = None
        if src.ndim == 3:
            view = dst.rearrange("p (a b) -> p a b", b=src.shape[2])
        else:
            view = dst
        return i, view, ev

    def load_cached(self, src, cache_ap, ckey, cev_map):
        ev_wb = cev_map.get(ckey)
        if ev_wb is not None:
            i, view, ev = self.load(cache_ap, extra=[ev_wb])
        else:
            i, view, ev = self.load(src)
            wb = self.E.dma("sync", cache_ap, view, grp="wb", waits=[ev],
                            ngrp=4)
            cev_map[ckey] = wb
            self.slot_wb[i] = wb
        return i, view, ev

    def done(self, i, ev):
        g = [ev]
        wb = self.slot_wb.pop(i, None)
        if wb is not None:
            g.append(wb)
        self.guard[i] = g

def emit_dbg(meta, name, ap_fn, shape, dtype, waits, queue="sync"):
    if name not in meta["dbg"]:
        return
    nc, E = meta["nc"], meta["E"]
    t = nc.dram_tensor("dbg_" + name, list(shape), dtype, kind="ExternalOutput").ap()
    meta["dbg_out"][name] = t
    ev = E.dma(queue, t, ap_fn(), waits=waits)
    for eng in ENGINES:
        E.wait(eng, [ev])


def emit_forward(meta):
    E = meta["E"]
    emit_load_initial(meta)
    emit_input_and_norm1(meta)
    if _PARTIAL >= 1:
        emit_attn_a(meta)
    if _PARTIAL >= 2:
        emit_ffn_block(meta, app=0)
    if _PARTIAL >= 3:
        emit_attn_b(meta)
    if _PARTIAL >= 4:
        emit_ffn_block(meta, app=1)
    emit_store_output(meta)


def emit_load_initial(meta):
    nc, E = meta["nc"], meta["E"]
    dram = meta["dram"]
    meta["ev_ident"] = E.dma("sync", meta["MISC"][:, 0:128], dram["identity"])
    meta["ev_maskU"] = E.dma("sync", meta["MISC"][:, 128:256], dram["maskU"])
    meta["ev_maskL"] = E.dma("sync", meta["MISC"][:, 256:384], dram["maskL"])
    meta["ev_dks"] = E.dma("sync", meta["SMALL"][:, 184:192], dram["decay_ks"])
    ev_c = E.dma("sync", meta["SMALL"][:, 192:194], dram["consts"])
    meta["ev_consts"] = ev_c
    E.op("vector", lambda e: e.memset(meta["ONESB"][:], 1.0),
         waits=[ev_c], tick=True)
    E.op("vector", lambda e: e.memset(meta["SMALL"][:, 208:209], 0.0))
    E.op("vector", lambda e: e.memset(meta["NFR"][:, :, 0:16], 0.0))
    E.op("vector", lambda e: e.memset(meta["SMALL"][0:1, 0:128], 1.0))
    v_z = E.op("vector", lambda e: e.memset(meta["SMALL"][:, 209:210], 1.0), tick=True)
    meta["v_zo"] = (E.ms["vector"], v_z)


def _tile_stats(meta, xtile, col, eps, dump, waits=(), dim=float(D)):
    E = meta["E"]
    STAT, SMALL = meta["STAT"], meta["SMALL"]
    prev_vv = meta.get("_ts_vv")
    v_r = E.op("vector", (lambda e: e.reduce_sum(SMALL[:, 200:201], xtile, axis=AX.X)),
               waits=list(waits), tick=True)
    v_s = E.op("scalar", (lambda e: e.activation(dump, xtile, AF.Square,
                                                 accum_out=SMALL[:, 201:202])),
               waits=list(waits) + ([(E.ms["vector"], prev_vv)] if prev_vv else []),
               tick=True)
    E.op("vector", (lambda e: e.tensor_scalar(STAT[:, col:col + 1], SMALL[:, 200:201],
                                              -1.0 / dim, None, ALU.mult)),
         waits=[(E.ms["scalar"], v_s), (E.ms["vector"], v_r)])
    E.op("vector", (lambda e: e.tensor_mul(SMALL[:, 202:203], STAT[:, col:col + 1],
                                           STAT[:, col:col + 1])))
    E.op("vector", (lambda e: e.scalar_tensor_tensor(SMALL[:, 203:204], SMALL[:, 201:202],
                                                     1.0 / dim, SMALL[:, 202:203],
                                                     ALU.mult, ALU.subtract)))
    v_v = E.op("vector", (lambda e: e.tensor_scalar(SMALL[:, 203:204], SMALL[:, 203:204],
                                                    eps, None, ALU.add)), tick=True)
    meta["_ts_vv"] = v_v
    E.op("scalar", (lambda e: e.activation(SMALL[:, 204:205], SMALL[:, 203:204], AF.Ln)),
         waits=[(E.ms["vector"], v_v)])
    v_rs = E.op("scalar", (lambda e: e.activation(STAT[:, col + 16:col + 17],
                                                  SMALL[:, 204:205], AF.Exp, scale=-0.5)),
                tick=True)
    return v_rs


def emit_input_and_norm1(meta):
    nc, E = meta["nc"], meta["E"]
    dram = meta["dram"]
    BIG, NF, XI, PS = meta["BIG"], meta["NF"], meta["XI"], meta["PS"]
    ident = meta["MISC"][:, 0:128]
    xf32 = BIG[:].bitcast(F32)
    dump = BIG[:, 8192:9216]
    xs = dram["x"]
    slot_free = [None] * 4
    v_tr_last = None
    guard = {}
    for tt in range(NT):
        sl = tt % 4
        xslot = xf32[:, sl * 1024:(sl + 1) * 1024]
        w = [slot_free[sl]] if slot_free[sl] else []
        if meta.get("_in_slot_scalar", {}).get(sl) is not None:
            w.append((E.ms["scalar"], meta["_in_slot_scalar"][sl]))
        ev_ld = E.dma("sync", xslot, xs[tt * 128:(tt + 1) * 128, :], waits=w)
        sc = tt % 8
        v_rs = _tile_stats(meta, xslot, sc, 1e-5, dump, waits=[ev_ld])
        asl = BIG[:, 9216 + (tt % 2) * 1024: 10240 + (tt % 2) * 1024]
        aw = [(E.ms["scalar"], v_rs)]
        if guard.get("a" + str(tt % 2)):
            aw.append(guard["a" + str(tt % 2)])
        v_ap = E.op("vector",
                    (lambda e, xslot=xslot, asl=asl, sc=sc: e.tensor_scalar(
                        asl, xslot, meta["STAT"][:, sc:sc + 1],
                        meta["STAT"][:, 16 + sc:17 + sc], ALU.add, ALU.mult)),
                    waits=aw, tick=True)
        ev_tr = E.dma_tr("sync", NF[:, :, tt * 128:(tt + 1) * 128], asl,
                         waits=[(E.ms["vector"], v_ap)])
        guard["a" + str(tt % 2)] = ev_tr
        v_tr_last = ev_tr
        last_ev = None
        for dt in range(ND):
            bank = (tt * ND + dt) % 2
            pst = PS[6 + bank]
            w2 = [guard[6 + bank]] if (6 + bank) in guard else []
            vtw = E.op("tensor",
                       (lambda e, s=xslot.rearrange("p (d q) -> p d q", q=128)[:, dt, :],
                        p=pst: e.transpose(p[:, 0:128], s, ident)),
                       waits=w2 + ([ev_ld, meta["ev_ident"]] if dt == 0 else []),
                       tick=True)
            last_ev = E.op("vector",
                           (lambda e, p=pst, d=XI[:, dt, tt * 128:(tt + 1) * 128]:
                            e.tensor_copy(d, p[:, 0:128])),
                           waits=[(E.ms["tensor"], vtw)], tick=True)
            guard[6 + bank] = (E.ms["vector"], last_ev)
            meta["ps_guard"][6 + bank] = guard[6 + bank]
        slot_free[sl] = (E.ms["vector"], last_ev)
        meta.setdefault("_in_slot_scalar", {})[sl] = v_rs
    meta["v_nf_ready"] = E.grp_waits("tr")
    meta["v_xi_ready"] = (E.ms["vector"], E.ms["vector"].n)


def emit_norm_F(meta, srcF, outF, eps, gamma=None, beta=None, shift=False,
                waits=(), out_dt_f32=None):
    nc, E = meta["nc"], meta["E"]
    BIG = meta["BIG"]
    STAT = meta["STAT"]
    SMALL = meta["SMALL"]
    stageT = BIG[:, 8192:24576].rearrange("p (tt d) -> p tt d", d=D)
    dump = meta["AD"][:, 15360:16384]
    cast_guard = {}
    v_tr = None
    stage_evs = []
    for dt in range(ND):
        csl = BIG[:, (dt % 2) * 2048:(dt % 2) * 2048 + 2048]
        w = list(waits) if dt == 0 else []
        if cast_guard.get(dt % 2):
            w.append(cast_guard[dt % 2])
        if shift and dt >= 4:
            E.op("scalar", (lambda e, d=csl[:, 0:2]: e.memzero(d)), waits=w)
            v_c = E.op("scalar", (lambda e, s=srcF[:, dt, 0:T - 1], d=csl[:, 1:T]:
                                  e.copy(d, s)), tick=True)
        else:
            v_c = E.op("scalar", (lambda e, s=srcF[:, dt, :], d=csl: e.copy(d, s)),
                       waits=w, tick=True)
        tw = [(E.ms["scalar"], v_c)]
        if dt == 0 and meta.get("_stageT_guard"):
            tw.append(meta["_stageT_guard"])
        v_tr = E.dma_tr("sync", stageT[:, :, dt * 128:(dt + 1) * 128], csl,
                        waits=tw)
        cast_guard[dt % 2] = v_tr
        stage_evs.append(v_tr)
    meta["_norm_cast_evs"] = list(stage_evs)
    all_cast = E.grp_waits("tr")
    rdy_tb = []
    for tb in range(4):
        base = 212 + (tb % 2) * 16
        sums = SMALL[:, base:base + 4]
        sqs = SMALL[:, base + 4:base + 8]
        m2 = SMALL[:, base + 8:base + 12]
        sc0 = (tb * 4) % 8
        nm = STAT[:, sc0:sc0 + 4]
        rs = STAT[:, 16 + sc0:16 + sc0 + 4]
        E.op("vector", (lambda e, s3=stageT[:, tb * 4:(tb + 1) * 4, :], o=sums:
                        e.reduce_sum(o, s3, axis=AX.X)),
             waits=all_cast if tb == 0 else [])
        v_s = None
        for j in range(4):
            v_s = E.op("scalar",
                       (lambda e, t=stageT[:, tb * 4 + j, :], o=sqs[:, j:j + 1]:
                        e.activation(dump, t, AF.Square, accum_out=o)),
                       waits=all_cast if (tb == 0 and j == 0) else [], tick=True)
        E.op("vector", (lambda e, nm=nm, s=sums: e.tensor_scalar(
            nm, s, -1.0 / float(D), None, ALU.mult)))
        E.op("vector", (lambda e, m2=m2, nm=nm: e.tensor_mul(m2, nm, nm)))
        E.op("vector", (lambda e, m2=m2, eps=eps: e.tensor_scalar(
            m2, m2, -eps, None, ALU.add)))
        v_var = E.op("vector", (lambda e, m2=m2, q=sqs: e.scalar_tensor_tensor(
            m2, q, 1.0 / float(D), m2, ALU.mult, ALU.subtract)),
            waits=[(E.ms["scalar"], v_s)], tick=True)
        E.op("scalar", (lambda e, m2=m2: e.activation(m2, m2, AF.Ln)),
             waits=[(E.ms["vector"], v_var)])
        v_rs = E.op("scalar", (lambda e, rs=rs, m2=m2: e.activation(
            rs, m2, AF.Exp, scale=-0.5)), tick=True)
        bk_evs = []
        for j in range(4):
            tt = tb * 4 + j
            sc = tt % 8
            v_ap = E.op("vector",
                        (lambda e, t=stageT[:, tt, :], sc=sc: e.tensor_scalar(
                            t, t, STAT[:, sc:sc + 1], STAT[:, 16 + sc:17 + sc],
                            ALU.add, ALU.mult)),
                        waits=[(E.ms["scalar"], v_rs)] if j == 0 else [],
                        tick=True)
            bk_evs.append(E.dma_tr("sync", outF[:, :, tt * 128:(tt + 1) * 128],
                                   stageT[:, tt, :],
                                   waits=[(E.ms["vector"], v_ap)]))
        if gamma is not None:
            v_g = None
            for dt in range(ND):
                v_g = E.op("vector",
                           (lambda e, dt=dt, t0=tb * 512: e.tensor_scalar(
                               outF[:, dt, t0:t0 + 512], outF[:, dt, t0:t0 + 512],
                               gamma[:, dt:dt + 1], beta[:, dt:dt + 1],
                               ALU.mult, ALU.add)),
                           waits=bk_evs if dt == 0 else [], tick=True)
            rdy_tb.append([(E.ms["vector"], v_g)])
        else:
            rdy_tb.append(list(bk_evs))
    meta["_stageT_guard"] = E.grp_waits("tr")
    meta["_norm_rdy_tb"] = rdy_tb
    rdy = rdy_tb[-1] if gamma is not None else E.grp_waits("tr")
    return rdy


def emit_store_output(meta):
    nc, E = meta["nc"], meta["E"]
    XI, BIG, PS = meta["XI"], meta["BIG"], meta["PS"]
    ident = meta["MISC"][:, 0:128]
    xf32 = BIG[:].bitcast(F32)
    E.barrier()
    guard = {k: v for k, v in meta["ps_guard"].items() if k in (6, 7)}
    v_out = None
    for tt in range(NT):
        sl = tt % 4
        xslot = xf32[:, sl * 1024:(sl + 1) * 1024]
        for dt in range(ND):
            bank = (tt * ND + dt) % 2
            pst = PS[6 + bank]
            w = [guard[6 + bank]] if (6 + bank) in guard else []
            vtw = E.op("tensor",
                       (lambda e, s=XI[:, dt, tt * 128:(tt + 1) * 128], p=pst:
                        e.transpose(p[:, 0:128], s, ident)),
                       waits=w + ([guard["s" + str(sl)]] if ("s" + str(sl)) in guard and dt == 0 else []),
                       tick=True)
            ve = E.op("vector",
                      (lambda e, p=pst, d=xslot.rearrange("p (d q) -> p d q", q=128)[:, dt, :]:
                       e.tensor_copy(d, p[:, 0:128])),
                      waits=[(E.ms["tensor"], vtw)], tick=True)
            guard[6 + bank] = (E.ms["vector"], ve)
            last_ev = ve
        v_out = E.dma("sync", meta["y"][tt * 128:(tt + 1) * 128, :], xslot,
                      grp="y", waits=[(E.ms["vector"], last_ev)])
        guard["s" + str(sl)] = v_out
    E.wait("sync", [v_out])


def emit_attn_a(meta):
    nc, E = meta["nc"], meta["E"]
    dram = meta["dram"]
    NF, AD, XI, BIG = meta["NF"], meta["AD"], meta["XI"], meta["BIG"]
    PS = meta["PS"]
    TAB, SC32, MISCB, ONESB = meta["TAB"], meta["SC32"], meta["MISCB"], meta["ONESB"]
    nf_rdy = meta["v_nf_ready"]

    ev_t1 = E.dma("sync", TAB[:, :, 0:64], dram["ropeA_cos"])
    ev_t2 = E.dma("sync", TAB[:, :, 64:128], dram["ropeA_sin"])
    cosT = TAB[:, :, 0:64]
    sinT = TAB[:, :, 64:128]

    ev_mask = E.dma("sync", MISCB[:, 512:640], dram["mask01T"])

    AD_ = AD[:]
    qF = AD_[:, 0:2048]
    kF = AD_[:, 2048:4096]
    vFb = AD_[:, 4096:6144]
    qT = AD_[:, 6144:8192]
    kT = AD_[:, 8192:10240]
    vT = AD_[:, 10240:12288]
    qrT = AD_[:, 12288:14336]
    krT = AD_[:, 14336:16384]
    oT_all = BIG[:, 8192:24576].rearrange(
        "p (tt c) -> p tt c", c=1024)
    ndT = BIG[:, 4096:6176].rearrange("p (c a) -> p c a", a=130)
    bigf = BIG[:].bitcast(F32)
    Rf32 = SC32[:, 0:128]
    Rb16 = MISCB[:, 0:128]
    stT = MISCB[:, 256:512].rearrange("p (a c) -> p a c", a=2)
    mask01 = MISCB[:, 512:640]

    W = {"q": dram["aWq"], "k": dram["aWk"], "v": dram["aWv"]}

    def wslice(w, hp):
        return W[w][hp]

    order = [(hp, w) for hp in range(8) for w in ("q", "k", "v")]
    wslots = {}
    ws = WStream(meta)

    def load_w(i):
        hp, w = order[i]
        sl, view, ev = ws.load(wslice(w, hp))
        wslots[(hp, w)] = (view, ev, sl)

    load_w(0)
    load_w(1)
    nload = [2]

    ps_guard = meta["ps_guard"]

    def mm(bank_i, out_sl, lhsT, rhs, first, last, waits=(), tp=None, po=(0, 128),
           notick=False, msem=None):
        w = list(waits)
        if first and bank_i in ps_guard:
            w.append(ps_guard[bank_i])
        return E.op("tensor",
                    (lambda e, o=PS[bank_i][po[0]:po[1], out_sl[0]:out_sl[1]],
                     l=lhsT, r=rhs, f=first, s=last, tp=tp:
                     e.matmul(o, l, r, start=f, stop=s, tile_position=tp,
                              skip_group_check=True)),
                    waits=w, tick=(last and not notick), tick_sem=msem)

    scan_done = None
    for hp in range(8):
        proj_wait = [nf_rdy, ev_t1, ev_t2, ev_mask] if hp == 0 else []
        if scan_done is not None:
            proj_wait = [(E.ms["tensor"], scan_done)]
        evacs = []
        for wi, w in enumerate(("q", "k", "v")):
            buf, ev_w, sl = wslots[(hp, w)]
            dstF = (qF, kF, vFb)[wi]
            lastmm = None
            for tb in range(4):
                bank_i = tb % 2
                for kk in range(ND):
                    lastmm = mm(bank_i, (0, 512), buf[:, kk, 0:128],
                                NF[:, kk, tb * 512:(tb + 1) * 512],
                                first=(kk == 0), last=(kk == ND - 1),
                                waits=([ev_w] + proj_wait)
                                if (tb == 0 and kk == 0) else ())
                ve = E.op("scalar",
                          (lambda e, b=PS[bank_i], d=dstF[:, tb * 512:(tb + 1) * 512]:
                           e.copy(d, b[:, 0:512])),
                          waits=[(E.ms["tensor"], lastmm)], tick=True)
                ps_guard[bank_i] = (E.ms["scalar"], ve)
                evacs.append(ve)
            ws.done(sl, (E.ms["tensor"], lastmm))
            if nload[0] < len(order):
                load_w(nload[0])
                nload[0] += 1
        v_evac = evacs[-1]
        ev_tq = E.dma_tr("sync", qT.rearrange("p (tt c) -> p tt c", c=128), qF,
                         waits=[(E.ms["scalar"], v_evac)])
        ev_tk = E.dma_tr("sync", kT.rearrange("p (tt c) -> p tt c", c=128), kF)
        ev_tv = E.dma_tr("sync", vT.rearrange("p (tt c) -> p tt c", c=128), vFb)
        def rope_phi(eng, src, dst, scr0, scr1, wait_tr, sguard):
            v_d = None
            for h in range(2):
                s4 = src[:, h * 1024:(h + 1) * 1024].rearrange(
                    "p (tt a c) -> p tt a c", a=2, c=64)
                d4 = dst[:, h * 1024:(h + 1) * 1024].rearrange(
                    "p (tt a c) -> p tt a c", a=2, c=64)
                r0 = scr0.rearrange("p (tt a c) -> p tt a c", a=2, c=64)
                r1 = scr1.rearrange("p (tt a c) -> p tt a c", a=2, c=64)
                tsl = slice(h * 8, (h + 1) * 8)
                cb = cosT[:, tsl, :].unsqueeze(2).broadcast_to([128, 8, 2, 64])
                sb_lo = sinT[:, tsl, 0:32].unsqueeze(2).broadcast_to([128, 8, 2, 32])
                sb_hi = sinT[:, tsl, 32:64].unsqueeze(2).broadcast_to([128, 8, 2, 32])
                w0 = [wait_tr] if h == 0 else []
                if h == 0 and sguard:
                    w0.append(sguard)
                if h == 1 and eng != "vector" and v_d is not None:
                    w0.append((E.ms["vector"], v_d))
                E.op(eng, (lambda e, r0=r0, s4=s4, cb=cb: e.tensor_mul(r0, s4, cb)),
                     waits=w0)
                E.op(eng, (lambda e, r1=r1, s4=s4, sb_lo=sb_lo:
                           e.tensor_mul(r1[:, :, :, 0:32], s4[:, :, :, 32:64], sb_lo)))
                E.op(eng, (lambda e, r1=r1, s4=s4, sb_hi=sb_hi:
                           e.tensor_mul(r1[:, :, :, 32:64], s4[:, :, :, 0:32], sb_hi)))
                v_a = E.op(eng, (lambda e, r0=r0, r1=r1: e.tensor_add(r0, r0, r1)),
                           tick=True)
                r0f = scr0
                r1f = scr1
                mw = [] if eng == "vector" else [(E.ms[eng], v_a)]
                v_m = E.op("vector", (lambda e, r0f=r0f, r1f=r1f:
                                      e.tensor_scalar(r1f, r0f, 0.0, None, ALU.min)),
                           waits=mw, tick=True)
                v_e = E.op("scalar", (lambda e, r1f=r1f: e.activation(r1f, r1f, AF.Exp)),
                           waits=[(E.ms["vector"], v_m)], tick=True)
                v_d = E.op("vector", (lambda e, d4=d4, r0=r0, r1=r1:
                                      e.scalar_tensor_tensor(d4, r0, 1.0, r1,
                                                             ALU.add, ALU.max)),
                           waits=[(E.ms["scalar"], v_e)], tick=True)
            return v_d
        v_q = rope_phi("vector", qT, qrT, bigf[:, 0:1024], bigf[:, 1024:2048],
                       ev_tq, meta.get("_aa_scrq"))
        v_k = rope_phi("gpsimd", kT, krT, bigf[:, 2048:3072], bigf[:, 3072:4096],
                       ev_tk, meta.get("_aa_scrk"))
        meta["_aa_scrq"] = (E.ms["vector"], v_q)
        meta["_aa_scrk"] = (E.ms["vector"], v_k)
        ev_bq = E.dma_tr("sync", qF.rearrange("p (tt c) -> p tt c", c=128), qrT,
                         waits=[(E.ms["vector"], v_q)])
        ev_bk = E.dma_tr("sync", kF.rearrange("p (tt c) -> p tt c", c=128), krT,
                         waits=[(E.ms["vector"], v_k)])
        qFr = qF.rearrange("p (cc c) -> p cc c", c=128)
        kFr = kF.rearrange("p (cc c) -> p cc c", c=128)
        vTr = vT.rearrange("p (tt c) -> p tt c", c=128)
        krTr = krT.rearrange("p (tt c) -> p tt c", c=128)

        v_state = None
        for c in range(NT):
            cw = [ev_bq, ev_bk, ev_tv] if c == 0 else []
            mSa = mm(2, (0, 128), kFr[0:64, c, :], qFr[0:64, c, :], True, True,
                     waits=cw)
            mSb = mm(3, (0, 128), kFr[64:128, c, :], qFr[64:128, c, :], True, True,
                     tp=(64, 0), msem="tensorB")
            vsA = E.op("vector", lambda e: e.tensor_mul(stT[:, 0, :], PS[2][:, 0:128], mask01),
                       waits=[(E.ms["tensor"], mSa)], tick=True)
            vsB = E.op("vector", lambda e: e.tensor_mul(stT[:, 1, :], PS[3][:, 0:128], mask01),
                       waits=[(E.ms["tensorB"], mSb)], tick=True)
            ps_guard[2] = (E.ms["vector"], vsA)
            ps_guard[3] = (E.ms["vector"], vsB)
            nm = {}
            for hh, bank_i in enumerate((4, 5)):
                e0, e1 = (0, 64) if hh == 0 else (64, 128)
                vs = vsA if hh == 0 else vsB
                msem = None if hh == 0 else "tensorB"
                w0 = [(E.ms["vector"], vs)]
                mm(bank_i, (0, 64), stT[:, hh, :], vTr[:, c, e0:e1], True, False,
                   waits=w0)
                last = (c == 0)
                nm[hh] = mm(bank_i, (64, 65), stT[:, hh, :], ONESB[:, 0:1], False,
                            last, msem=msem)
                if c > 0:
                    nm[hh] = mm(bank_i, (0, 65), qFr[e0:e1, c, :], Rb16[e0:e1, 0:65],
                                False, True, waits=[(E.ms["vector"], v_state)],
                                tp=None if hh == 0 else (64, 0), msem=msem)
            dw = [] if v_state is None else [(E.ms["vector"], v_state)]
            mm(6, (0, 64), krTr[:, c, 0:64], vTr[:, c, 0:64], True, False, waits=dw,
               po=(0, 64))
            mda = mm(6, (64, 65), krTr[:, c, 0:64], ONESB[:, 0:1], False, True,
                     po=(0, 64))
            mm(7, (0, 64), krTr[:, c, 64:128], vTr[:, c, 64:128], True, False,
               waits=dw, tp=(0, 64), po=(64, 128), msem="tensorB")
            mdb = mm(7, (64, 65), krTr[:, c, 64:128], ONESB[:, 0:1], False, True,
                     tp=(0, 64), po=(64, 128), msem="tensorB")
            dwait = [(E.ms["tensor"], mda), (E.ms["tensorB"], mdb)]
            if c == 0:
                E.op("vector", lambda e: e.tensor_copy(Rf32[0:64, 0:65], PS[6][0:64, 0:65]),
                     waits=dwait)
                vsa = E.op("vector", lambda e: e.tensor_copy(Rf32[64:128, 0:65], PS[7][64:128, 0:65]),
                           tick=True)
            else:
                E.op("vector", lambda e: e.tensor_add(Rf32[0:64, 0:65], Rf32[0:64, 0:65],
                                                      PS[6][0:64, 0:65]),
                     waits=dwait)
                vsa = E.op("vector", lambda e: e.tensor_add(Rf32[64:128, 0:65], Rf32[64:128, 0:65],
                                                            PS[7][64:128, 0:65]),
                           tick=True)
            v_state = E.op("vector", lambda e: e.tensor_copy(Rb16[:, 0:65], Rf32[:, 0:65]),
                           tick=True)
            ps_guard[6] = (E.ms["vector"], v_state)
            ps_guard[7] = (E.ms["vector"], v_state)
            for hh in range(2):
                bank_i = 4 + hh
                mw = ([(E.ms["tensor"], nm[0])] if hh == 0
                      else [(E.ms["tensorB"], nm[1])])
                vsc = E.op("scalar",
                           (lambda e, b=PS[bank_i], d=ndT[:, c, hh * 65:(hh + 1) * 65]:
                            e.copy(d, b[:, 0:65])),
                           waits=mw, tick=True)
                ps_guard[bank_i] = (E.ms["scalar"], vsc)
        SM = meta["SMALL"]
        v_nd = E.op("vector", lambda e: e.tensor_scalar(
            SM[:, 128:160], ndT[:, :, :].rearrange("p c (a b) -> p c a b", b=65)[:, :, :, 64],
            1e-6, None, ALU.max), waits=[(E.ms["scalar"], E.ms["scalar"].n)])
        v_nd = E.op("vector", lambda e: e.reciprocal(SM[:, 160:192], SM[:, 128:160]),
                    tick=True)
        nd4 = ndT[:, :, :].rearrange("p c (a b) -> p c a b", b=65)[:, :, :, 0:64]
        rd4 = SM[:, 160:192].rearrange("p (c a) -> p c a", a=2).unsqueeze(3).broadcast_to(
            [128, 16, 2, 64])
        o4 = oT_all[:, :, hp * 128:(hp + 1) * 128].rearrange(
            "p c (a b) -> p c a b", b=64)
        v_div = E.op("vector", (lambda e, nd4=nd4, rd4=rd4, o4=o4:
                                e.tensor_mul(o4, nd4, rd4)), tick=True)
        scan_done = E.op("tensor", lambda e: e.nop(),
                         waits=[(E.ms["tensorB"], E.ms["tensorB"].n),
                                (E.ms["vector"], v_div)], tick=True)

    v_oT = E.op("scalar", lambda e: e.nop(), tick=True)
    for tt in range(NT):
        E.dma_tr("sync", NF[:, :, tt * 128:(tt + 1) * 128],
                 oT_all[:, tt, :],
                 waits=[(E.ms["scalar"], v_oT),
                        (E.ms["vector"], E.ms["vector"].n)] if tt == 0 else [])
    ev_of = E.grp_waits("tr")
    emit_proj_accum(meta, dram["aWo"], NF, XI, ev_of, ps_guard)
    emit_dbg(meta, "x1F", lambda: XI[:], (128, ND, T), F32,
             [(E.ms["vector"], E.ms["vector"].n)])


def emit_proj_accum(meta, wdram, srcF, XI, src_rdy, ps_guard, bias_col=None,
                    n_in=ND, accumulate=True, outF=None, act=None):
    nc, E = meta["nc"], meta["E"]
    PS = meta["PS"]
    ws = WStream(meta)
    loads = {}

    def load_m(j):
        sl, view, ev = ws.load(wdram[j])
        loads[j] = (view[:, 0:n_in, :], ev, sl)

    load_m(0)
    load_m(1)
    for m in range(ND):
        buf, ev_w, sl = loads[m // 2]
        half = m % 2
        for tb in range(4):
            bank_i = tb % 2
            lastmm = None
            for kk in range(n_in):
                w = []
                if tb == 0 and kk == 0:
                    w = [ev_w, src_rdy]
                if kk == 0 and bank_i in ps_guard:
                    w.append(ps_guard[bank_i])
                lastmm = E.op("tensor",
                              (lambda e, b=buf, kk=kk, tb=tb, bi=bank_i, f=(kk == 0),
                               s=(kk == n_in - 1), h0=half * 128:
                               e.matmul(PS[bi][:, 0:512], b[:, kk, h0:h0 + 128],
                                        srcF[:, kk, tb * 512:(tb + 1) * 512],
                                        start=f, stop=s, skip_group_check=True)),
                              waits=w, tick=(kk == n_in - 1))
            if accumulate:
                ve = E.op("vector",
                          (lambda e, bi=bank_i, m=m, tb=tb:
                           e.tensor_add(XI[:, m, tb * 512:(tb + 1) * 512],
                                        XI[:, m, tb * 512:(tb + 1) * 512],
                                        PS[bi][:, 0:512])),
                          waits=[(E.ms["tensor"], lastmm)], tick=True)
                ps_guard[bank_i] = (E.ms["vector"], ve)
            else:
                dst = (outF if outF is not None else XI)
                if act is not None:
                    ve = E.op("scalar",
                              (lambda e, bi=bank_i, d=dst[:, m, tb * 512:(tb + 1) * 512]:
                               e.activation(d, PS[bi][:, 0:512], act)),
                              waits=[(E.ms["tensor"], lastmm)], tick=True)
                else:
                    ve = E.op("scalar",
                              (lambda e, bi=bank_i, d=dst[:, m, tb * 512:(tb + 1) * 512]:
                               e.copy(d, PS[bi][:, 0:512])),
                              waits=[(E.ms["tensor"], lastmm)], tick=True)
                ps_guard[bank_i] = (E.ms["scalar"], ve)
        if m % 2 == 1:
            ws.done(sl, (E.ms["tensor"], E.ms["tensor"].n))
            if m // 2 + 2 < ND // 2:
                load_m(m // 2 + 2)


def emit_ffn_block(meta, app):
    meta["_ffn_app"] = app
    nc, E = meta["nc"], meta["E"]
    dram = meta["dram"]
    XI, BIG, NF, AD = meta["XI"], meta["BIG"], meta["NF"], meta["AD"]
    PS, PSALL = meta["PS"], meta["PSALL"]
    ps_guard = meta["ps_guard"]
    SC32, STAT, SMALL, TAB, MISC = (meta["SC32"], meta["STAT"], meta["SMALL"],
                                    meta["TAB"], meta["MISC"])
    xo = meta["xo_spill"]

    pw = [(E.ms["vector"], E.ms["vector"].n), (E.ms["scalar"], E.ms["scalar"].n),
          (E.ms["tensor"], E.ms["tensor"].n)]
    ev_sp = E.dma("sync", xo[:], XI[:], grp="spill", waits=pw, ngrp=2)
    rdy_n = emit_norm_F(meta, XI, NF, eps=1e-5, waits=pw)
    meta["_blk_rdy_tb"] = meta["_norm_rdy_tb"]
    v_up = None
    for dt in range(ND):
        v_up = E.op("scalar",
                    (lambda e, dt=dt: e.copy(XI[:, dt, :], NF[:, dt, :])),
                    waits=([ev_sp] + rdy_n) if dt == 0 else [], tick=True)
    meta["_g_ready"] = (E.ms["scalar"], v_up)
    emit_dbg(meta, f"a{app}_n2F", lambda: meta["NF"][:], (128, ND, T), BF16,
             rdy_n)

    for l in range(2):
        emit_ffn_layer(meta, app, l)

    rb_guard = [(E.ms["vector"], E.ms["vector"].n)]
    bigf = BIG[:].bitcast(F32)
    for dt in range(ND):
        sl = dt % 2
        stg = bigf[:, sl * 2048:(sl + 1) * 2048]
        w = list(rb_guard) if dt < 2 else [meta[f"_rb{sl}"]]
        ev = E.dma("sync", stg, xo[:, dt, :], grp="spill", waits=w, ngrp=2)
        v_a = E.op("vector",
                   (lambda e, dt=dt, stg=stg: e.tensor_add(XI[:, dt, :],
                                                           XI[:, dt, :], stg)),
                   waits=[ev], tick=True)
        meta[f"_rb{sl}"] = (E.ms["vector"], v_a)
    emit_dbg(meta, f"x{2 if app == 0 else 4}F", lambda: XI[:], (128, ND, T), F32,
             [(E.ms["vector"], E.ms["vector"].n)])


def emit_ffn_layer(meta, app, l):
    nc, E = meta["nc"], meta["E"]
    dram = meta["dram"]
    XI, BIG, NF, AD = meta["XI"], meta["BIG"], meta["NF"], meta["AD"]
    PS, PSALL = meta["PS"], meta["PSALL"]
    ps_guard = meta["ps_guard"]
    SC32, STAT, SMALL, TAB = meta["SC32"], meta["STAT"], meta["SMALL"], meta["TAB"]
    NFR = meta["NFR"]

    app = meta.get("_ffn_app", 0)
    par = (app * 2 + l) % 2
    rb = 256 + par * 144
    qs_row = SC32[0:1, rb:rb + 64]
    ks_row = SC32[0:1, rb + 64:rb + 128]
    lng = SC32[:, rb + 128:rb + 136]
    lnb = SC32[:, rb + 136:rb + 144]
    ev_qs = E.dma("sync", qs_row, dram["fqs"][l:l + 1, :], grp="io")
    ev_ks = E.dma("sync", ks_row, dram["fks"][l:l + 1, :], grp="io")
    ev_ln = E.dma("sync", lng, dram["flng"][l], grp="io")
    ev_lnb = E.dma("sync", lnb, dram["flnb"][l], grp="io")
    snapT = E.snap_all()
    ev_tb = []
    for i, nm in enumerate(("rotF_qcos", "rotF_qsin", "rotF_kcos", "rotF_ksin")):
        ev_tb.append(E.dma("gpsimd", TAB[:, :, i * 64:(i + 1) * 64], dram[nm],
                           grp="w", ngrp=2, waits=snapT if i == 0 else []))
    qs_bc = SC32[:, 544:608]
    ks_bc = SC32[:, 608:672]
    mm0 = E.op("tensor",
               (lambda e: e.matmul(PS[6][:, 0:64], SMALL[0:1, 0:128],
                                   qs_row, start=True, stop=True,
                                   skip_group_check=True)),
               waits=[ev_qs, meta["v_zo"]] + ([ps_guard[6]] if 6 in ps_guard else []),
               tick=True)
    mm1 = E.op("tensor",
               (lambda e: e.matmul(PS[7][:, 0:64], SMALL[0:1, 0:128],
                                   ks_row, start=True, stop=True,
                                   skip_group_check=True)),
               waits=[ev_ks] + ([ps_guard[7]] if 7 in ps_guard else []), tick=True)
    v_qsb = E.op("vector", lambda e: e.tensor_scalar(qs_bc, PS[6][:, 0:64], 8.0,
                                                     None, ALU.mult),
                 waits=[(E.ms["tensor"], mm0)], tick=True)
    v_ksb = E.op("vector", lambda e: e.tensor_copy(ks_bc, PS[7][:, 0:64]),
                 waits=[(E.ms["tensor"], mm1)], tick=True)
    ps_guard[6] = (E.ms["vector"], v_qsb)
    ps_guard[7] = (E.ms["vector"], v_ksb)
    tQC, tQS = TAB[:, :, 0:64], TAB[:, :, 64:128]
    tKC, tKS = TAB[:, :, 128:192], TAB[:, :, 192:256]
    mQC, mKC = TAB[:, :, 256:320], TAB[:, :, 320:384]
    mQS, mKS = TAB[:, :, 384:448], TAB[:, :, 448:512]

    def bcn(col_ap, n=64):
        return col_ap.unsqueeze(1).broadcast_to([128, NT, n])

    E.op("vector", lambda e: e.tensor_mul(mQC, tQC, bcn(qs_bc)), waits=ev_tb)
    E.op("vector", lambda e: e.tensor_mul(mKC, tKC, bcn(ks_bc)))
    E.op("vector", lambda e: e.tensor_mul(mQS[:, :, 0:32], tQS[:, :, 0:32],
                                          bcn(qs_bc[:, 32:64], 32)))
    E.op("vector", lambda e: e.tensor_mul(mQS[:, :, 32:64], tQS[:, :, 32:64],
                                          bcn(qs_bc[:, 0:32], 32)))
    E.op("vector", lambda e: e.tensor_mul(mKS[:, :, 0:32], tKS[:, :, 0:32],
                                          bcn(ks_bc[:, 32:64], 32)))
    v_mg = E.op("vector", lambda e: e.tensor_mul(mKS[:, :, 32:64], tKS[:, :, 32:64],
                                                 bcn(ks_bc[:, 0:32], 32)), tick=True)

    def h_rhs(kk, t0, t1):
        if kk < 4:
            return NF[:, kk, t0:t1]
        return NFR[:, kk, 15 + t0:15 + t1]
    qkT = BIG[:, 8192:24576].rearrange("p (tt c) -> p tt c", c=1024)
    vTb = BIG[:, 0:8192].rearrange("p (tt c) -> p tt c", c=512)
    evsl = [AD[:, 0:2048], AD[:, 2048:4096]]
    esl_guard = {}
    vt_ev = {}
    wq = dram["fwqkv"][l]
    wslots = {}
    ws = WStream(meta)

    def load_qkv(j):
        sl, view, ev = ws.load(wq[j])
        wslots[j] = (view, ev, sl)

    load_qkv(0)
    load_qkv(1)
    if l > 0:
        v_c = None
        for dt in range(ND):
            v_c = E.op("scalar",
                       (lambda e, dt=dt: e.copy(NF[:, dt, :], XI[:, dt, :])),
                       waits=[(E.ms["vector"], E.ms["vector"].n)] if dt == 0 else [],
                       tick=True)
        nf_tb = [[(E.ms["scalar"], v_c)]] * 4
    else:
        nf_tb = meta["_blk_rdy_tb"]
    for mi in range(12):
        buf, ev_w, sl = wslots[mi // 2]
        halfq = (mi % 2) * 128
        lastmm = None
        for tb in range(4):
            bank_i = tb % 2
            for kk in range(ND):
                w = []
                if kk == 0:
                    if tb == 0:
                        w = [ev_w]
                    if mi == 0:
                        w += list(E._flat(nf_tb[tb]))
                        if tb > 0:
                            w += list(E._flat(nf_tb[tb - 1]))
                    if bank_i in ps_guard:
                        w.append(ps_guard[bank_i])
                lastmm = E.op("tensor",
                              (lambda e, b=buf, kk=kk,
                               rhs=h_rhs(kk, tb * 512, (tb + 1) * 512),
                               bi=bank_i, f=(kk == 0), s=(kk == ND - 1), h0=halfq:
                               e.matmul(PS[bi][:, 0:512], b[:, kk, h0:h0 + 128],
                                        rhs, start=f, stop=s,
                                        skip_group_check=True)),
                              waits=w, tick=(kk == ND - 1))
            ew = [(E.ms["tensor"], lastmm)]
            if (mi % 2) in esl_guard and tb == 0:
                ew.append(esl_guard[mi % 2])
            ve = E.op("scalar",
                      (lambda e, bi=bank_i,
                       d=evsl[mi % 2][:, tb * 512:(tb + 1) * 512]:
                       e.copy(d, PS[bi][:, 0:512])),
                      waits=ew, tick=True)
            ps_guard[bank_i] = (E.ms["scalar"], ve)
        if mi % 2 == 1:
            ws.done(sl, (E.ms["tensor"], lastmm))
            if mi // 2 + 2 < 6:
                load_qkv(mi // 2 + 2)
        if mi < 4:
            dst = qkT[:, :, mi * 128:(mi + 1) * 128]
        elif mi < 8:
            dst = qkT[:, :, 512 + (mi - 4) * 128:512 + (mi - 3) * 128]
        else:
            dst = vTb[:, :, (mi - 8) * 128:(mi - 7) * 128]
        ev_t = E.dma_tr("sync", dst, evsl[mi % 2], waits=[(E.ms["scalar"], ve)])
        esl_guard[mi % 2] = ev_t
        if mi >= 8:
            vt_ev[mi - 8] = ev_t
    qkT_rdy = E.grp_waits("tr")

    MB = NF[:, 0, 0:1280]
    mw0 = [(E.ms["tensor"], E.ms["tensor"].n), meta["ev_maskU"], meta["ev_maskL"]]
    E.op("vector", lambda e: e.memset(MB, 0.0), waits=mw0)
    E.op("vector", lambda e: e.tensor_copy(MB[:, 0:128], meta["MISC"][:, 256:384]))
    E.op("vector", lambda e: e.tensor_copy(MB[:, 512:640], meta["MISC"][:, 128:256]))
    E.op("vector", lambda e: e.tensor_copy(MB[:, 1152:1280], meta["MISC"][:, 128:256]))

    nfrf = NFR[:].rearrange("p a b -> p (a b)").bitcast(F32)
    sq = nfrf[:, 2064:3088]
    u_s = nfrf[:, 3088:3600]
    w_s = nfrf[:, 3600:4112]
    u_g = nfrf[:, 4112:4624]
    w_g = nfrf[:, 4624:5136]
    qF = AD[:, 0:8192].rearrange("p (j t) -> p j t", t=2048)
    kF = AD[:, 8192:16384].rearrange("p (j t) -> p j t", t=2048)
    v_rot_last = None
    g_rn_prev = None
    for tt in range(NT):
        w0 = list(qkT_rdy) + [(E.ms["vector"], v_mg)] if tt == 0 else []
        v_sq = E.op("scalar", (lambda e, tt=tt: e.activation(sq, qkT[:, tt, :],
                                                             AF.Square)),
                    waits=w0, tick=True)
        E.op("vector", (lambda e: e.reduce_sum(
            STAT[:, 32:48], sq.rearrange("p (s c) -> p s c", c=64), axis=AX.X)),
            waits=[(E.ms["scalar"], v_sq)] +
                  ([(E.ms["gpsimd"], g_rn_prev)] if g_rn_prev else []))
        v_ns = E.op("vector", lambda e: e.tensor_scalar(STAT[:, 32:48], STAT[:, 32:48],
                                                        1e-24, None, ALU.max),
                    tick=True)
        E.op("scalar", lambda e: e.activation(STAT[:, 48:64], STAT[:, 32:48], AF.Ln),
             waits=[(E.ms["vector"], v_ns)])
        v_rn = E.op("scalar", lambda e: e.activation(STAT[:, 32:48], STAT[:, 48:64],
                                                     AF.Exp, scale=-0.5), tick=True)
        for half in range(2):
            eng = "vector" if half == 0 else "gpsimd"
            seg = qkT[:, tt, half * 512:(half + 1) * 512].rearrange(
                "p (s c) -> p s c", c=64)
            rn_bc = STAT[:, 32 + half * 8:40 + half * 8].unsqueeze(2).broadcast_to(
                [128, 8, 64])
            cosb = (mQC if half == 0 else mKC)[:, tt, :]
            sinb = (mQS if half == 0 else mKS)[:, tt, :]
            cos_bc = cosb.unsqueeze(1).broadcast_to([128, 8, 64])
            sin_lo = sinb[:, 0:32].unsqueeze(1).broadcast_to([128, 8, 32])
            sin_hi = sinb[:, 32:64].unsqueeze(1).broadcast_to([128, 8, 32])
            uv = (u_s if half == 0 else u_g).rearrange("p (s c) -> p s c", c=64)
            wv = (w_s if half == 0 else w_g).rearrange("p (s c) -> p s c", c=64)
            hw = [(E.ms["scalar"], v_rn)]
            if half == 1 and tt == 0:
                hw.append((E.ms["vector"], v_mg))
            g_rn = E.op(eng, (lambda e, seg=seg, rn_bc=rn_bc, uv=uv:
                              e.tensor_mul(uv, seg, rn_bc)),
                        waits=hw, tick=True)
            if half == 1:
                g_rn_prev = g_rn
            E.op(eng, (lambda e, uv=uv, wv=wv, sin_lo=sin_lo:
                       e.tensor_mul(wv[:, :, 0:32], uv[:, :, 32:64], sin_lo)))
            E.op(eng, (lambda e, uv=uv, wv=wv, sin_hi=sin_hi:
                       e.tensor_mul(wv[:, :, 32:64], uv[:, :, 0:32], sin_hi)))
            E.op(eng, (lambda e, uv=uv, cos_bc=cos_bc:
                       e.tensor_mul(uv, uv, cos_bc)))
            v_rot_last = E.op(eng, (lambda e, seg=seg, uv=uv, wv=wv:
                                    e.tensor_add(seg, uv, wv)), tick=True)
            if half == 0:
                v_rot_q = v_rot_last
            else:
                g_rot_k = v_rot_last
        E.dma_tr("sync", qF[:, :, tt * 128:(tt + 1) * 128], qkT[:, tt, 0:512],
                 waits=[(E.ms["vector"], v_rot_q)])
        E.dma_tr("sync", kF[:, :, tt * 128:(tt + 1) * 128], qkT[:, tt, 512:1024],
                 waits=[(E.ms["gpsimd"], g_rot_k)])
    qkF_rdy = E.grp_waits("tr")

    srow = [BIG[:, 8192:9216], BIG[:, 9216:10240],
            BIG[:, 14336:15360], BIG[:, 15360:16384]]
    stT = BIG[:, 10240:14336].rearrange("p (j i) -> p j i", i=512)
    stTB = TAB[:].rearrange("p a b -> p (a b)")[:, 0:4096].rearrange(
        "p (j i) -> p j i", i=512)
    oTF = BIG[:, 16384:24576].rearrange("p (j t) -> p j t", t=2048)
    SREG = (0, 1024, 3072)
    sg = {}
    st_ev = {}
    strip = 0
    reg_first = [True, True, True]
    for pj in range(4):
        for w in range(4):
            for it in range(4):
                for hh in range(2):
                    e0, e1 = (0, 64) if hh == 0 else (64, 128)
                    r = strip % 3
                    strip += 1
                    base = SREG[r]
                    msem = None if hh == 0 else "tensorB"
                    mssem = E.ms["tensor" if hh == 0 else "tensorB"]
                    span = (it + 1) * 128 if w == 0 else 640
                    jcol0 = (w - 1) * 512 + it * 128 if w > 0 else 0
                    icol = w * 512 + it * 128
                    Sreg = PSALL[:, base: base + span]
                    wk = list(qkF_rdy) if (pj == 0 and w == 0 and hh == 0
                                           and it == 0) else []
                    if reg_first[r]:
                        reg_first[r] = False
                        for b in ((0, 1), (2, 3), (6, 7))[r]:
                            if b in ps_guard:
                                wk.append(ps_guard[b])
                    if ("s", r) in sg:
                        wk.append(sg[("s", r)])
                    pieces = [512, span - 512] if span > 512 else [span]
                    off = 0
                    mS = None
                    for pi, nw in enumerate(pieces):
                        if nw <= 0:
                            continue
                        lastp = (pi == len(pieces) - 1) or pieces[-1] <= 0
                        mS = E.op("tensor",
                                  (lambda e,
                                   o=PSALL[:, base + off: base + off + nw],
                                   lh=qF[e0:e1, pj, icol:icol + 128],
                                   rh=kF[e0:e1, pj, jcol0 + off:jcol0 + off + nw],
                                   tp=(64, 0) if hh else None:
                                   e.matmul(o, lh, rh, start=True, stop=True,
                                            tile_position=tp,
                                            skip_group_check=True)),
                                  waits=wk if off == 0 else [],
                                  tick=lastp, tick_sem=msem)
                        off += nw
                    msel = MB[:, 0:640] if w > 0 else MB[:, 1280 - span:1280]
                    macol = SMALL[:, 140 + r:141 + r]
                    sucol = SMALL[:, 144 + r:145 + r]
                    recol = SMALL[:, 148 + r:149 + r]
                    E.op("vector",
                         (lambda e, S=Sreg, M=msel:
                          e.tensor_add(S, S, M)), waits=[(mssem, mS)])
                    v_ttr = E.op("vector",
                                 (lambda e, S=Sreg, macol=macol:
                                  e.reduce_max(macol, S, axis=AX.X,
                                               negate=True)), tick=True)
                    ssl = srow[(2 * it + hh) % 4]
                    ew = [(E.ms["vector"], v_ttr)]
                    if ("sr", (2 * it + hh) % 4) in sg:
                        ew.append(sg[("sr", (2 * it + hh) % 4)])
                    v_exp = E.op("scalar",
                                 (lambda e, S=Sreg, d=ssl, span=span,
                                  macol=macol, sucol=sucol:
                                  e.activation(d[:, 0:span], S, AF.Exp,
                                               bias=macol,
                                               accum_out=sucol)),
                                 waits=ew, tick=True)
                    sg[("s", r)] = (E.ms["scalar"], v_exp)
                    rcw = [(E.ms["scalar"], v_exp)]
                    if ("rc", r) in sg:
                        rcw.append(sg[("rc", r)])
                    v_rc = E.op("vector",
                                (lambda e, recol=recol, sucol=sucol:
                                 e.reciprocal(recol, sucol)),
                                waits=rcw, tick=True)
                    v_sc = E.op("scalar",
                                (lambda e, d=ssl, span=span, recol=recol:
                                 e.activation(d[:, 0:span], d[:, 0:span],
                                              AF.Copy, scale=recol)),
                                waits=[(E.ms["vector"], v_rc)], tick=True)
                    sg[("rc", r)] = (E.ms["scalar"], v_sc)
                    nblk = span // 128
                    stT_h = stT if hh == 0 else stTB
                    j0 = 4 if w == 0 else it
                    ev_st = E.dma_tr("sync",
                                     stT_h[:, j0:j0 + nblk,
                                           it * 128:(it + 1) * 128],
                                     ssl[:, 0:span],
                                     waits=[(E.ms["scalar"], v_sc)])
                    sg[("sr", (2 * it + hh) % 4)] = ev_st
                    st_ev[(it, hh)] = ev_st
            for hh in range(2):
                e0, e1 = (0, 64) if hh == 0 else (64, 128)
                msem = None if hh == 0 else "tensorB"
                mssem = E.ms["tensor" if hh == 0 else "tensorB"]
                ob = 4 if hh == 0 else 5
                stT_h = stT if hh == 0 else stTB
                for it in range(4):
                    jr = list(range(it, it + 5)) if w > 0 else list(range(4, 5 + it))
                    ow = [st_ev[(it, hh)]]
                    if (ob, "o") in sg:
                        ow.append(sg.pop((ob, "o")))
                    elif ob in ps_guard:
                        ow.append(ps_guard.pop(ob))
                    if w == 0 and it == 0:
                        ow.append(vt_ev[pj])
                    lastmm = None
                    for ji, jrel in enumerate(jr):
                        jt_abs = (w - 1) * 4 + jrel
                        lastmm = E.op("tensor",
                                      (lambda e,
                                       o=PSALL[e0:e1, ob * 512:ob * 512 + 128],
                                       lh=vTb[:, jt_abs, pj * 128 + e0:pj * 128 + e1],
                                       rh=stT_h[:, jrel, it * 128:(it + 1) * 128],
                                       f=(ji == 0), s=(ji == len(jr) - 1),
                                       tp=None if hh == 0 else (0, 64):
                                       e.matmul(o, lh, rh, start=f, stop=s,
                                                tile_position=tp,
                                                skip_group_check=True)),
                                      waits=ow if ji == 0 else [],
                                      tick=(ji == len(jr) - 1), tick_sem=msem)
                    v_oe = E.op("scalar",
                                (lambda e, it=it,
                                 d=oTF[e0:e1, pj, w * 512 + it * 128:w * 512 + (it + 1) * 128],
                                 s=PSALL[e0:e1, ob * 512:ob * 512 + 128]:
                                 e.copy(d, s)),
                                waits=[(mssem, lastmm)], tick=True)
                    sg[(ob, "o")] = (E.ms["scalar"], v_oe)
    v_attn = E.op("scalar", lambda e: e.nop(), tick=True)
    emit_dbg(meta, f"fl{l}_oT", lambda: BIG[:, 16384:24576], (128, 8192), BF16,
             [(E.ms["scalar"], v_attn)])
    emit_dbg(meta, f"fl{l}_qkT", lambda: BIG[:, 8192:24576], (128, 16384), BF16,
             [(E.ms["vector"], v_rot_last)])
    ps_guard[0] = ps_guard[1] = sg[("s", 0)]
    ps_guard[2] = ps_guard[3] = sg[("s", 1)]
    ps_guard[6] = ps_guard[7] = sg[("s", 2)]
    ps_guard[4] = sg[(4, "o")]
    ps_guard[5] = sg[(5, "o")]

    emit_proj_accum(meta, dram["fwo"][l], oTF, XI,
                    [(E.ms["scalar"], v_attn)], ps_guard, n_in=4)
    v_res1 = (E.ms["vector"], E.ms["vector"].n)
    emit_dbg(meta, f"fl{l}_res1", lambda: XI[:], (128, ND, T), F32, [v_res1])

    rdy2 = emit_norm_F(meta, XI, NF, eps=1e-5, gamma=lng,
                       beta=lnb, shift=True,
                       waits=[v_res1, ev_ln, ev_lnb])

    emit_ffn_glu(meta, l, rdy2)
    emit_dbg(meta, f"fl{l}_res2", lambda: XI[:],
             (128, ND, T), F32, [(E.ms["vector"], E.ms["vector"].n)])


def emit_ffn_glu(meta, l, rdy):
    nc, E = meta["nc"], meta["E"]
    dram = meta["dram"]
    XI, NF, AD, PS, PSALL = (meta["XI"], meta["NF"], meta["AD"], meta["PS"],
                             meta["PSALL"])
    ps_guard = meta["ps_guard"]
    SC32 = meta["SC32"]
    w1r = dram["fw1"][l]
    w2r = dram["fw2"][l]
    NH = 2730
    nch = 22
    glu = AD[:, 0:11264].rearrange("p (i t) -> p i t", t=512)
    gel = AD[:, 11264:11776]
    ev_b1 = E.dma("sync", SC32[:, 672:716], dram["fb1"][l], grp="io",
                  waits=[meta["_norm_cast_evs"][0]])
    ev_b2 = E.dma("sync", SC32[:, 716:724], dram["fb2"][l], grp="io")
    rdy_tb = meta["_norm_rdy_tb"]
    wload = {}
    BIGW = meta["BIG"][:, 0:8192].rearrange("p (s b) -> p s b", s=4)
    ws = WStream(meta, key="bigw", buf=BIGW, nslot=4, grp="bigw",
                 init_extra=list(E._flat(meta["_norm_cast_evs"])) +
                 list(E._flat(E.snap_w())))

    wc, wce = meta["wcache"], meta["wcache_ev"]

    def load_w1(i):
        cw = min(128, NH - i * 128)
        sl, view, ev = ws.load_cached(w1r[i], wc["fw1"][l][i],
                                      ("fw1", l, i), wce)
        wload[i] = (view, ev, sl, cw)

    def load_w2(i):
        rw = min(128, NH - i * 128)
        sl, view, ev = ws.load_cached(w2r[i], wc["fw2"][l][i],
                                      ("fw2", l, i), wce)
        wload[("w2", i)] = (view, ev, sl, rw)

    for tb in range(4):
        t0 = tb * 512
        for _j in range(4):
            load_w1(_j)
        for i in range(nch):
            bufP, evP, slP, cwP = wload[i]
            ba = i % 2
            bg = 2 + i % 2
            lastA = lastG = None
            for ag, bi in ((0, ba), (1, bg)):
                cw = cwP
                last = None
                for kk in range(ND):
                    w = []
                    if kk == 0:
                        w = [evP]
                        if i == 0 and ag == 0:
                            w += list(E._flat(rdy_tb[tb]))
                        if bi in ps_guard:
                            w.append(ps_guard[bi])
                    last = E.op("tensor",
                                (lambda e, b=bufP, kk=kk, cw=cw, bi=bi,
                                 f=(kk == 0), s=(kk == ND - 1), t0=t0,
                                 h0=ag * 128:
                                 e.matmul(PSALL[0:cw, bi * 512:bi * 512 + 512],
                                          b[:, kk, h0:h0 + cw],
                                          NF[:, kk, t0:t0 + 512],
                                          start=f, stop=s, skip_group_check=True)),
                                waits=w, tick=(kk == ND - 1))
                if ag == 0:
                    lastA = last
                else:
                    lastG = last
            cwA = cwG = cwP
            ws.done(slP, (E.ms["tensor"], lastG))
            if i + 4 < nch:
                load_w1(i + 4)
            gw = [(E.ms["tensor"], lastG), ev_b1]
            if "pglu" in wload:
                gw.append(wload["pglu"])
            v_g = E.op("scalar",
                       (lambda e, cw=cwG, bi=bg, i=i:
                        e.activation(gel[0:cw, 0:512],
                                     PSALL[0:cw, bi * 512:bi * 512 + 512], AF.Gelu,
                                     bias=SC32[0:cw, 694 + i:695 + i])),
                       waits=gw, tick=True)
            ps_guard[bg] = (E.ms["scalar"], v_g)
            v_glu = E.op("vector",
                         (lambda e, cw=cwA, bi=ba, i=i:
                          e.scalar_tensor_tensor(glu[0:cw, i, :],
                                                 PSALL[0:cw, bi * 512:bi * 512 + 512],
                                                 SC32[0:cw, 672 + i:673 + i],
                                                 gel[0:cw, 0:512],
                                                 ALU.add, ALU.mult)),
                         waits=[(E.ms["tensor"], lastA), (E.ms["scalar"], v_g),
                                ev_b1], tick=True)
            ps_guard[ba] = (E.ms["vector"], v_glu)
            wload["pglu"] = (E.ms["vector"], v_glu)
        v_glu_all = (E.ms["vector"], E.ms["vector"].n)
        for _j in range(4):
            load_w2(_j)
        vfin = None
        for i in range(nch):
            bview, ev, sl, rw = wload[("w2", i)]
            for m in range(ND):
                w = []
                if m == 0:
                    w = [ev]
                    if i == 0:
                        w.append(v_glu_all)
                if i == 0 and m in ps_guard:
                    w.append(ps_guard[m])
                vfin = E.op("tensor",
                            (lambda e, bv=bview, m=m, rw=rw, f=(i == 0),
                             s=(i == nch - 1), i=i:
                             e.matmul(PSALL[:, m * 512:m * 512 + 512],
                                      bv[0:rw, m * 128:(m + 1) * 128],
                                      glu[0:rw, i, :], start=f, stop=s,
                                      skip_group_check=True)),
                            waits=w, tick=(m == ND - 1))
            ws.done(sl, (E.ms["tensor"], vfin))
            if i + 4 < nch:
                load_w2(i + 4)
        for m in range(ND):
            v_e = E.op("vector",
                       (lambda e, m=m, t0=t0:
                        e.scalar_tensor_tensor(XI[:, m, t0:t0 + 512],
                                               PSALL[:, m * 512:m * 512 + 512],
                                               SC32[:, 716 + m:717 + m],
                                               XI[:, m, t0:t0 + 512],
                                               ALU.add, ALU.add)),
                       waits=[(E.ms["tensor"], vfin), ev_b2], tick=True)
            ps_guard[m] = (E.ms["vector"], v_e)


def emit_attn_b(meta):
    nc, E = meta["nc"], meta["E"]
    dram = meta["dram"]
    XI, BIG, NF, AD = meta["XI"], meta["BIG"], meta["NF"], meta["AD"]
    xo = meta["xo_spill"]
    SC32 = meta["SC32"]

    pw = [(E.ms["vector"], E.ms["vector"].n), (E.ms["scalar"], E.ms["scalar"].n),
          (E.ms["tensor"], E.ms["tensor"].n)]
    ev_sp = E.dma("sync", xo[:], XI[:], grp="spill", waits=pw, ngrp=2)
    rdy_n = emit_norm_F(meta, XI, NF, eps=1e-5, waits=pw)
    v_up = None
    for dt in range(ND):
        v_up = E.op("scalar",
                    (lambda e, dt=dt: e.copy(XI[:, dt, :], NF[:, dt, :])),
                    waits=([ev_sp] + rdy_n) if dt == 0 else [], tick=True)
    meta["_g_ready"] = (E.ms["scalar"], v_up)
    snapT = E.snap_all()
    TAB = meta["TAB"]
    for i, nm in enumerate(("xpos_qcos", "xpos_qsin", "xpos_kcos", "xpos_ksin")):
        E.dma("sync", TAB[:, :, i * 128:(i + 1) * 128], dram[nm], grp="io",
              waits=snapT if i == 0 else [])
    meta["_xpos_rdy"] = E.grp_waits("io")

    for l in range(2):
        emit_retention_layer(meta, l)

    rb_guard = [(E.ms["vector"], E.ms["vector"].n),
                (E.ms["scalar"], E.ms["scalar"].n),
                (E.ms["tensor"], E.ms["tensor"].n)]
    bigf = BIG[:].bitcast(F32)
    for dt in range(ND):
        sl = dt % 2
        stg = bigf[:, sl * 2048:(sl + 1) * 2048]
        w = list(rb_guard) if dt < 2 else [meta[f"_rb{sl}"]]
        ev = E.dma("sync", stg, xo[:, dt, :], grp="spill", waits=w, ngrp=2)
        v_a = E.op("vector",
                   (lambda e, dt=dt, stg=stg: e.tensor_add(XI[:, dt, :],
                                                           XI[:, dt, :], stg)),
                   waits=[ev], tick=True)
        meta[f"_rb{sl}"] = (E.ms["vector"], v_a)
    emit_dbg(meta, "x3F", lambda: XI[:], (128, ND, T), F32,
             [(E.ms["vector"], E.ms["vector"].n)])


def emit_retention_layer(meta, l):
    nc, E = meta["nc"], meta["E"]
    dram = meta["dram"]
    XI, BIG, NF, AD = meta["XI"], meta["BIG"], meta["NF"], meta["AD"]
    PS, PSALL = meta["PS"], meta["PSALL"]
    ps_guard = meta["ps_guard"]
    SC32, STAT, SMALL, TAB, MISCB = (meta["SC32"], meta["STAT"], meta["SMALL"],
                                     meta["TAB"], meta["MISCB"])
    gam = [1.0 - 2.0 ** (-5.0 - hh) for hh in range(8)]
    gamC = [float(g ** 128) for g in gam]

    rb = 724 + (l % 2) * 48
    E.dma("sync", SC32[:, rb:rb + 8], dram["bln1s"][l], grp="io")
    E.dma("sync", SC32[:, rb + 8:rb + 16], dram["bln1b"][l], grp="io")
    E.dma("sync", SC32[:, rb + 16:rb + 24], dram["bgns"][l], grp="io")
    E.dma("sync", SC32[:, rb + 24:rb + 32], dram["bgnb"][l], grp="io")
    E.dma("sync", SC32[:, rb + 32:rb + 40], dram["bln2s"][l], grp="io")
    E.dma("sync", SC32[:, rb + 40:rb + 48], dram["bln2b"][l], grp="io")
    ev_small = E.grp_waits("io")

    gw = [meta["_g_ready"]] if l == 0 else [(E.ms["vector"], E.ms["vector"].n)]
    rdy1 = emit_norm_F(meta, XI, NF, eps=1e-6, gamma=SC32[:, rb:rb + 8],
                       beta=SC32[:, rb + 8:rb + 16],
                       waits=gw + ev_small + meta["_xpos_rdy"])
    rdy1_tb = meta["_norm_rdy_tb"]

    qT = AD[:, 0:2048]
    kT = AD[:, 2048:4096]
    vT = AD[:, 4096:6144]
    stgbuf = [AD[:, 6144:8192], AD[:, 14336:16384]]
    kFh = AD[:, 8192:10240]
    scr_u = AD[:, 10240:12288]
    scr_w = AD[:, 12288:14336]
    qTs = AD[:, 14336:14464]
    ksc = AD[:, 14464:14592]
    stT = MISCB[:, 256:384]
    stT2 = MISCB[:, 128:256]
    decT = MISCB[:, 384:512]
    decX = MISCB[:, 512:640]
    Rf32 = SC32[:, 128:256]
    Rb16 = MISCB[:, 0:128]
    OTF = BIG[:, 8192:24576].rearrange("p (h t) -> p h t", t=2048)

    wslots = {}
    order = [(h, w) for h in range(8) for w in ("q", "k", "v")]
    Wd = {"q": dram["bWq"], "k": dram["bWk"], "v": dram["bWv"]}
    ws = WStream(meta)

    def load_w(i):
        h, w = order[i]
        sl, view, ev = ws.load(Wd[w][l, h])
        wslots[(h, w)] = (view, ev, sl)

    load_w(0)
    load_w(1)
    nload = [2]
    tQC, tQS = TAB[:, :, 0:128], TAB[:, :, 128:256]
    tKC, tKS = TAB[:, :, 256:384], TAB[:, :, 384:512]
    dec_guard = (E.ms["vector"], E.ms["vector"].n) if E.ms["vector"].n else None
    scan_done = None
    prev_scan = {0: None, 1: None}
    prev_bq = None
    prev_bk = None
    prev_hoist = None
    g2 = {}
    for key, bank in ((("S", 0), 2), (("S", 1), 3), (("O", 0), 4),
                      (("O", 1), 5), (("R", 0), 6), (("R", 1), 7)):
        if bank in ps_guard:
            g2[key] = ps_guard.pop(bank)
    for h in range(8):
        par = h % 2
        qFh = stgbuf[par]
        dw = [dec_guard] if dec_guard else []
        ev_dec1 = E.dma("scalar", decT, dram["decayT"][h], grp="io", waits=dw)
        ev_dec = [ev_dec1, E.dma("scalar", decX, dram["decay_cross"][h],
                                 grp="io")]
        stg_guard = [prev_scan[par]] if prev_scan[par] else []
        ev_t = {}
        for wi, w in enumerate(("q", "k", "v")):
            buf, ev_w, sl = wslots[(h, w)]
            lastmm = None
            for tb in range(4):
                bank_i = tb % 2
                for kk in range(ND):
                    wt = []
                    if kk == 0:
                        if tb == 0:
                            wt = [ev_w]
                        if h == 0 and wi == 0:
                            wt += list(E._flat(rdy1_tb[tb]))
                        if bank_i in ps_guard:
                            wt.append(ps_guard[bank_i])
                    lastmm = E.op("tensor",
                                  (lambda e, b=buf, kk=kk, bi=bank_i, tb=tb,
                                   f=(kk == 0), s=(kk == ND - 1):
                                   e.matmul(PS[bi][:, 0:512], b[:, kk, 0:128],
                                            NF[:, kk, tb * 512:(tb + 1) * 512],
                                            start=f, stop=s, skip_group_check=True)),
                                  waits=wt, tick=(kk == ND - 1))
                ew = [(E.ms["tensor"], lastmm)]
                if tb == 0:
                    ew += list(stg_guard)
                ve = E.op("scalar",
                          (lambda e, bi=bank_i, d=qFh[:, tb * 512:(tb + 1) * 512]:
                           e.copy(d, PS[bi][:, 0:512])),
                          waits=ew, tick=True)
                ps_guard[bank_i] = (E.ms["scalar"], ve)
            ws.done(sl, (E.ms["tensor"], lastmm))
            if nload[0] < len(order):
                load_w(nload[0])
                nload[0] += 1
            if wi == 0:
                dstT, tw = qT, ([prev_bq] if prev_bq else [])
            elif wi == 1:
                dstT, tw = kT, (([prev_bk] if prev_bk else []) +
                                ([(E.ms["vector"], prev_hoist)]
                                 if prev_hoist else []))
            else:
                dstT, tw = vT, ([(E.ms["tensor"], scan_done)]
                                if scan_done is not None else [])
            ev_t[w] = E.dma_tr("sync", dstT.rearrange("p (tt c) -> p tt c", c=128),
                               qFh, waits=[(E.ms["scalar"], ve)] + tw)
            stg_guard = [ev_t[w]]

        def xpos2(src, cosT_, sinT_, wait0, eng):
            s3 = src.rearrange("p (tt c) -> p tt c", c=128)
            w3 = scr_w.rearrange("p (tt c) -> p tt c", c=128)
            s4 = src.rearrange("p (tt a b) -> p tt a b", a=64, b=2)
            u4 = scr_u.rearrange("p (tt a b) -> p tt a b", a=64, b=2)
            sf4 = sinT_.rearrange("p tt (a b) -> p tt a b", b=2)
            E.op(eng, (lambda e: e.tensor_tensor(w3, s3, cosT_, ALU.mult)),
                 waits=wait0)
            E.op(eng, (lambda e: e.tensor_tensor(u4[:, :, :, 0:1], s4[:, :, :, 1:2],
                                                 sf4[:, :, :, 0:1], ALU.mult)))
            E.op(eng, (lambda e: e.tensor_tensor(u4[:, :, :, 1:2], s4[:, :, :, 0:1],
                                                 sf4[:, :, :, 1:2], ALU.mult)))
            return E.op(eng, (lambda e: e.tensor_tensor(src, scr_w, scr_u,
                                                        ALU.add)), tick=True)
        xw = [ev_t["q"], ev_t["k"], ev_t["v"]]
        if scan_done is not None:
            xw.append((E.ms["tensor"], scan_done))
        v_xq = xpos2(qT, tQC, tQS, xw, "vector")
        v_xk = xpos2(kT, tKC, tKS, [(E.ms["vector"], v_xq)], "vector")
        ev_bq = E.dma_tr("sync", qFh.rearrange("p (tt c) -> p tt c", c=128), qT,
                         waits=[(E.ms["vector"], v_xk)])
        ev_bk = E.dma_tr("sync", kFh.rearrange("p (tt c) -> p tt c", c=128), kT)
        qFr = qFh.rearrange("p (cc c) -> p cc c", c=128)
        kFr = kFh.rearrange("p (cc c) -> p cc c", c=128)
        vTr = vT.rearrange("p (cc c) -> p cc c", c=128)
        qTs_all = scr_u.rearrange("p (cc c) -> p cc c", c=128)
        ksc_all = scr_w.rearrange("p (cc c) -> p cc c", c=128)
        E.op("vector", (lambda e: e.tensor_tensor(
            qTs_all, qFr, decX.unsqueeze(1).broadcast_to([128, NT, 128]),
            ALU.mult)), waits=[ev_bq, ev_dec[1]])
        v_hoist = E.op("vector", (lambda e: e.tensor_scalar(
            scr_w, kT, SMALL[:, 184 + h:185 + h], None, ALU.mult)), tick=True)
        v_state = None
        for c in range(NT):
            cp = c % 2
            bS, bN, bR = (2, 4, 6) if cp == 0 else (3, 5, 7)
            stTc = stT if cp == 0 else stT2
            wS = [ev_bq, ev_bk] + ev_dec if c == 0 else []
            if ("S", cp) in g2:
                wS.append(g2[("S", cp)])
            mS = E.op("tensor",
                      (lambda e, c=c, bS=bS: e.matmul(PS[bS][:, 0:128],
                                                      kFr[:, c, :],
                                                      qFr[:, c, :], start=True,
                                                      stop=True,
                                                      skip_group_check=True)),
                      waits=wS, tick=True)
            v_st = E.op("vector", (lambda e, bS=bS, stTc=stTc:
                                   e.tensor_mul(stTc, PS[bS][:, 0:128], decT)),
                        waits=[(E.ms["tensor"], mS)], tick=True)
            g2[("S", cp)] = (E.ms["vector"], v_st)
            wn = [(E.ms["vector"], v_st)]
            if c == 0:
                wn.append((E.ms["vector"], v_hoist))
            if ("O", cp) in g2:
                wn.append(g2[("O", cp)])
            mnum = E.op("tensor",
                        (lambda e, c=c, s=(c == 0), bN=bN, stTc=stTc:
                         e.matmul(PS[bN][:, 0:128], vTr[:, c, :], stTc,
                                  start=True, stop=s, skip_group_check=True)),
                        waits=wn, tick=(c == 0))
            if c > 0:
                mnum = E.op("tensor",
                            (lambda e, c=c, bN=bN: e.matmul(PS[bN][:, 0:128],
                                                            Rb16,
                                                            qTs_all[:, c, :],
                                                            start=False,
                                                            stop=True,
                                                            skip_group_check=True)),
                            waits=[(E.ms["vector"], v_state)], tick=True)
            wd = []
            if ("R", cp) in g2:
                wd.append(g2[("R", cp)])
            if v_state is not None:
                wd.append((E.ms["vector"], v_state))
            mdr = E.op("tensor",
                       (lambda e, c=c, bR=bR: e.matmul(PS[bR][:, 0:128],
                                                       ksc_all[:, c, :],
                                                       vTr[:, c, :],
                                                       start=True, stop=True,
                                                       skip_group_check=True)),
                       waits=wd, tick=True)
            if c == 0:
                vsa = E.op("vector", (lambda e, bR=bR:
                                      e.tensor_copy(Rf32, PS[bR][:, 0:128])),
                           waits=[(E.ms["tensor"], mdr)], tick=True)
            else:
                vsa = E.op("vector",
                           (lambda e, g=gamC[h], bR=bR: e.scalar_tensor_tensor(
                               Rf32, Rf32, g, PS[bR][:, 0:128],
                               ALU.mult, ALU.add)),
                           waits=[(E.ms["tensor"], mdr)], tick=True)
            v_state = E.op("vector", lambda e: e.tensor_copy(Rb16, Rf32),
                           tick=True)
            g2[("R", cp)] = (E.ms["vector"], v_state)
            v_oe = E.op("scalar",
                        (lambda e, c=c, h=h, bN=bN:
                         e.copy(OTF[:, h, c * 128:(c + 1) * 128],
                                PS[bN][:, 0:128])),
                        waits=[(E.ms["tensor"], mnum)], tick=True)
            g2[("O", cp)] = (E.ms["scalar"], v_oe)
        scan_done = E.op("tensor", lambda e: e.nop(),
                         waits=[(E.ms["scalar"], E.ms["scalar"].n)], tick=True)
        prev_scan[par] = (E.ms["tensor"], scan_done)
        prev_bq, prev_bk, prev_hoist = ev_bq, ev_bk, v_hoist
        dec_guard = (E.ms["vector"], E.ms["vector"].n)
    ps_guard[2] = g2[("S", 0)]
    ps_guard[3] = g2[("S", 1)]
    ps_guard[4] = g2[("O", 0)]
    ps_guard[5] = g2[("O", 1)]
    ps_guard[6] = g2[("R", 0)]
    ps_guard[7] = g2[("R", 1)]

    adT = AD[:].rearrange("p (tt d) -> p tt d", d=1024)
    for h in range(8):
        E.dma_tr("sync", adT[:, :, h * 128:(h + 1) * 128], OTF[:, h, :],
                 waits=[(E.ms["tensor"], scan_done),
                        (E.ms["scalar"], E.ms["scalar"].n),
                        (E.ms["vector"], E.ms["vector"].n)] if h == 0 else [])
    ot_moved = E.grp_waits("tr")

    emit_proj_accum(meta, dram["bWg"][l], NF, None, ot_moved, ps_guard,
                    n_in=ND, accumulate=False, outF=OTF, act=AF.Silu)
    v_gate = (E.ms["scalar"], E.ms["scalar"].n)

    dump = BIG[:, 0:1024]
    v_ap = None
    for tt in range(NT):
        w0 = list(ot_moved) + [v_gate] if tt == 0 else []
        a3 = adT[:, tt, :].rearrange("p (s c) -> p s c", c=128)
        v_sq = E.op("scalar", (lambda e, tt=tt: e.activation(
            dump, adT[:, tt, :], AF.Square)), waits=w0, tick=True)
        E.op("vector", (lambda e, a3=a3: e.reduce_sum(STAT[:, 32:40], a3,
                                                      axis=AX.X)),
             waits=w0 if tt == 0 else [])
        E.op("vector", (lambda e: e.reduce_sum(
            SMALL[:, 144:152], dump.rearrange("p (s c) -> p s c", c=128),
            axis=AX.X)), waits=[(E.ms["scalar"], v_sq)])
        E.op("vector", lambda e: e.tensor_scalar(STAT[:, 32:40], STAT[:, 32:40],
                                                 -1.0 / 128.0, None, ALU.mult))
        E.op("vector", lambda e: e.tensor_mul(SMALL[:, 152:160], STAT[:, 32:40],
                                              STAT[:, 32:40]))
        E.op("vector", lambda e: e.scalar_tensor_tensor(
            SMALL[:, 144:152], SMALL[:, 144:152], 1.0 / 128.0, SMALL[:, 152:160],
            ALU.mult, ALU.subtract))
        v_v = E.op("vector", lambda e: e.tensor_scalar(
            SMALL[:, 144:152], SMALL[:, 144:152], 1e-6, None, ALU.add), tick=True)
        E.op("scalar", lambda e: e.activation(SMALL[:, 152:160], SMALL[:, 144:152],
                                              AF.Ln), waits=[(E.ms["vector"], v_v)])
        v_rs = E.op("scalar", lambda e: e.activation(STAT[:, 40:48],
                                                     SMALL[:, 152:160], AF.Exp,
                                                     scale=-0.5), tick=True)
        nm_bc = STAT[:, 32:40].unsqueeze(2).broadcast_to([128, 8, 128])
        rs_bc = STAT[:, 40:48].unsqueeze(2).broadcast_to([128, 8, 128])
        E.op("vector", (lambda e, a3=a3, nm_bc=nm_bc: e.tensor_tensor(
            a3, a3, nm_bc, ALU.add)), waits=[(E.ms["scalar"], v_rs)])
        v_ap = E.op("vector", (lambda e, a3=a3, rs_bc=rs_bc: e.tensor_tensor(
            a3, a3, rs_bc, ALU.mult)), tick=True)
    for tt in range(NT):
        E.dma_tr("sync", NF[:, :, tt * 128:(tt + 1) * 128], adT[:, tt, :],
                 waits=[(E.ms["vector"], v_ap), v_gate] if tt == 0 else [])
    y_rdy = E.grp_waits("tr")
    v_gg = None
    for dt in range(ND):
        E.op("vector",
             (lambda e, dt=dt: e.tensor_scalar(NF[:, dt, :], NF[:, dt, :],
                                               SC32[:, 208 + dt:209 + dt],
                                               SC32[:, 216 + dt:217 + dt],
                                               ALU.mult, ALU.add)),
             waits=list(y_rdy) if dt == 0 else [])
        v_gg = E.op("vector",
                    (lambda e, dt=dt: e.tensor_mul(OTF[:, dt, :], NF[:, dt, :],
                                                   OTF[:, dt, :])), tick=True)
    emit_proj_accum(meta, dram["bWo"][l], OTF, XI, [(E.ms["vector"], v_gg)],
                    ps_guard, n_in=ND)
    v_res1 = (E.ms["vector"], E.ms["vector"].n)
    v_wo_mm = (E.ms["tensor"], E.ms["tensor"].n)
    emit_dbg(meta, f"bl{l}_res1", lambda: XI[:], (128, ND, T), F32, [v_res1])

    rdy2 = emit_norm_F(meta, XI, NF, eps=1e-6, gamma=SC32[:, rb + 32:rb + 40],
                       beta=SC32[:, rb + 40:rb + 48], waits=[v_res1, v_wo_mm])

    emit_dbg(meta, f"bl{l}_xn2", lambda: NF[:], (128, ND, T), BF16, rdy2)
    emit_mlp_plain(meta, dram["bw1"][l], dram["bw2"][l], 4096, AF.Gelu_apprx_tanh,
                   rdy2, l)
    emit_dbg(meta, f"bl{l}_hst", lambda: meta["AD"][:], (128, 16384), BF16,
             [(E.ms["scalar"], E.ms["scalar"].n)])
    emit_dbg(meta, f"bl{l}_res2", lambda: XI[:], (128, ND, T), F32,
             [(E.ms["vector"], E.ms["vector"].n)])


def emit_mlp_plain(meta, w1d, w2d, NH, act, rdy, l):
    nc, E = meta["nc"], meta["E"]
    XI, NF, AD, PS, PSALL = (meta["XI"], meta["NF"], meta["AD"], meta["PS"],
                             meta["PSALL"])
    ps_guard = meta["ps_guard"]
    nch = NH // 128
    hst = AD[:, 0:nch * 512].rearrange("p (i t) -> p i t", t=512)
    rdy_tb = meta["_norm_rdy_tb"]
    wload = {}
    BIGW = meta["BIG"][:, 0:8192].rearrange("p (s b) -> p s b", s=4)
    ws = WStream(meta, key="bigw", buf=BIGW, nslot=4, grp="bigw",
                 init_extra=list(E._flat(meta["_norm_cast_evs"])))
    wc, wce = meta["wcache"], meta["wcache_ev"]

    def load_w1(i):
        sl, view, ev = ws.load_cached(w1d[i], wc["bw1"][l][i],
                                      ("bw1", l, i), wce)
        wload[i] = (view, ev, sl)

    def load_w2(i):
        sl, view, ev = ws.load_cached(w2d[i * 128:(i + 1) * 128, :],
                                      wc["bw2"][l][i], ("bw2", l, i), wce)
        wload[("w2", i)] = (view, ev, sl)

    for tb in range(4):
        t0 = tb * 512
        for _j in range(4):
            load_w1(_j)
        for i in range(nch):
            buf, ev, sl = wload[i // 2]
            halfm = (i % 2) * 128
            bi = i % 2
            last = None
            for kk in range(ND):
                w = []
                if kk == 0:
                    w = [ev]
                    if i == 0:
                        w += list(E._flat(rdy_tb[tb]))
                    if bi in ps_guard:
                        w.append(ps_guard[bi])
                last = E.op("tensor",
                            (lambda e, b=buf, kk=kk, bi=bi, f=(kk == 0),
                             s=(kk == ND - 1), t0=t0, h0=halfm:
                             e.matmul(PSALL[:, bi * 512:bi * 512 + 512],
                                      b[:, kk, h0:h0 + 128],
                                      NF[:, kk, t0:t0 + 512],
                                      start=f, stop=s, skip_group_check=True)),
                            waits=w, tick=(kk == ND - 1))
            if i % 2 == 1:
                ws.done(sl, (E.ms["tensor"], last))
                if i // 2 + 4 < nch // 2:
                    load_w1(i // 2 + 4)
            v_a = E.op("scalar",
                       (lambda e, bi=bi, i=i: e.activation(
                           hst[:, i, :], PSALL[:, bi * 512:bi * 512 + 512], act)),
                       waits=[(E.ms["tensor"], last)], tick=True)
            ps_guard[bi] = (E.ms["scalar"], v_a)
        v_all = (E.ms["scalar"], E.ms["scalar"].n)
        for _j in range(4):
            load_w2(_j)
        vfin = None
        for i in range(nch):
            bview, ev, sl = wload[("w2", i)]
            for m in range(ND):
                w = []
                if m == 0:
                    w = [ev]
                    if i == 0:
                        w.append(v_all)
                if i == 0 and (4 + m % 4) in ps_guard and m < 4:
                    pass
                if i == 0 and m in ps_guard:
                    w.append(ps_guard[m])
                vfin = E.op("tensor",
                            (lambda e, bv=bview, m=m, f=(i == 0),
                             s=(i == nch - 1), hv=hst[:, i, :]:
                             e.matmul(PSALL[:, m * 512:m * 512 + 512],
                                      bv[:, m * 128:(m + 1) * 128],
                                      hv, start=f, stop=s,
                                      skip_group_check=True)),
                            waits=w, tick=(m == ND - 1))
            ws.done(sl, (E.ms["tensor"], vfin))
            if i + 4 < nch:
                load_w2(i + 4)
        for m in range(ND):
            v_e = E.op("vector",
                       (lambda e, m=m, t0=t0:
                        e.tensor_add(XI[:, m, t0:t0 + 512],
                                     XI[:, m, t0:t0 + 512],
                                     PSALL[:, m * 512:m * 512 + 512])),
                       waits=[(E.ms["tensor"], vfin)], tick=True)
            ps_guard[m] = (E.ms["vector"], v_e)


_BUILD_CACHE = {}
_DBG = ()


def _chunkmaj(W):
    K, M = W.shape
    return np.ascontiguousarray(
        W.reshape(K // 128, 128, M // 128, 128).transpose(2, 1, 0, 3))


def _pairmaj(W):
    K, M = W.shape
    a = W.reshape(K // 128, 128, M // 256, 256).transpose(2, 1, 0, 3)
    return np.ascontiguousarray(a)


def _layout_inputs(inputs):
    f = np.float32
    o = {}
    o["aWq"] = _chunkmaj(inputs["L0_attn_a_Wq"].T.astype(f))
    o["aWk"] = _chunkmaj(inputs["L0_attn_a_Wk"].T.astype(f))
    o["aWv"] = _chunkmaj(inputs["L0_attn_a_Wv"].T.astype(f))
    o["aWo"] = _pairmaj(inputs["L0_attn_a_Wo"].T.astype(f))
    o["fwqkv"] = np.stack([_pairmaj(inputs["L0_ffn_wqkv"][l]) for l in range(2)])
    o["fwo"] = np.stack([_pairmaj(inputs["L0_ffn_wo"][l]) for l in range(2)])
    w1 = inputs["L0_ffn_w1"]
    w1p = np.zeros((2, 22, 128, 8, 256), f)
    for l in range(2):
        for i in range(22):
            for ag in range(2):
                c0 = i * 128 + ag * 2730
                cw = min(128, 2730 - i * 128)
                blk = w1[l][:, c0:c0 + cw]
                w1p[l, i, :, :, ag * 128:ag * 128 + cw] = \
                    blk.reshape(8, 128, cw).transpose(1, 0, 2)
    o["fw1"] = w1p
    w2 = inputs["L0_ffn_w2"]
    w2p = np.zeros((2, 22, 128, 1024), f)
    for l in range(2):
        for i in range(22):
            rw = min(128, 2730 - i * 128)
            w2p[l, i, 0:rw] = w2[l][i * 128:i * 128 + rw]
    o["fw2"] = w2p
    o["fqs"] = np.ascontiguousarray(inputs["L0_ffn_q_scale"])
    o["fks"] = np.ascontiguousarray(inputs["L0_ffn_k_scale"])

    def vec_F(v):
        sh = v.shape[:-1]
        return np.ascontiguousarray(v.reshape(sh + (ND, 128)).swapaxes(-1, -2))
    o["flng"] = vec_F(inputs["L0_ffn_ln_g"])
    o["flnb"] = vec_F(inputs["L0_ffn_ln_b"])
    b1 = inputs["L0_ffn_b1"]
    b1p = np.zeros((2, 128, 44), f)
    for i in range(22):
        aw = min(128, 2730 - i * 128)
        b1p[:, 0:aw, i] = b1[:, i * 128:i * 128 + aw]
        b1p[:, 0:aw, 22 + i] = b1[:, 2730 + i * 128:2730 + i * 128 + aw]
    o["fb1"] = np.ascontiguousarray(b1p)
    o["fb2"] = vec_F(inputs["L0_ffn_b2"])
    def headmaj(Wh):
        return np.ascontiguousarray(
            Wh.reshape(2, 8, 8, 128, 128).transpose(0, 1, 3, 2, 4))
    o["bWq"] = headmaj(inputs["L0_attn_b_Wq"])
    o["bWk"] = headmaj(inputs["L0_attn_b_Wk"])
    o["bWv"] = headmaj(inputs["L0_attn_b_Wv"])
    o["bWg"] = np.stack([_pairmaj(inputs["L0_attn_b_Wg"][l]) for l in range(2)])
    o["bWo"] = np.stack([_pairmaj(inputs["L0_attn_b_Wo"][l]) for l in range(2)])
    o["bw1"] = np.stack([_pairmaj(inputs["L0_attn_b_ffn_w1"][l]) for l in range(2)])
    o["bw2"] = np.ascontiguousarray(inputs["L0_attn_b_ffn_w2"])
    o["bgns"] = vec_F(inputs["L0_attn_b_gn_scale"])
    o["bgnb"] = vec_F(inputs["L0_attn_b_gn_bias"])
    o["bln1s"] = vec_F(inputs["L0_attn_b_ln1_scale"])
    o["bln1b"] = vec_F(inputs["L0_attn_b_ln1_bias"])
    o["bln2s"] = vec_F(inputs["L0_attn_b_ln2_scale"])
    o["bln2b"] = vec_F(inputs["L0_attn_b_ln2_bias"])
    return o


def kernel(**inputs):
    key = (_PARTIAL, tuple(_DBG))
    if key not in _BUILD_CACHE:
        _BUILD_CACHE[key] = build_program(dbg=tuple(_DBG))
    nc, meta = _BUILD_CACHE[key]
    tabs = meta["tabs"]
    shared = _layout_inputs(inputs)
    for k, v in tabs.items():
        if k == "gammaC":
            continue
        shared[k] = np.asarray(v)
    x_full = np.asarray(inputs["L0_attn_a_x"])
    in_maps = []
    for c in range(8):
        m = dict(shared)
        m["x"] = np.ascontiguousarray(x_full[c])
        in_maps.append(m)
    res = run_bass_kernel_spmd(nc, in_maps, core_ids=list(range(8)))
    out = np.stack([np.asarray(r["y"]) for r in res.results])
    if _DBG:
        kernel._dbg_results = [{k: np.asarray(r["dbg_" + k]) for k in _DBG}
                               for r in res.results]
    return out
